# version 01
# baseline (speedup 1.0000x reference)
import numpy as np
import ml_dtypes
import concourse.bass as bass
import concourse.mybir as mybir
from concourse.bass_core_rust import IndirectOffsetOnAxis
from concourse.bass_utils import run_bass_kernel_spmd

F32 = mybir.dt.float32
BF16 = mybir.dt.bfloat16
I32 = mybir.dt.int32
AF = mybir.ActivationFunctionType
ALU = mybir.AluOpType
AX = mybir.AxisListType

NCORES = 8
D = 1024
NDC = 8
T = 4096
TH = 2048
HN = 4
DH = 64
NL = 256
EPC = 16384
NET = 128
SLAB = 256
NSLAB = T // SLAB

PHASES = 6
DEBUG = False


class S:
    registry = []

    def __init__(self, nc, name):
        self.h = nc.alloc_semaphore(name)
        self.v = 0
        S.registry.append(self)


def build_program():
    S.registry = []
    nc = bass.Bass("TRN2", target_bir_lowering=False, debug=False,
                   num_devices=NCORES, num_swdge_queues=4)

    def inp(name, shape, dt=F32):
        return nc.dram_tensor(name, shape, dt, kind="ExternalInput").ap()

    x0T = inp("x0T", [D, T])
    qkvw = inp("qkvw", [D, 768])
    wv = inp("wv", [D, D])
    wvb = inp("wvb", [128, NDC])
    wqk = inp("wqk", [D, 512])
    outw_o = inp("outw_o", [512, D])
    outw_m = inp("outw_m", [256, D])
    outb = inp("outb", [128, NDC])
    w1s = inp("w1s", [512, D])
    w2s = inp("w2s", [D, 512])
    b1c = inp("b1c", [128, 32])
    b2c = inp("b2c", [128, NDC])
    adjr = inp("adjr", [128, NET], I32)
    adjc = inp("adjc", [128, NET], I32)
    adjv = inp("adjv", [128, NET])
    bands = inp("bands", [128, HN * 7 * 128])
    iota64 = inp("iota64", [128, 64])
    ones_bf = inp("ones_bf", [128, 128], BF16)
    ident_bf = inp("ident_bf", [128, 128], BF16)
    ikc = inp("ikc", [128, 2 * 768], BF16)
    selalpha = inp("selalpha", [64, 32], BF16)

    y = nc.dram_tensor("y", [D, TH], F32, kind="ExternalOutput").ap()
    dbg = {}

    def dbgout(name, shape, dt):
        if DEBUG:
            dbg[name] = nc.dram_tensor(name, shape, dt, kind="ExternalOutput").ap()

    dbgout("dbg_ln0", [128, NDC, T], BF16)
    dbgout("dbg_q", [128, 2, T], BF16)
    dbgout("dbg_k", [128, 2, T], BF16)
    dbgout("dbg_v", [128, 32 * 256], BF16)
    dbgout("dbg_ql", [128, 512], BF16)
    dbgout("dbg_kl", [128, 512], BF16)
    dbgout("dbg_a2", [128, 2048], BF16)
    dbgout("dbg_scale", [1, 8], F32)
    dbgout("dbg_f", [128, 512], BF16)
    dbgout("dbg_g", [128, 512], BF16)
    dbgout("dbg_out", [128, 2, T], BF16)
    dbgout("dbg_enc", [128, NDC, TH], BF16)
    dbgout("dbg_val", [128, NDC, TH], BF16)
    dbgout("dbg_a2d", [64, 64], F32)
    dbgout("dbg_alpha", [1, TH], BF16)
    dbgout("dbg_ln1", [128, NDC, TH], BF16)
    dbgout("dbg_wq1", [128, 32 * NDC * 128], BF16)
    dbgout("dbg_wq2a", [128, 16 * NDC * 128], BF16)
    dbgout("dbg_wq2b", [128, 16 * NDC * 128], BF16)
    dbgout("dbg_h", [128, 32 * 256], BF16)
    dbgout("dbg_ccwqi", [2048, 512], BF16)
    dbgout("dbg_ccwqo", [NCORES * 2048, 512], BF16)

    def dram(name, shape, dt):
        return nc.dram_tensor(name, shape, dt, kind="Internal").ap()

    cc_scale_i = dram("cc_scale_i", [1, 2], F32)
    cc_scale_o = dram("cc_scale_o", [1, 2], F32)
    cc_gam_i = dram("cc_gam_i", [1, 2], F32)
    cc_gam_o = dram("cc_gam_o", [1, 2], F32)
    cc_out_i = dram("cc_out_i", [256, T], BF16)
    cc_out_o = dram("cc_out_o", [512, T], BF16)
    cc_xg_i = dram("cc_xg_i", [D, TH], BF16)
    cc_xg_o = dram("cc_xg_o", [4 * D, TH], BF16)
    cc_tab_i = dram("cc_tab_i", [TH, 512], BF16)
    cc_tab_o = dram("cc_tab_o", [NCORES * TH, 512], BF16)
    cc_wq_i = dram("cc_wq_i", [2048, 512], BF16)
    cc_wq_o = dram("cc_wq_o", [NCORES * 2048, 512], BF16)
    cc_a2d_i = dram("cc_a2d_i", [64, 64], F32)
    cc_a2d_o = dram("cc_a2d_o", [64, 64], F32)
    d_alph = dram("d_alph", [32, 64], BF16)
    d_x1 = nc.dram_tensor("d_x1", [D, TH], F32,
                          kind="ExternalOutput" if DEBUG else "Internal").ap()

    SB0 = 16512

    def at(name, shape, dtt, offset):
        return nc.alloc_sbuf_tensor_at(name, shape, dtt, offset=SB0 + offset)
    encT = at("encT", [128, NDC, TH], BF16, offset=0)
    ln0h = at("ln0h", [128, NDC, TH], BF16, offset=32768)
    valT = at("valT", [128, NDC, TH], BF16, offset=65536)
    qTs = at("qTs", [128, 2, T], BF16, offset=98304)
    kTs = at("kTs", [128, 2, T], BF16, offset=114688)
    vtok = at("vtok", [128, 32, 256], BF16, offset=131072)
    qkvw_sb = at("qkvw_sb", [128, NDC, 768], BF16, offset=147456)
    outTo = at("outTo", [128, 2, T], BF16, offset=147456)

    CW = 163840

    def atw(name, shape, dt):
        nonlocal CW
        t = at(name, shape, dt, offset=CW)
        per = int(np.prod(shape[1:])) * mybir.dt.size(dt)
        CW = (CW + per + 31) & ~31
        return t

    iota_sb = atw("iota_sb", [128, 64], F32)
    ones_sb = atw("ones_sb", [128, 128], BF16)
    ident_sb = atw("ident_sb", [128, 128], BF16)
    sela_sb = atw("sela_sb", [64, 32], BF16)
    wvb_sb = atw("wvb_sb", [128, NDC], F32)
    outb_sb = atw("outb_sb", [128, NDC], F32)
    b1_sb = atw("b1_sb", [128, 32], F32)
    b2_sb = atw("b2_sb", [128, NDC], F32)
    adjr_sb = atw("adjr_sb", [128, NET], I32)
    adjc_sb = atw("adjc_sb", [128, NET], I32)
    adjv_sb = atw("adjv_sb", [128, NET], F32)
    SCR = CW
    SCR_END = 212000
    assert SCR <= 168500, f"const region too big: {SCR}"

    def scr(name, shape, dt, off):
        per = int(np.prod(shape[1:])) * mybir.dt.size(dt)
        assert SCR + off + per <= SCR_END, (name, SCR + off + per - SCR_END)
        return at(name, shape, dt, offset=SCR + off)

    xslab = [scr(f"xslab{i}", [128, NDC, SLAB], F32, 8192 * i) for i in range(2)]
    xb_sb = scr("xb", [128, NDC, SLAB], BF16, 16384)
    sqb_sb = scr("sqb", [128, NDC, SLAB], BF16, 20480)
    lnoth = scr("lnoth", [128, NDC, SLAB], BF16, 24576)
    statrows = scr("statrows", [1, 2048], F32, 28672)
    browbf = scr("browbf", [1, 2 * SLAB], BF16, 35840)
    wstg = scr("wstg", [128, 2048], F32, 0)

    pb = [nc.alloc_psum_tensor(f"pb{i}", [128, 512], F32) for i in range(7)]
    pbt = nc.alloc_psum_tensor("pbt", [128, 1024], BF16)

    sld = S(nc, "sld")
    sx2 = [S(nc, "sx0"), S(nc, "sx1")]
    sci = S(nc, "sci")
    sv = S(nc, "sv")
    sa = S(nc, "sa")
    sp = S(nc, "sp")
    sm = S(nc, "sm")
    sst = S(nc, "sst")
    scc = S(nc, "scc")
    sg = S(nc, "sg")

    blk = nc.Block()
    blk.__enter__()

    def _(sync):
        for ap_in, sb in [(iota64, iota_sb), (ones_bf, ones_sb),
                          (ident_bf, ident_sb), (selalpha, sela_sb),
                          (adjr, adjr_sb), (adjc, adjc_sb), (adjv, adjv_sb),
                          (wvb, wvb_sb), (outb, outb_sb), (b1c, b1_sb),
                          (b2c, b2_sb)]:
            sync.dma_start(out=sb[:, :], in_=ap_in).then_inc(sld.h, 16)
            sld.v += 16
        for r in range(4):
            if r > 0:
                sync.wait_ge(sv.h, r)
            sync.dma_start(
                out=wstg[:, 0:1536].rearrange("p (c n) -> p c n", c=2),
                in_=qkvw.rearrange("(c p) n -> p c n", p=128)[:, 2 * r: 2 * r + 2, :],
            ).then_inc(sld.h, 16)
            sld.v += 16
    blk.sync(_)

    def _(v):
        base = 16 * 11
        for r in range(4):
            v.wait_ge(sld.h, base + 16 * (r + 1))
            wv2 = wstg[:, 0:1536].rearrange("p (c n) -> p c n", c=2)
            v.tensor_single_scalar(qkvw_sb[:, 2 * r: 2 * r + 2, 0:256],
                                   wv2[:, :, 0:256], 0.125, ALU.mult)
            v.tensor_copy(qkvw_sb[:, 2 * r: 2 * r + 2, 256:768],
                          wv2[:, :, 256:768]).then_inc(sv.h, 1)
            sv.v += 1
    blk.vector(_)
    QKVW_CAST = sv.v
    P0_LD = 16 * 12

    BA, BM, BV = sa.v, sm.v, sv.v

    def _(sync):
        for i in range(NSLAB):
            if i >= 2:
                sync.wait_ge(sv.h, BV + 8 * (i - 2) + 7)
            else:
                sync.wait_ge(sv.h, QKVW_CAST)
            sync.dma_start(
                out=xslab[i % 2][:, :, :],
                in_=x0T.rearrange("(c p) t -> p c t", p=128)[:, :, SLAB * i: SLAB * (i + 1)],
            ).then_inc(sx2[i % 2].h, 16)
            sx2[i % 2].v += 16
    blk.sync(_)

    def lnsrc(i, c):
        if i < 8:
            return ln0h[:, c, SLAB * i: SLAB * (i + 1)]
        return lnoth[:, c, :]

    def _(s):
        for i in range(NSLAB):
            xs = xslab[i % 2]
            s.wait_ge(sx2[i % 2].h, 16 * (i // 2 + 1))
            if i >= 1:
                s.wait_ge(sm.h, BM + 8 * (i - 1) + 1)
            s.activation(xb_sb[:, :, :], xs[:, :, :], AF.Copy)
            s.activation(sqb_sb[:, :, :], xs[:, :, :], AF.Square).then_inc(sa.h, 1)
            s.wait_ge(sv.h, BV + 8 * i + 4)
            s.activation(statrows[0:1, 4 * SLAB: 5 * SLAB],
                         statrows[0:1, 3 * SLAB: 4 * SLAB], AF.Sqrt).then_inc(sa.h, 1)
            sa.v += 2
            for g in range(6):
                s.wait_ge(sm.h, BM + 8 * i + 3 + g)
                if g < 2:
                    s.activation(qTs[:, g, SLAB * i: SLAB * (i + 1)],
                                 pb[2 + (g % 2)][:, 0:SLAB], AF.Copy).then_inc(sa.h, 1)
                elif g < 4:
                    s.activation(kTs[:, g - 2, SLAB * i: SLAB * (i + 1)],
                                 pb[2 + (g % 2)][:, 0:SLAB], AF.Copy).then_inc(sa.h, 1)
                else:
                    s.activation(vtok[:, 2 * i + (g - 4), :],
                                 pb[4 + (g - 4)][:, 0:256], AF.Copy).then_inc(sa.h, 1)
                sa.v += 1
    blk.scalar(_)

    def _(v):
        inv = 1.0 / D
        r = statrows
        SL = SLAB
        for i in range(NSLAB):
            v.wait_ge(sm.h, BM + 8 * i + 1)
            v.tensor_single_scalar(r[0:1, 0:SL], pb[0][0:1, 0:SL], -inv, ALU.mult)
            v.tensor_single_scalar(r[0:1, SL:2 * SL], pb[0][0:1, SL:2 * SL], inv,
                                   ALU.mult).then_inc(sv.h, 1)
            v.wait_ge(sv.h, BV + 8 * i + 1)
            v.tensor_tensor(r[0:1, 2 * SL:3 * SL], r[0:1, 0:SL], r[0:1, 0:SL],
                            ALU.mult).then_inc(sv.h, 1)
            v.wait_ge(sv.h, BV + 8 * i + 2)
            v.tensor_tensor(r[0:1, 3 * SL:4 * SL], r[0:1, SL:2 * SL],
                            r[0:1, 2 * SL:3 * SL], ALU.subtract).then_inc(sv.h, 1)
            v.wait_ge(sv.h, BV + 8 * i + 3)
            v.tensor_single_scalar(r[0:1, 3 * SL:4 * SL], r[0:1, 3 * SL:4 * SL],
                                   1e-5, ALU.add).then_inc(sv.h, 1)
            v.wait_ge(sa.h, BA + 8 * i + 2)
            v.reciprocal(r[0:1, 5 * SL:6 * SL], r[0:1, 4 * SL:5 * SL]).then_inc(sv.h, 1)
            v.wait_ge(sv.h, BV + 8 * i + 5)
            v.tensor_copy(browbf[0:1, 0:SL], r[0:1, 0:SL])
            v.tensor_copy(browbf[0:1, SL:2 * SL], r[0:1, 5 * SL:6 * SL]).then_inc(sv.h, 1)
            v.wait_ge(sm.h, BM + 8 * i + 2)
            for c in range(NDC):
                ins = v.tensor_tensor(lnsrc(i, c), xslab[i % 2][:, c, :],
                                      pb[1][:, 0:SL], ALU.add)
            ins.then_inc(sv.h, 1)
            v.wait_ge(sv.h, BV + 8 * i + 7)
            for c in range(NDC):
                ins = v.tensor_tensor(lnsrc(i, c), lnsrc(i, c),
                                      pb[1][:, SL:2 * SL], ALU.mult)
            ins.then_inc(sv.h, 1)
            sv.v += 8
    blk.vector(_)

    def _(t):
        onecol = ones_sb[:, 0:1]
        onerow = ones_sb[0:1, :]
        SL = SLAB
        for i in range(NSLAB):
            t.wait_ge(sa.h, BA + 8 * i + 1)
            if i >= 1:
                t.wait_ge(sv.h, BV + 8 * (i - 1) + 1)
            for c in range(NDC):
                t.matmul(pb[0][0:1, 0:SL], onecol, xb_sb[:, c, :],
                         start=(c == 0), stop=(c == NDC - 1))
            for c in range(NDC):
                mm = t.matmul(pb[0][0:1, SL:2 * SL], onecol, sqb_sb[:, c, :],
                              start=(c == 0), stop=(c == NDC - 1))
            mm.then_inc(sm.h, 1)
            sm.v += 1
            t.wait_ge(sv.h, BV + 8 * i + 6)
            t.matmul(pb[1][:, 0:SL], onerow, browbf[0:1, 0:SL], start=True, stop=True)
            t.matmul(pb[1][:, SL:2 * SL], onerow, browbf[0:1, SL:2 * SL],
                     start=True, stop=True).then_inc(sm.h, 1)
            sm.v += 1
            t.wait_ge(sv.h, BV + 8 * i + 8)
            for g in range(6):
                if g < 4:
                    bank = pb[2 + (g % 2)]
                    if g >= 2:
                        t.wait_ge(sa.h, BA + 8 * i + 3 + (g - 2))
                    elif i >= 1:
                        t.wait_ge(sa.h, BA + 8 * (i - 1) + 3 + (g + 2))
                    for c in range(NDC):
                        mm = t.matmul(bank[:, 0:SL],
                                      qkvw_sb[:, c, 128 * g: 128 * (g + 1)],
                                      lnsrc(i, c), start=(c == 0), stop=(c == NDC - 1))
                else:
                    bank = pb[4 + (g - 4)]
                    if i >= 1:
                        t.wait_ge(sa.h, BA + 8 * (i - 1) + 3 + g)
                    sub = g - 4
                    for c in range(NDC):
                        mm = t.matmul(bank[:, 0:256],
                                      lnsrc(i, c)[:, 128 * sub: 128 * (sub + 1)],
                                      qkvw_sb[:, c, 512:768],
                                      start=(c == 0), stop=(c == NDC - 1))
                mm.then_inc(sm.h, 1)
                sm.v += 1
    blk.tensor(_)

    P1_SA, P1_SM, P1_SV = sa.v, sm.v, sv.v

    qlf = scr("qlf", [128, 2, 256], F32, 0)
    klf = scr("klf", [128, 2, 256], F32, 2048)
    ql_bf = scr("ql_bf", [128, 2, 256], BF16, 4096)
    kl_bf = scr("kl_bf", [128, 2, 256], BF16, 5120)
    E2f = scr("E2f", [128, 2, 256], F32, 6144)
    nmcol = scr("nmcol", [128, 2], F32, 8192)
    sums2 = scr("sums2", [128, 2], F32, 8256)
    rs2 = scr("rs2", [128, 2], F32, 8320)
    RS = scr("RS", [128, 8], F32, 8384)
    RSbf = scr("RSbf", [128, 8], BF16, 8448)
    rm8 = scr("rm8", [128, 1], F32, 8480)
    rm8bf = scr("rm8bf", [128, 1], BF16, 8512)
    csr = scr("csr", [1, 8], F32, 8544)
    scrow = scr("scrow", [1, 8], F32, 45248)
    scbf = scr("scbf", [1, 2], BF16, 8608)
    rsc = scr("rsc", [128, 1], F32, 8640)
    r3row = scr("r3row", [1, 256], F32, 8704)
    r3bf = scr("r3bf", [1, 256], BF16, 9728)
    r3c = scr("r3c", [128, 2], F32, 10240)
    ikc_sb = scr("ikc_sb", [128, 2, 768], BF16, 10304)
    r1row = scr("r1row", [1, 512], F32, 10304)
    r1bf = scr("r1bf", [1, 512], BF16, 12352)
    A2bf4 = scr("A2bf4", [128, 2, 1024], BF16, 13376)
    A2T4 = scr("A2T4", [128, 2, 1024], BF16, 17472)
    zb = [scr(f"zb{i}", [128, 2, 256], BF16, 21568 + 1024 * i) for i in range(2)]
    zTb = [scr(f"zTb{i}", [128, 2, 256], BF16, 23616 + 1024 * i) for i in range(2)]
    xzTbf = scr("xzTbf", [128, 2, 256], BF16, 25664)
    w1bf = scr("w1bf", [128, 2, 256], BF16, 26688)
    w1Tbf = scr("w1Tbf", [128, 2, 256], BF16, 27712)
    w2bf = scr("w2bf", [128, 2, 256], BF16, 28736)
    w2Tbf = scr("w2Tbf", [128, 2, 256], BF16, 29760)
    w4bf = scr("w4bf", [128, 2, 256], BF16, 30784)
    w4Tbf = scr("w4Tbf", [128, 2, 256], BF16, 31808)
    F4bf = scr("F4bf", [128, 2, 256], BF16, 32832)
    G4bf = scr("G4bf", [128, 2, 256], BF16, 33856)
    E3b = scr("E3b", [128, 2, 256], BF16, 34880)
    E1b = scr("E1b", [128, 2, 512], BF16, 35904)
    band_sb = scr("band_sb", [128, HN * 7 * 128], BF16, 37952)

    if PHASES >= 2:
        def _(sync):
            sync.wait_ge(sv.h, P1_SV)
            sync.dma_start(out=ikc_sb[:, :, :],
                           in_=ikc.rearrange("p (c n) -> p c n", c=2)).then_inc(sld.h, 16)
            sld.v += 16
            for r4 in range(4):
                if r4 > 0:
                    sync.wait_ge(sv.h, P1_SV + r4)
                w = 896
                sync.dma_start(out=wstg[:, 0:w],
                               in_=bands[:, w * r4: w * (r4 + 1)]).then_inc(sld.h, 16)
                sld.v += 16
        blk.sync(_)
        BLD = sld.v

        def _(v):
            for r4 in range(4):
                v.wait_ge(sld.h, BLD - 16 * (3 - r4))
                w = 896
                ins = v.tensor_copy(band_sb[:, w * r4: w * (r4 + 1)], wstg[:, 0:w])
                ins.then_inc(sv.h, 1)
                sv.v += 1
        blk.vector(_)
        BCAST2 = sv.v

        def _(v):
            v.wait_ge(sa.h, P1_SA)
            v.wait_ge(sm.h, P1_SM)
            v.wait_ge(sv.h, BCAST2)
            for hc in range(2):
                v.tensor_reduce(qlf[:, hc, :],
                                qTs[:, hc, :].rearrange("p (l g) -> p l g", g=16),
                                AX.X, ALU.add)
                ins = v.tensor_reduce(klf[:, hc, :],
                                      kTs[:, hc, :].rearrange("p (l g) -> p l g", g=16),
                                      AX.X, ALU.add)
            ins.then_inc(sv.h, 1)
            sv.v += 1
            v.wait_ge(sv.h, sv.v)
            for hc in range(2):
                v.tensor_single_scalar(ql_bf[:, hc, :], qlf[:, hc, :], 1.0 / 16, ALU.mult)
                ins = v.tensor_single_scalar(kl_bf[:, hc, :], klf[:, hc, :], 1.0 / 16, ALU.mult)
            ins.then_inc(sv.h, 1)
            sv.v += 1
        blk.vector(_)
        LM = sv.v

        B2M, B2V, B2A = sm.v, sv.v, sa.v

        def _(t):
            for h in range(4):
                hc, off = h // 2, 64 * (h % 2)
                if h == 0:
                    t.wait_ge(sv.h, LM)
                else:
                    t.wait_ge(sa.h, B2A + 2 * (h - 1) + 1)
                for lc in range(2):
                    mm = t.matmul(pb[lc][:, 0:256],
                                  ql_bf[off:off + 64, hc, 128 * lc:128 * (lc + 1)],
                                  kl_bf[off:off + 64, hc, :],
                                  start=True, stop=True)
                mm.then_inc(sm.h, 1)
                sm.v += 1
                t.wait_ge(sv.h, B2V + 4 * h + 3)
                if h >= 1:
                    t.wait_ge(sv.h, B2V + 4 * (h - 1) + 4)
                for lc in range(2):
                    mm = t.matmul(pb[2][0:1, 0:256], ones_sb[:, 0:1],
                                  A2bf4[:, lc, 256 * h:256 * (h + 1)],
                                  start=(lc == 0), stop=(lc == 1))
                mm.then_inc(sm.h, 1)
                sm.v += 1
                if h >= 1:
                    t.wait_ge(sa.h, B2A + 2 * (h - 1) + 2)
                for lc in range(2):
                    for cc in range(2):
                        ins = t.transpose(
                            pbt[:, 256 * lc + 128 * cc: 256 * lc + 128 * (cc + 1)],
                            A2bf4[:, lc, 256 * h + 128 * cc: 256 * h + 128 * (cc + 1)],
                            ident_sb[:, :])
                ins.then_inc(sm.h, 1)
                sm.v += 1
        blk.tensor(_)

        def _(v):
            for h in range(4):
                v.wait_ge(sm.h, B2M + 3 * h + 1)
                for lc in range(2):
                    ins = v.tensor_reduce(nmcol[:, lc:lc + 1], pb[lc][:, 0:256], AX.X,
                                          ALU.max, negate=True)
                ins.then_inc(sv.h, 1)
                v.wait_ge(sa.h, B2A + 2 * h + 1)
                v.reciprocal(rs2[:, 0:2], sums2[:, 0:2]).then_inc(sv.h, 1)
                v.wait_ge(sv.h, B2V + 4 * h + 2)
                for lc in range(2):
                    ins = v.tensor_single_scalar(A2bf4[:, lc, 256 * h:256 * (h + 1)],
                                                 E2f[:, lc, :], rs2[:, lc:lc + 1], ALU.mult)
                ins.then_inc(sv.h, 1)
                v.wait_ge(sv.h, B2V + 4 * h + 3)
                for lc in range(2):
                    v.tensor_reduce(RS[:, 2 * h + lc: 2 * h + lc + 1],
                                    A2bf4[:, lc, 256 * h:256 * (h + 1)], AX.X, ALU.add)
                v.wait_ge(sm.h, B2M + 3 * h + 2)
                ins = v.tensor_reduce(csr[0:1, h:h + 1], pb[2][0:1, 0:256], AX.X, ALU.max)
                ins.then_inc(sv.h, 1)
                sv.v += 4
        blk.vector(_)

        def _(s):
            for h in range(4):
                s.wait_ge(sv.h, B2V + 4 * h + 1)
                for lc in range(2):
                    s.activation(E2f[:, lc, :], pb[lc][:, 0:256], AF.Exp,
                                 bias=nmcol[:, lc:lc + 1],
                                 accum_out=sums2[:, lc:lc + 1]).then_inc(
                                     sa.h, (1 if lc == 1 else 0))
                sa.v += 1
                s.wait_ge(sm.h, B2M + 3 * h + 3)
                for lc in range(2):
                    for cc in range(2):
                        ins = s.activation(
                            A2T4[:, cc, 256 * h + 128 * lc: 256 * h + 128 * (lc + 1)],
                            pbt[:, 256 * lc + 128 * cc: 256 * lc + 128 * (cc + 1)],
                            AF.Copy)
                ins.then_inc(sa.h, 1)
                sa.v += 1
        blk.scalar(_)
        S2M, S2V, S2A = sm.v, sv.v, sa.v

        def _(v):
            v.wait_ge(sv.h, S2V)
            v.tensor_copy(RSbf[:, :], RS[:, :]).then_inc(sv.h, 1)
            sv.v += 1
        blk.vector(_)

        def _(t):
            t.wait_ge(sv.h, sv.v)
            t.wait_ge(sa.h, S2A)
            t.transpose(pbt[0:8, 0:128], RSbf[:, :], ident_sb[:, :]).then_inc(sm.h, 1)
            sm.v += 1
        blk.tensor(_)
        TR1 = sm.v

        def _(v):
            v.wait_ge(sm.h, TR1)
            v.tensor_reduce(rm8[0:8, 0:1], pbt[0:8, 0:128], AX.X,
                            ALU.max).then_inc(sv.h, 1)
            sv.v += 1
            v.wait_ge(sv.h, sv.v)
            v.tensor_copy(rm8bf[0:8, 0:1], rm8[0:8, 0:1]).then_inc(sv.h, 1)
            sv.v += 1
        blk.vector(_)

        def _(t):
            t.wait_ge(sv.h, sv.v)
            t.transpose(pbt[0:1, 128:136], rm8bf[0:8, 0:1], ident_sb[0:8, 0:8]).then_inc(sm.h, 1)
            sm.v += 1
        blk.tensor(_)
        TR2 = sm.v

        def _(v):
            v.wait_ge(sm.h, TR2)
            v.tensor_reduce(scrow[0:1, 0:1], pbt[0:1, 128:136], AX.X, ALU.max)
            v.tensor_reduce(scrow[0:1, 1:2], csr[0:1, 0:4], AX.X,
                            ALU.max).then_inc(sv.h, 1)
            sv.v += 1
        blk.vector(_)
        SCROW = sv.v

        def _(sync):
            sync.wait_ge(sv.h, SCROW)
            sync.dma_start(out=cc_scale_i, in_=scrow[0:1, 0:2]).then_inc(sci.h, 16)
            sci.v += 16
        blk.sync(_)
        SCW = sci.v

        def _(g):
            g.wait_ge(sci.h, SCW)
            g.collective_compute(
                "AllReduce", ALU.max, replica_groups=[list(range(NCORES))],
                ins=[cc_scale_i.opt()], outs=[cc_scale_o.opt()],
            ).then_inc(scc.h)
            scc.v += 1
        blk.gpsimd(_)

        def _(sync):
            sync.wait_ge(scc.h, scc.v)
            sync.dma_start(out=scrow[0:1, 2:4], in_=cc_scale_o).then_inc(sst.h, 16)
            sst.v += 16
        blk.sync(_)
        SCR2 = sst.v

        def _(v):
            v.wait_ge(sst.h, SCR2)
            v.tensor_tensor(scrow[0:1, 4:5], scrow[0:1, 2:3], scrow[0:1, 3:4],
                            ALU.mult).then_inc(sv.h, 1)
            sv.v += 1
            v.wait_ge(sv.h, sv.v)
            v.reciprocal(scrow[0:1, 5:6], scrow[0:1, 4:5]).then_inc(sv.h, 1)
            sv.v += 1
            v.wait_ge(sv.h, sv.v)
            v.tensor_copy(scbf[0:1, 0:1], scrow[0:1, 5:6]).then_inc(sv.h, 1)
            sv.v += 1
        blk.vector(_)
        RSCBF = sv.v

        def _(t):
            t.wait_ge(sv.h, RSCBF)
            t.matmul(pb[5][:, 0:1], ones_sb[0:1, :], scbf[0:1, 0:1],
                     start=True, stop=True).then_inc(sm.h, 1)
            sm.v += 1
        blk.tensor(_)
        RSCMM = sm.v

        def _(v):
            v.wait_ge(sm.h, RSCMM)
            v.tensor_copy(rsc[:, 0:1], pb[5][:, 0:1]).then_inc(sv.h, 1)
            sv.v += 1
        blk.vector(_)
        RSC = sv.v

        for h in range(HN):
            hc, off = h // 2, 64 * (h % 2)
            B5M, B5A, B5V = sm.v, sa.v, sv.v

            def _(t, h=h, hc=hc, off=off, B5M=B5M, B5A=B5A, B5V=B5V):
                t.wait_ge(sv.h, B5V)
                for tc in range(32):
                    if tc >= 1:
                        t.wait_ge(sa.h, B5A + tc)
                    elif h == 0:
                        t.wait_ge(sv.h, LM)
                    t.matmul(pb[0][:, 0:256],
                             kTs[off:off + 64, hc, 128 * tc:128 * (tc + 1)],
                             ql_bf[off:off + 64, hc, :],
                             start=True, stop=True).then_inc(sm.h, 1)
                    sm.v += 1
                    t.wait_ge(sa.h, B5A + tc + 1)
                    t.matmul(pb[3][0:1, 0:256], ones_sb[:, 0:1], E3b[:, tc % 2, :],
                             start=(tc == 0), stop=(tc == 31))
                    for lc in range(2):
                        mm = t.matmul(pb[1 + lc][:, 0:64],
                                      E3b[:, tc % 2, 128 * lc:128 * (lc + 1)],
                                      vtok[:, tc, 64 * h:64 * (h + 1)],
                                      start=(tc == 0), stop=(tc == 31))
                    mm.then_inc(sm.h, 1)
                    sm.v += 1
            blk.tensor(_)

            def _(s, B5M=B5M):
                for tc in range(32):
                    s.wait_ge(sm.h, B5M + 2 * tc + 1)
                    if tc >= 2:
                        s.wait_ge(sm.h, B5M + 2 * (tc - 2) + 2)
                    s.activation(E3b[:, tc % 2, :], pb[0][:, 0:256],
                                 AF.Exp).then_inc(sa.h, 1)
                    sa.v += 1
            blk.scalar(_)
            E3M = sm.v

            def _(v, h=h, E3M=E3M):
                v.wait_ge(sm.h, E3M)
                v.reciprocal(r3row[0:1, 0:256], pb[3][0:1, 0:256]).then_inc(sv.h, 1)
                sv.v += 1
                v.wait_ge(sv.h, sv.v)
                v.tensor_copy(r3bf[0:1, 0:256], r3row[0:1, 0:256]).then_inc(sv.h, 1)
                sv.v += 1
            blk.vector(_)
            R3V = sv.v

            def _(t, B2A_l=sa.v, R3V=R3V):
                t.wait_ge(sv.h, R3V)
                if h > 0:
                    t.wait_ge(sa.h, B2A_l)
                for lc in range(2):
                    ins = t.transpose(pbt[:, 2 * lc:2 * lc + 1],
                                      r3bf[0:1, 128 * lc:128 * (lc + 1)],
                                      ident_sb[0:1, 0:1])
                ins.then_inc(sm.h, 1)
                sm.v += 1
            blk.tensor(_)
            R3T = sm.v

            def _(s, R3T=R3T):
                s.wait_ge(sm.h, R3T)
                for lc in range(2):
                    ins = s.activation(r3c[:, lc:lc + 1], pbt[:, 2 * lc:2 * lc + 1],
                                       AF.Copy)
                ins.then_inc(sa.h, 1)
                sa.v += 1
            blk.scalar(_)
            R3C = sa.v

            def _(v, h=h, R3C=R3C):
                v.wait_ge(sa.h, R3C)
                for lc in range(2):
                    ins = v.tensor_single_scalar(F4bf[:, lc, 64 * h:64 * (h + 1)],
                                                 pb[1 + lc][:, 0:64],
                                                 r3c[:, lc:lc + 1], ALU.mult)
                ins.then_inc(sv.h, 1)
                sv.v += 1
            blk.vector(_)

        IK = {7.0: 0, 15.0: 1, 13.0: 2}

        for h in range(HN):
            B4V = sv.v

            def _(v, h=h, B4V=B4V, GPREV=sm.v, ILD=sld.v):
                v.wait_ge(sv.h, RSC)
                v.wait_ge(sm.h, GPREV)
                v.wait_ge(sld.h, ILD)
                for c in range(2):
                    v.tensor_single_scalar(zb[0][:, c, :],
                                           A2T4[:, c, 256 * h:256 * (h + 1)],
                                           rsc[:, 0:1], ALU.mult)
                    ins = v.tensor_single_scalar(zTb[0][:, c, :],
                                                 A2bf4[:, c, 256 * h:256 * (h + 1)],
                                                 rsc[:, 0:1], ALU.mult)
                ins.then_inc(sv.h, 1)
                sv.v += 1
            blk.vector(_)

            cur, curT = zb[0], zTb[0]
            nxt, nxtT = zb[1], zTb[1]
            A2T_h = lambda kc, oc, h=h: A2T4[:, kc, 256 * h + 128 * oc: 256 * h + 128 * (oc + 1)]

            for it in range(6):
                prods = [
                    (lambda kc, oc: A2T_h(kc, oc), cur, [(w1bf, ("isub", 0))]),
                    (lambda kc, oc, cur=cur: cur[:, kc, 128 * oc:128 * (oc + 1)],
                     A2T4[:, :, 256 * h:256 * (h + 1)],
                     [(xzTbf, ("copy",)), (w1Tbf, ("isub", 0))]),
                    (lambda kc, oc: xzTbf[:, kc, 128 * oc:128 * (oc + 1)], w1bf,
                     [(w2bf, ("isub", 1))]),
                    (lambda kc, oc: w1bf[:, kc, 128 * oc:128 * (oc + 1)], xzTbf,
                     [(w2Tbf, ("isub", 1))]),
                    (lambda kc, oc: xzTbf[:, kc, 128 * oc:128 * (oc + 1)], w2bf,
                     [(w4bf, ("isub", 2))]),
                    (lambda kc, oc: w2bf[:, kc, 128 * oc:128 * (oc + 1)], xzTbf,
                     [(w4Tbf, ("isub", 2))]),
                    (lambda kc, oc, curT=curT: curT[:, kc, 128 * oc:128 * (oc + 1)],
                     w4bf, [(nxt, ("scale", 0.25))]),
                    (lambda kc, oc, cur=cur: w4bf[:, kc, 128 * oc:128 * (oc + 1)],
                     curT, [(nxtT, ("scale", 0.25))]),
                ]
                for pi, (lf, rhs_t, evacs) in enumerate(prods):
                    bank = 2 * (pi % 2)
                    PV = sv.v

                    def _(t, lf=lf, rhs_t=rhs_t, bank=bank, PV=PV):
                        t.wait_ge(sv.h, PV)
                        for oc in range(2):
                            for kc in range(2):
                                mm = t.matmul(pb[bank + oc][:, 0:256], lf(kc, oc),
                                              rhs_t[:, kc, :],
                                              start=(kc == 0), stop=(kc == 1))
                        mm.then_inc(sm.h, 1)
                        sm.v += 1
                    blk.tensor(_)
                    PM = sm.v

                    def _(v, evacs=evacs, bank=bank, PM=PM):
                        v.wait_ge(sm.h, PM)
                        for dst, mode in evacs:
                            for oc in range(2):
                                if mode[0] == "isub":
                                    ins = v.tensor_tensor(
                                        dst[:, oc, :],
                                        ikc_sb[:, oc, 256 * mode[1]:256 * (mode[1] + 1)],
                                        pb[bank + oc][:, 0:256], ALU.subtract)
                                elif mode[0] == "copy":
                                    ins = v.tensor_copy(dst[:, oc, :], pb[bank + oc][:, 0:256])
                                else:
                                    ins = v.tensor_single_scalar(
                                        dst[:, oc, :], pb[bank + oc][:, 0:256],
                                        mode[1], ALU.mult)
                        ins.then_inc(sv.h, 1)
                        sv.v += 1
                    blk.vector(_)
                cur, nxt = nxt, cur
                curT, nxtT = nxtT, curT

            GV = sv.v

            def _(t, h=h, curT=curT, GV=GV):
                t.wait_ge(sv.h, GV)
                for oc in range(2):
                    for kc in range(2):
                        mm = t.matmul(pb[4 + oc][:, 0:64],
                                      curT[:, kc, 128 * oc:128 * (oc + 1)],
                                      F4bf[:, kc, 64 * h:64 * (h + 1)],
                                      start=(kc == 0), stop=(kc == 1))
                mm.then_inc(sm.h, 1)
                sm.v += 1
            blk.tensor(_)
            GM = sm.v

            def _(v, h=h, GM=GM):
                v.wait_ge(sm.h, GM)
                for oc in range(2):
                    ins = v.tensor_copy(G4bf[:, oc, 64 * h:64 * (h + 1)],
                                        pb[4 + oc][:, 0:64])
                ins.then_inc(sv.h, 1)
                sv.v += 1
            blk.vector(_)

        for h in range(HN):
            hc, off = h // 2, 64 * (h % 2)
            for tt in range(8):
                B6V = sv.v
                B6A = sa.v

                def _(t, hc=hc, off=off, tt=tt, B6V=B6V, B6A=B6A):
                    t.wait_ge(sv.h, B6V)
                    for lc in range(2):
                        if lc == 1 or tt > 0 or True:
                            t.wait_ge(sa.h, B6A + lc)
                        t.matmul(pb[0][:, 0:512],
                                 kl_bf[off:off + 64, hc, 128 * lc:128 * (lc + 1)],
                                 qTs[off:off + 64, hc, 512 * tt:512 * (tt + 1)],
                                 start=True, stop=True).then_inc(sm.h, 1)
                        sm.v += 1
                        t.wait_ge(sa.h, B6A + lc + 1)
                        t.matmul(pb[1][0:64, 0:512], G4bf[:, lc, 64 * h:64 * (h + 1)],
                                 E1b[:, lc, :], start=(lc == 0), stop=(lc == 1))
                        mm = t.matmul(pb[2][0:1, 0:512], ones_sb[:, 0:1],
                                      E1b[:, lc, :], start=(lc == 0), stop=(lc == 1))
                        mm.then_inc(sm.h, 1)
                        sm.v += 1
                blk.tensor(_)
                E1M0 = sm.v - 4

                def _(s, E1M0=E1M0):
                    for lc in range(2):
                        s.wait_ge(sm.h, E1M0 + 2 * lc + 1)
                        s.activation(E1b[:, lc, :], pb[0][:, 0:512],
                                     AF.Exp).then_inc(sa.h, 1)
                        sa.v += 1
                blk.scalar(_)
                E1M = sm.v

                def _(v, E1M=E1M):
                    v.wait_ge(sm.h, E1M)
                    v.reciprocal(r1row[0:1, 0:512], pb[2][0:1, 0:512]).then_inc(sv.h, 1)
                    sv.v += 1
                    v.wait_ge(sv.h, sv.v)
                    v.tensor_copy(r1bf[0:1, 0:512], r1row[0:1, 0:512]).then_inc(sv.h, 1)
                    sv.v += 1
                blk.vector(_)
                R1V = sv.v

                def _(t, R1V=R1V):
                    t.wait_ge(sv.h, R1V)
                    t.matmul(pb[3][0:64, 0:512], ones_sb[0:1, 0:64],
                             r1bf[0:1, 0:512], start=True, stop=True).then_inc(sm.h, 1)
                    sm.v += 1
                blk.tensor(_)
                R1M = sm.v

                def _(s, hc=hc, off=off, tt=tt, E1M=E1M):
                    s.wait_ge(sm.h, E1M)
                    s.activation(outTo[off:off + 64, hc, 512 * tt:512 * (tt + 1)],
                                 pb[1][0:64, 0:512], AF.Copy).then_inc(sa.h, 1)
                    sa.v += 1
                blk.scalar(_)
                OA = sa.v

                def _(v, hc=hc, off=off, tt=tt, OA=OA, R1M=R1M):
                    v.wait_ge(sa.h, OA)
                    v.wait_ge(sm.h, R1M)
                    v.tensor_tensor(outTo[off:off + 64, hc, 512 * tt:512 * (tt + 1)],
                                    outTo[off:off + 64, hc, 512 * tt:512 * (tt + 1)],
                                    pb[3][0:64, 0:512], ALU.mult).then_inc(sv.h, 1)
                    sv.v += 1
                blk.vector(_)

            CB = sv.v

            def _(t, h=h, CB=CB):
                t.wait_ge(sv.h, CB)
                for tc in range(32):
                    if tc >= 2:
                        t.wait_ge(sv.h, CB + tc - 1)
                    srcs = []
                    if tc == 0:
                        srcs.append((31, 3))
                    elif tc == 16:
                        srcs.append((15, 5))
                    else:
                        srcs.append((tc - 1, 0))
                    srcs.append((tc, 1))
                    if tc == 31:
                        srcs.append((0, 4))
                    elif tc == 15:
                        srcs.append((16, 6))
                    else:
                        srcs.append((tc + 1, 2))
                    for si, (sc_, bi) in enumerate(srcs):
                        mm = t.matmul(pb[4 + tc % 2][0:64, 0:128],
                                      vtok[:, sc_, 64 * h:64 * (h + 1)],
                                      band_sb[:, (h * 7 + bi) * 128:(h * 7 + bi + 1) * 128],
                                      start=(si == 0), stop=(si == 2))
                    mm.then_inc(sm.h, 1)
                    sm.v += 1
            blk.tensor(_)
            CM0 = sm.v - 32

            def _(v, h=h, hc=hc, off=off, CM0=CM0, CBV=sv.v):
                v.wait_ge(sv.h, CBV)
                for tc in range(32):
                    v.wait_ge(sm.h, CM0 + tc + 1)
                    v.tensor_tensor(outTo[off:off + 64, hc, 128 * tc:128 * (tc + 1)],
                                    outTo[off:off + 64, hc, 128 * tc:128 * (tc + 1)],
                                    pb[4 + tc % 2][0:64, 0:128], ALU.add).then_inc(sv.h, 1)
                    sv.v += 1
            blk.vector(_)

    readbuf = at("readbuf", [128, 4, TH], BF16, offset=98304)
    wv_bf = at("wv_bf", [128, NDC, D], BF16, offset=114688)
    wvstage = at("wvstage", [128, 2, D], F32, offset=131072)
    xgstage = at("xgstage", [128, NDC, 512], BF16, offset=139264)
    owo_bf = scr("owo_bf", [128, 4, D], BF16, 8192)
    owm_bf = scr("owm_bf", [128, 2, D], BF16, 16384)
    wqk_bf = scr("wqk_bf", [128, NDC, 512], BF16, 20480)
    q0stage = scr("q0stage", [128, 16, 512], BF16, 28672)
    xg0buf = [at(f"xg0buf{i}", [128, 1, TH], BF16, offset=98304 + 4096 * i)
              for i in range(2)]

    if PHASES >= 3:
        P2_SV, P2_SA, P2_SM = sv.v, sa.v, sm.v

        def _(sync):
            sync.wait_ge(sv.h, P2_SV)
            sync.dma_start(out=cc_out_i.rearrange("(c p) t -> p c t", p=128),
                           in_=outTo[:, :, :]).then_inc(sci.h, 16)
            sci.v += 16
        blk.sync(_)
        OUTW_CI = sci.v

        def _(g):
            g.wait_ge(sci.h, OUTW_CI)
            g.collective_compute(
                "AllGather", ALU.bypass,
                replica_groups=[[0, 1], [2, 3], [4, 5], [6, 7]],
                ins=[cc_out_i.opt()], outs=[cc_out_o.opt()],
            ).then_inc(scc.h)
            scc.v += 1
        blk.gpsimd(_)
        OUT_AG = scc.v

        def _(sync):
            sync.wait_ge(scc.h, OUT_AG)
            sync.dma_start(
                out=readbuf[:, :, :],
                in_=cc_out_o.rearrange("(c p) t -> p c t", p=128)[:, :, TH:T],
            ).then_inc(sx2[0].h, 16)
            sx2[0].v += 16
            sync.dma_start(out=wstg[:, 0:2048].rearrange("p (c n) -> p c n", c=2),
                           in_=outw_o.rearrange("(c p) n -> p c n", p=128)[:, 0:2, :]
                           ).then_inc(sld.h, 16)
            sld.v += 16
        blk.sync(_)
        RB_LD = sx2[0].v

        def _(sync):
            base = sld.v

            seq = [
                (outw_o.rearrange("(c p) n -> p c n", p=128)[:, 2:4, :], 2),
                (outw_m.rearrange("(c p) n -> p c n", p=128)[:, 0:2, :], 2),
                (wqk.rearrange("(c p) n -> p c n", p=128)[:, 0:4, :], 4),
                (wqk.rearrange("(c p) n -> p c n", p=128)[:, 4:8, :], 4),
            ]
            for i, (srcv, nch) in enumerate(seq):
                sync.wait_ge(sv.h, P2_SV + i + 1)
                sync.dma_start(out=wstg[:, 0:2048].rearrange("p (c n) -> p c n", c=nch),
                               in_=srcv).then_inc(sld.h, 16)
                sld.v += 16
            for r in range(4):
                sync.wait_ge(sv.h, P2_SV + 5 + r)
                sync.dma_start(out=wvstage[:, :, :],
                               in_=wv.rearrange("(c p) n -> p c n", p=128)[:, 2 * r:2 * r + 2, :]
                               ).then_inc(sld.h, 16)
                sld.v += 16
        blk.sync(_)

        def _(v):
            base = sld.v - 16 * 9
            v.wait_ge(sld.h, base + 16)
            v.tensor_copy(owo_bf[:, 0:2, :],
                          wstg[:, 0:2048].rearrange("p (c n) -> p c n", c=2)).then_inc(sv.h, 1)
            v.wait_ge(sld.h, base + 32)
            v.tensor_copy(owo_bf[:, 2:4, :],
                          wstg[:, 0:2048].rearrange("p (c n) -> p c n", c=2)).then_inc(sv.h, 1)
            v.wait_ge(sld.h, base + 48)
            v.tensor_copy(owm_bf[:, :, :],
                          wstg[:, 0:2048].rearrange("p (c n) -> p c n", c=2)).then_inc(sv.h, 1)
            v.wait_ge(sld.h, base + 64)
            v.tensor_copy(wqk_bf[:, 0:4, :],
                          wstg[:, 0:2048].rearrange("p (c n) -> p c n", c=4)).then_inc(sv.h, 1)
            v.wait_ge(sld.h, base + 80)
            v.tensor_copy(wqk_bf[:, 4:8, :],
                          wstg[:, 0:2048].rearrange("p (c n) -> p c n", c=4)).then_inc(sv.h, 1)
            for r in range(4):
                v.wait_ge(sld.h, base + 96 + 16 * r)
                v.tensor_copy(wv_bf[:, 2 * r:2 * r + 2, :],
                              wvstage[:, :, :]).then_inc(sv.h, 1)
            sv.v += 9
        blk.vector(_)
        WCAST3 = sv.v

        B3M, B3A = sm.v, sa.v

        def _(t):
            t.wait_ge(sv.h, WCAST3)
            t.wait_ge(sx2[0].h, RB_LD)
            for jt in range(4):
                for d2 in range(NDC):
                    bank = pb[d2 % 2]
                    idx = jt * NDC + d2
                    if idx >= 2:
                        t.wait_ge(sa.h, B3A + idx - 1)
                    first = True
                    for cch in range(4):
                        t.matmul(bank[:, 0:512], owo_bf[:, cch, 128 * d2:128 * (d2 + 1)],
                                 readbuf[:, cch, 512 * jt:512 * (jt + 1)],
                                 start=first, stop=False)
                        first = False
                    for hc in range(2):
                        mm = t.matmul(bank[:, 0:512],
                                      owm_bf[:, hc, 128 * d2:128 * (d2 + 1)],
                                      outTo[:, hc, 512 * jt:512 * (jt + 1)],
                                      start=False, stop=(hc == 1))
                    mm.then_inc(sm.h, 1)
                    sm.v += 1
        blk.tensor(_)

        def _(s, B3CI=sci.v):
            for jt in range(4):
                for d2 in range(NDC):
                    idx = jt * NDC + d2
                    s.wait_ge(sm.h, B3M + idx + 1)
                    if d2 == 0 and jt >= 1:
                        s.wait_ge(sci.h, B3CI + 16 * jt)
                    s.activation(xgstage[:, d2, :], pb[d2 % 2][:, 0:512], AF.Identity,
                                 bias=outb_sb[:, d2:d2 + 1]).then_inc(sa.h, 1)
                    sa.v += 1
        blk.scalar(_)
        XGA = sa.v

        def _(sync):
            for jt in range(4):
                sync.wait_ge(sa.h, B3A + NDC * (jt + 1))
                sync.dma_start(
                    out=cc_xg_i.rearrange("(c p) t -> p c t", p=128)[:, :, 512 * jt:512 * (jt + 1)],
                    in_=xgstage[:, :, :]).then_inc(sci.h, 16)
                sci.v += 16
        blk.sync(_)
        XG_CI = sci.v

        def _(g):
            g.wait_ge(sci.h, XG_CI)
            g.collective_compute(
                "AllGather", ALU.bypass,
                replica_groups=[[0, 2, 4, 6], [1, 3, 5, 7]],
                ins=[cc_xg_i.opt()], outs=[cc_xg_o.opt()],
            ).then_inc(scc.h)
            scc.v += 1
        blk.gpsimd(_)
        XG_AG = scc.v

        sxg = [S(nc, "sxg0"), S(nc, "sxg1")]
        ENC0 = sv.v

        def _(sync):
            sync.wait_ge(scc.h, XG_AG)
            for ch in range(NDC):
                if ch >= 2:
                    sync.wait_ge(sv.h, ENC0 + ch - 1)
                sync.dma_start(
                    out=xg0buf[ch % 2][:, 0, :],
                    in_=cc_xg_o.rearrange("(c p) t -> p c t", p=128)[:, ch, :],
                ).then_inc(sxg[ch % 2].h, 16)
                sxg[ch % 2].v += 16
        blk.sync(_)

        def _(v):
            for ch in range(NDC):
                v.wait_ge(sxg[ch % 2].h, 16 * (ch // 2 + 1))
                v.tensor_tensor(encT[:, ch, :], xg0buf[ch % 2][:, 0, :],
                                ln0h[:, ch, :], ALU.add).then_inc(sv.h, 1)
                sv.v += 1
        blk.vector(_)
        ENCV = sv.v

        B3M2, B3A2 = sm.v, sa.v

        def _(t):
            t.wait_ge(sv.h, WCAST3)
            for jt in range(4):
                for d2 in range(NDC):
                    bank = pb[2 + d2 % 2]
                    idx = jt * NDC + d2
                    if idx >= 2:
                        t.wait_ge(sa.h, B3A2 + idx - 1)
                    for dch in range(NDC):
                        mm = t.matmul(bank[:, 0:512],
                                      wv_bf[:, dch, 128 * d2:128 * (d2 + 1)],
                                      ln0h[:, dch, 512 * jt:512 * (jt + 1)],
                                      start=(dch == 0), stop=(dch == NDC - 1))
                    mm.then_inc(sm.h, 1)
                    sm.v += 1
        blk.tensor(_)

        def _(s):
            for jt in range(4):
                for d2 in range(NDC):
                    idx = jt * NDC + d2
                    s.wait_ge(sm.h, B3M2 + idx + 1)
                    s.activation(valT[:, d2, 512 * jt:512 * (jt + 1)],
                                 pb[2 + d2 % 2][:, 0:512], AF.Identity,
                                 bias=wvb_sb[:, d2:d2 + 1]).then_inc(sa.h, 1)
                    sa.v += 1
        blk.scalar(_)
        VALA = sa.v

    Qg = [at(f"Qg{i}", [128, 256], BF16, offset=32768 + 512 * i) for i in range(2)]
    Kg = [at(f"Kg{i}", [128, 256], BF16, offset=33792 + 512 * i) for i in range(2)]
    prod4 = at("prod4", [128, 256], F32, offset=34816)
    junk4 = at("junk4", [128, 256], F32, offset=35840)
    ctb = at("ctb", [128, NET], F32, offset=36864)
    av16 = at("av16", [128, NET], F32, offset=37376)
    hi_f = at("hi_f", [128, NET], F32, offset=37888)
    lo_f = at("lo_f", [128, NET], F32, offset=38400)
    hii4 = at("hii4", [128, NET], I32, offset=38912)
    loi4 = at("loi4", [128, NET], I32, offset=39424)
    adjc_cl = at("adjc_cl", [128, NET], I32, offset=46592)
    Ub = [at(f"Ub{i}", [128, 64], BF16, offset=39936 + 128 * i) for i in range(2)]
    Vb = [at(f"Vb{i}", [128, 64], BF16, offset=40192 + 128 * i) for i in range(2)]
    a2dr = at("a2dr", [64, 64], F32, offset=40448)
    acols = at("acols", [64, 8], F32, offset=40704)
    arow1 = at("arow1", [1, 64], F32, offset=40736)
    alpha2 = at("alpha2", [64, 64], F32, offset=41024)
    alpha2b = at("alpha2b", [64, 72], BF16, offset=41280)
    selout = at("selout", [32, 64], BF16, offset=41440)
    alpharow = scr("alpharow", [1, TH], BF16, 36992)
    a2dsb = at("a2dsb", [64, 64], F32, offset=46080)

    if PHASES >= 4:
        B4M, B4A = sm.v, sa.v

        def _(t):
            t.wait_ge(sv.h, ENCV)
            for tch in range(16):
                bank = pb[4 + tch % 2]
                if tch >= 2:
                    t.wait_ge(sa.h, B4A + tch - 1)
                for dch in range(NDC):
                    mm = t.matmul(bank[:, 0:512],
                                  encT[:, dch, 128 * tch:128 * (tch + 1)],
                                  wqk_bf[:, dch, :],
                                  start=(dch == 0), stop=(dch == NDC - 1))
                mm.then_inc(sm.h, 1)
                sm.v += 1
        blk.tensor(_)

        def _(s):
            for tch in range(16):
                s.wait_ge(sm.h, B4M + tch + 1)
                s.activation(q0stage[:, tch, :], pb[4 + tch % 2][:, 0:512],
                             AF.Copy).then_inc(sa.h, 1)
                sa.v += 1
        blk.scalar(_)
        Q0A = sa.v

        def _(sync):
            sync.wait_ge(sa.h, Q0A)
            sync.dma_start(out=cc_tab_i.rearrange("(c p) t -> p c t", p=128),
                           in_=q0stage[:, :, :]).then_inc(sci.h, 16)
            sci.v += 16
        blk.sync(_)
        TAB_CI = sci.v

        def _(g):
            g.wait_ge(sci.h, TAB_CI)
            g.collective_compute(
                "AllGather", ALU.bypass, replica_groups=[list(range(NCORES))],
                ins=[cc_tab_i.opt()], outs=[cc_tab_o.opt()],
            ).then_inc(scc.h)
            scc.v += 1
        blk.gpsimd(_)
        TAB_AG = scc.v

        def _(v):
            v.wait_ge(sld.h, P0_LD)
            v.wait_ge(sa.h, VALA)
            v.wait_ge(sv.h, ENCV)
            v.tensor_single_scalar(av16[:, :], adjv_sb[:, :], 1.0 / 16.0, ALU.mult)
            v.tensor_single_scalar(hii4[:, :], adjr_sb[:, :], 6, ALU.logical_shift_right)
            v.tensor_single_scalar(loi4[:, :], adjr_sb[:, :], 63, ALU.bitwise_and)
            v.tensor_single_scalar(adjc_cl[:, :], adjc_sb[:, :], 4095,
                                   ALU.min).then_inc(sv.h, 1)
            sv.v += 1
            v.wait_ge(sv.h, sv.v)
            v.tensor_copy(hi_f[:, :], hii4[:, :])
            v.tensor_copy(lo_f[:, :], loi4[:, :]).then_inc(sv.h, 1)
            sv.v += 1
        blk.vector(_)
        EPREP = sv.v

        BGV, BGA, BGM = sv.v, sa.v, sm.v
        sg2 = [S(nc, "sg20"), S(nc, "sg21")]

        def _(g):
            g.wait_ge(scc.h, TAB_AG)
            g.wait_ge(sv.h, EPREP)
            for et in range(NET):
                if et >= 2:
                    g.wait_ge(sv.h, BGV + 2 * (et - 2) + 1)
                g.indirect_dma_start(
                    Qg[et % 2][:, :], None, cc_tab_o,
                    IndirectOffsetOnAxis(adjr_sb[:, et:et + 1], 0),
                ).then_inc(sg2[et % 2].h, 16)
                g.indirect_dma_start(
                    Kg[et % 2][:, :], None, cc_tab_o,
                    IndirectOffsetOnAxis(adjc_cl[:, et:et + 1], 0),
                    element_offset=256,
                ).then_inc(sg2[et % 2].h, 16)
                sg2[et % 2].v += 32
        blk.gpsimd(_)

        def _(v):
            for et in range(NET):
                v.wait_ge(sg2[et % 2].h, 32 * (et // 2 + 1))
                if et >= 1:
                    v.wait_ge(sa.h, BGA + et)
                v.tensor_tensor(prod4[:, :], Qg[et % 2][:, :], Kg[et % 2][:, :],
                                ALU.mult).then_inc(sv.h, 1)
                v.wait_ge(sa.h, BGA + et + 1)
                if et >= 2:
                    v.wait_ge(sm.h, BGM + et - 1)
                v.tensor_scalar(Ub[et % 2][:, :], iota_sb[:, :], hi_f[:, et:et + 1],
                                ctb[:, et:et + 1], ALU.is_equal, ALU.mult)
                v.tensor_single_scalar(Vb[et % 2][:, :], iota_sb[:, :],
                                       lo_f[:, et:et + 1],
                                       ALU.is_equal).then_inc(sv.h, 1)
                sv.v += 2
        blk.vector(_)

        def _(s):
            for et in range(NET):
                s.wait_ge(sv.h, BGV + 2 * et + 1)
                s.activation(junk4[:, :], prod4[:, :], AF.Copy,
                             scale=av16[:, et:et + 1],
                             accum_out=ctb[:, et:et + 1]).then_inc(sa.h, 1)
                sa.v += 1
        blk.scalar(_)

        def _(t):
            for et in range(NET):
                t.wait_ge(sv.h, BGV + 2 * et + 2)
                t.matmul(pb[6][0:64, 0:64], Ub[et % 2][:, :], Vb[et % 2][:, :],
                         start=(et == 0), stop=(et == NET - 1)).then_inc(sm.h, 1)
                sm.v += 1
        blk.tensor(_)
        A2DM = sm.v

        def _(v):
            v.wait_ge(sm.h, A2DM)
            v.tensor_copy(a2dsb[:, :], pb[6][0:64, 0:64]).then_inc(sv.h, 1)
            sv.v += 1
        blk.vector(_)
        A2DV = sv.v

        def _(sync):
            sync.wait_ge(sv.h, A2DV)
            sync.dma_start(out=cc_a2d_i, in_=a2dsb[:, :]).then_inc(sci.h, 16)
            sci.v += 16
        blk.sync(_)
        A2D_CI = sci.v

        def _(g):
            g.wait_ge(sci.h, A2D_CI)
            g.collective_compute(
                "AllReduce", ALU.add, replica_groups=[list(range(NCORES))],
                ins=[cc_a2d_i.opt()], outs=[cc_a2d_o.opt()],
            ).then_inc(scc.h)
            scc.v += 1
        blk.gpsimd(_)
        A2D_AR = scc.v

        def _(sync):
            sync.wait_ge(scc.h, A2D_AR)
            sync.dma_start(out=a2dr[:, :], in_=cc_a2d_o).then_inc(sx2[0].h, 16)
            sx2[0].v += 16
        blk.sync(_)
        A2DR_LD = sx2[0].v

        def _(v):
            v.wait_ge(sx2[0].h, A2DR_LD)
            v.tensor_reduce(acols[:, 0:1], a2dr[:, :], AX.X, ALU.max).then_inc(sv.h, 1)
            sv.v += 1
        blk.vector(_)
        AM1 = sv.v

        def _(v):
            v.wait_ge(sv.h, AM1)
            v.tensor_copy(alpha2b[:, 0:1], acols[0:64, 0:1]).then_inc(sv.h, 1)
            sv.v += 1
        blk.vector(_)
        AM2 = sv.v

        def _(t):
            t.wait_ge(sv.h, AM2)
            t.transpose(pbt[0:1, 256:320], alpha2b[0:64, 0:1],
                        ident_sb[0:64, 0:64]).then_inc(sm.h, 1)
            sm.v += 1
        blk.tensor(_)
        AM3 = sm.v

        def _(v):
            v.wait_ge(sm.h, AM3)
            v.tensor_reduce(arow1[0:1, 0:1], pbt[0:1, 256:320], AX.X,
                            ALU.max).then_inc(sv.h, 1)
            sv.v += 1
            v.wait_ge(sv.h, sv.v)
            v.tensor_single_scalar(arow1[0:1, 1:2], arow1[0:1, 0:1], -1.0,
                                   ALU.mult).then_inc(sv.h, 1)
            sv.v += 1
            v.wait_ge(sv.h, sv.v)
            v.tensor_copy(alpha2b[0:1, 1:2], arow1[0:1, 1:2]).then_inc(sv.h, 1)
            sv.v += 1
        blk.vector(_)
        AM4 = sv.v

        def _(t):
            t.wait_ge(sv.h, AM4)
            t.matmul(pb[6][0:64, 64:65], ones_sb[0:1, 0:64], alpha2b[0:1, 1:2],
                     start=True, stop=True).then_inc(sm.h, 1)
            sm.v += 1
        blk.tensor(_)
        AM5 = sm.v

        def _(v):
            v.wait_ge(sm.h, AM5)
            v.tensor_copy(acols[:, 1:2], pb[6][0:64, 64:65]).then_inc(sv.h, 1)
            sv.v += 1
        blk.vector(_)
        AM6 = sv.v

        def _(s):
            s.wait_ge(sv.h, AM6)
            s.activation(alpha2[:, :], a2dr[:, :], AF.Exp, bias=acols[:, 1:2],
                         accum_out=acols[:, 2:3]).then_inc(sa.h, 1)
            sa.v += 1
        blk.scalar(_)
        AM7 = sa.v

        def _(v):
            v.wait_ge(sa.h, AM7)
            v.tensor_copy(alpha2b[:, 2:3], acols[0:64, 2:3]).then_inc(sv.h, 1)
            sv.v += 1
        blk.vector(_)
        AM8 = sv.v

        def _(t):
            t.wait_ge(sv.h, AM8)
            t.transpose(pbt[0:1, 320:384], alpha2b[0:64, 2:3],
                        ident_sb[0:64, 0:64]).then_inc(sm.h, 1)
            sm.v += 1
        blk.tensor(_)
        AM9 = sm.v

        def _(v):
            v.wait_ge(sm.h, AM9)
            v.tensor_reduce(arow1[0:1, 2:3], pbt[0:1, 320:384], AX.X,
                            ALU.add).then_inc(sv.h, 1)
            sv.v += 1
            v.wait_ge(sv.h, sv.v)
            v.reciprocal(arow1[0:1, 3:4], arow1[0:1, 2:3]).then_inc(sv.h, 1)
            sv.v += 1
            v.wait_ge(sv.h, sv.v)
            v.tensor_copy(alpha2b[0:1, 3:4], arow1[0:1, 3:4]).then_inc(sv.h, 1)
            sv.v += 1
        blk.vector(_)
        AM10 = sv.v

        def _(t):
            t.wait_ge(sv.h, AM10)
            t.matmul(pb[6][0:64, 66:67], ones_sb[0:1, 0:64], alpha2b[0:1, 3:4],
                     start=True, stop=True).then_inc(sm.h, 1)
            sm.v += 1
        blk.tensor(_)
        AM11 = sm.v

        def _(v):
            v.wait_ge(sm.h, AM11)
            v.tensor_copy(acols[:, 3:4], pb[6][0:64, 66:67]).then_inc(sv.h, 1)
            sv.v += 1
            v.wait_ge(sv.h, sv.v)
            v.tensor_single_scalar(alpha2b[:, 4:68], alpha2[:, :],
                                   acols[:, 3:4], ALU.mult).then_inc(sv.h, 1)
            sv.v += 1
        blk.vector(_)
        AM12 = sv.v

        def _(t):
            t.wait_ge(sv.h, AM12)
            t.matmul(pb[6][0:32, 128:192], sela_sb[:, :], alpha2b[:, 4:68],
                     start=True, stop=True).then_inc(sm.h, 1)
            sm.v += 1
        blk.tensor(_)
        AM13 = sm.v

        def _(v):
            v.wait_ge(sm.h, AM13)
            v.tensor_copy(selout[:, :], pb[6][0:32, 128:192]).then_inc(sv.h, 1)
            sv.v += 1
        blk.vector(_)
        AM14 = sv.v

        def _(sync):
            sync.wait_ge(sv.h, AM14)
            sync.dma_start(out=d_alph, in_=selout[:, :]).then_inc(sst.h, 16)
            sst.v += 16
            sync.wait_ge(sst.h, sst.v)
            sync.dma_start(out=alpharow[0:1, :],
                           in_=d_alph.rearrange("a b -> (a b)").unsqueeze(0)).then_inc(sx2[1].h, 16)
            sx2[1].v += 16
        blk.sync(_)
        ALPHA_LD = sx2[1].v

    ln1T = at("ln1T", [128, NDC, TH], BF16, offset=32768)
    x0s5 = [at("x0s5", [128, NDC, 512], F32, offset=98304)] * 2
    xl5 = at("xl5", [128, NDC, 512], F32, offset=131072)
    sg5 = at("sg5", [128, NDC, 512], F32, offset=147456)
    sw5 = scr("sw5", [128, NDC, 512], F32, 0)
    x1b5 = at("x1b5", [128, NDC, 512], F32, offset=114688)
    x1bb = at("x1bb", [128, NDC, 512], BF16, offset=147456)
    sq15 = at("sq15", [128, NDC, 512], BF16, offset=155648)
    alpbr = scr("alpbr", [1, 2 * 512], BF16, 41216)

    if PHASES >= 5:
        sxw = S(nc, "sxw")
        sx5 = [S(nc, "sx50"), S(nc, "sx51")]
        GBASE_V, GBASE_A, GBASE_M = sv.v, sa.v, sm.v
        for jt in range(4):
            TV, TA, TM = sv.v, sa.v, sm.v

            def _(sync, jt=jt, TV=TV):
                if jt >= 1:
                    sync.wait_ge(sv.h, TV)
                sync.dma_start(
                    out=x0s5[0][:, :, :],
                    in_=x0T.rearrange("(c p) t -> p c t", p=128)[:, :, 512 * jt:512 * (jt + 1)],
                ).then_inc(sx5[0].h, 16)
                sx5[0].v += 16
            blk.sync(_)

            def _(t, jt=jt, TV=TV, TA=TA):
                if jt == 0:
                    t.wait_ge(sx2[1].h, sx2[1].v)
                if jt >= 1:
                    t.wait_ge(sv.h, TV)
                t.matmul(pb[5][:, 0:512], ones_sb[0:1, :],
                         alpharow[0:1, 512 * jt:512 * (jt + 1)],
                         start=True, stop=True).then_inc(sm.h, 1)
                sm.v += 1
            blk.tensor(_)
            ALB = sm.v

            def _(v, jt=jt, ALB=ALB, TA=TA):
                v.wait_ge(sm.h, ALB)
                for ch in range(NDC):
                    ins = v.tensor_tensor(xl5[:, ch, :],
                                          valT[:, ch, 512 * jt:512 * (jt + 1)],
                                          pb[5][:, 0:512], ALU.mult)
                ins.then_inc(sv.h, 1)
                sv.v += 1
            blk.vector(_)
            G1 = sv.v

            def _(s, jt=jt, G1=G1):
                s.wait_ge(sv.h, G1)
                for ch in range(NDC):
                    ins = s.activation(sg5[:, ch, :], xl5[:, ch, :], AF.Sigmoid,
                                       scale=-1.0)
                ins.then_inc(sa.h, 1)
                sa.v += 1
                s.wait_ge(sa.h, sa.v)
                for ch in range(NDC):
                    ins = s.activation(sw5[:, ch, :], sg5[:, ch, :], AF.Square)
                ins.then_inc(sa.h, 1)
                sa.v += 1
            blk.scalar(_)
            G3 = sa.v

            def _(v, jt=jt, G3=G3):
                v.wait_ge(sa.h, G3)
                for ch in range(NDC):
                    ins = v.tensor_tensor(sg5[:, ch, :], xl5[:, ch, :],
                                          encT[:, ch, 512 * jt:512 * (jt + 1)],
                                          ALU.subtract)
                ins.then_inc(sv.h, 1)
                sv.v += 1
                v.wait_ge(sv.h, sv.v)
                for ch in range(NDC):
                    ins = v.tensor_tensor(xl5[:, ch, :], sw5[:, ch, :],
                                          sg5[:, ch, :], ALU.mult)
                ins.then_inc(sv.h, 1)
                sv.v += 1
                v.wait_ge(sv.h, sv.v)
                v.wait_ge(sx5[0].h, 16 * (jt + 1))
                for ch in range(NDC):
                    ins = v.scalar_tensor_tensor(sg5[:, ch, :], xl5[:, ch, :], 2.0,
                                                 x0s5[0][:, ch, :],
                                                 ALU.mult, ALU.add)
                ins.then_inc(sv.h, 1)
                sv.v += 1
                v.wait_ge(sv.h, sv.v)
                v.wait_ge(sxw.h, 16 * jt)
                for ch in range(NDC):
                    ins = v.scalar_tensor_tensor(x1b5[:, ch, :],
                                                 encT[:, ch, 512 * jt:512 * (jt + 1)],
                                                 2.0, sg5[:, ch, :],
                                                 ALU.mult, ALU.add)
                ins.then_inc(sv.h, 1)
                sv.v += 1
            blk.vector(_)
            G7 = sv.v

            def _(sync, jt=jt, G7=G7):
                sync.wait_ge(sv.h, G7)
                sync.dma_start(out=d_x1.rearrange("(c p) t -> p c t", p=128)
                               [:, :, 512 * jt:512 * (jt + 1)],
                               in_=x1b5[:, :, :]).then_inc(sxw.h, 16)
                sxw.v += 16
            blk.sync(_)

            def _(s, jt=jt, G7=G7):
                s.wait_ge(sv.h, G7)
                s.activation(x1bb[:, :, :], x1b5[:, :, :], AF.Copy)
                s.activation(sq15[:, :, :], x1b5[:, :, :], AF.Square).then_inc(sa.h, 1)
                sa.v += 1
            blk.scalar(_)
            L1A = sa.v

            def _(t, jt=jt, L1A=L1A):
                t.wait_ge(sa.h, L1A)
                onecol = ones_sb[:, 0:1]
                for c in range(NDC):
                    t.matmul(pb[0][0:1, 0:512], onecol, x1bb[:, c, :],
                             start=(c == 0), stop=(c == NDC - 1))
                for c in range(NDC):
                    mm = t.matmul(pb[1][0:1, 0:512], onecol, sq15[:, c, :],
                                  start=(c == 0), stop=(c == NDC - 1))
                mm.then_inc(sm.h, 1)
                sm.v += 1
            blk.tensor(_)
            L1M = sm.v

            def _(v, jt=jt, L1M=L1M):
                r = statrows
                inv = 1.0 / D
                v.wait_ge(sm.h, L1M)
                v.tensor_single_scalar(r[0:1, 0:512], pb[0][0:1, 0:512], -inv, ALU.mult)
                v.tensor_single_scalar(r[0:1, 512:1024], pb[1][0:1, 0:512], inv,
                                       ALU.mult).then_inc(sv.h, 1)
                sv.v += 1
                v.wait_ge(sv.h, sv.v)
                v.tensor_tensor(r[0:1, 1024:1536], r[0:1, 0:512], r[0:1, 0:512],
                                ALU.mult).then_inc(sv.h, 1)
                sv.v += 1
                v.wait_ge(sv.h, sv.v)
                v.tensor_tensor(r[0:1, 1024:1536], r[0:1, 512:1024], r[0:1, 1024:1536],
                                ALU.subtract).then_inc(sv.h, 1)
                sv.v += 1
                v.wait_ge(sv.h, sv.v)
                v.tensor_single_scalar(r[0:1, 1024:1536], r[0:1, 1024:1536], 1e-5,
                                       ALU.add).then_inc(sv.h, 1)
                sv.v += 1
            blk.vector(_)
            L1V = sv.v

            def _(s, jt=jt, L1V=L1V):
                s.wait_ge(sv.h, L1V)
                s.activation(statrows[0:1, 1536:2 * 1024], statrows[0:1, 1024:1536],
                             AF.Sqrt).then_inc(sa.h, 1)
                sa.v += 1
            blk.scalar(_)
            L1S = sa.v

            def _(v, jt=jt, L1S=L1S):
                r = statrows
                v.wait_ge(sa.h, L1S)
                v.reciprocal(r[0:1, 1024:1536], r[0:1, 1536:2048]).then_inc(sv.h, 1)
                sv.v += 1
                v.wait_ge(sv.h, sv.v)
                v.tensor_copy(alpbr[0:1, 0:512], r[0:1, 0:512])
                v.tensor_copy(alpbr[0:1, 512:1024], r[0:1, 1024:1536]).then_inc(sv.h, 1)
                sv.v += 1
            blk.vector(_)
            L1R = sv.v

            def _(t, jt=jt, L1R=L1R):
                t.wait_ge(sv.h, L1R)
                t.matmul(pb[2][:, 0:512], ones_sb[0:1, :], alpbr[0:1, 0:512],
                         start=True, stop=True)
                t.matmul(pb[3][:, 0:512], ones_sb[0:1, :], alpbr[0:1, 512:1024],
                         start=True, stop=True).then_inc(sm.h, 1)
                sm.v += 1
            blk.tensor(_)
            L1B = sm.v

            def _(v, jt=jt, L1B=L1B):
                v.wait_ge(sm.h, L1B)
                for c in range(NDC):
                    ins = v.tensor_tensor(ln1T[:, c, 512 * jt:512 * (jt + 1)],
                                          x1b5[:, c, :], pb[2][:, 0:512], ALU.add)
                ins.then_inc(sv.h, 1)
                sv.v += 1
                v.wait_ge(sv.h, sv.v)
                for c in range(NDC):
                    ins = v.tensor_tensor(ln1T[:, c, 512 * jt:512 * (jt + 1)],
                                          ln1T[:, c, 512 * jt:512 * (jt + 1)],
                                          pb[3][:, 0:512], ALU.mult)
                ins.then_inc(sv.h, 1)
                sv.v += 1
            blk.vector(_)

    wst6 = [at(f"wst6{i}", [128, NDC // 2, D], F32, offset=98304 + 16384 * i)
            for i in range(2)]
    gs6 = at("gs6", [128, 2], F32, offset=131072)
    onef_c = at("onef_c", [128, 1], F32, offset=131136)
    onef_r = at("onef_r", [1, 128], F32, offset=131200)
    grow = at("grow", [1, 8], F32, offset=131776)
    scol6 = at("scol6", [128, 2], F32, offset=132352)
    qge = scr("qge", [128, 4, D], BF16, 0)
    qle = scr("qle", [128, 4, D], BF16, 8192)
    wqq = scr("wqq", [128, 4, D], BF16, 16384)
    wqst = at("wqst", [128, NDC, 512], BF16, offset=132928)

    wst6b = at("wst6b", [128, NDC, 512], F32, offset=114688)
    if PHASES >= 6:
        P6V, P6A, P6M = sv.v, sa.v, sm.v

        def _(sync):
            sync.wait_ge(sv.h, P6V)
            sync.wait_ge(sxw.h, sxw.v)
            sync.dma_start(out=wst6[0][:, :, :],
                           in_=w1s.rearrange("(c p) n -> p c n", p=128)).then_inc(sx2[0].h, 16)
            sx2[0].v += 16
            sync.dma_start(out=wst6b[:, :, :],
                           in_=w2s.rearrange("(c p) n -> p c n", p=128)).then_inc(sx2[0].h, 16)
            sx2[0].v += 16
        blk.sync(_)
        W6L = sx2[0].v

        def _(v):
            v.wait_ge(sx2[0].h, W6L)
            v.memset(onef_c[:, 0:1], 1.0)
            v.memset(onef_r[0:1, :], 1.0)
            v.tensor_reduce(gs6[:, 0:1], wst6[0][:, :, :], AX.XY, ALU.add,
                            apply_absolute_value=True)
            v.tensor_reduce(gs6[:, 1:2], wst6b[:, :, :], AX.XY, ALU.add,
                            apply_absolute_value=True).then_inc(sv.h, 1)
            sv.v += 1
        blk.vector(_)
        W6R = sv.v

        def _(t):
            t.wait_ge(sv.h, W6R)
            t.matmul(pb[5][0:1, 0:2], onef_c[:, 0:1], gs6[:, 0:2],
                     start=True, stop=True).then_inc(sm.h, 1)
            sm.v += 1
        blk.tensor(_)
        W6M = sm.v

        def _(v):
            v.wait_ge(sm.h, W6M)
            v.tensor_copy(grow[0:1, 0:2], pb[5][0:1, 0:2]).then_inc(sv.h, 1)
            sv.v += 1
        blk.vector(_)
        W6C = sv.v

        def _(sync):
            sync.wait_ge(sv.h, W6C)
            sync.dma_start(out=cc_gam_i, in_=grow[0:1, 0:2]).then_inc(sci.h, 16)
            sci.v += 16
        blk.sync(_)
        GAM_CI = sci.v

        def _(g):
            g.wait_ge(sci.h, GAM_CI)
            g.collective_compute(
                "AllReduce", ALU.add, replica_groups=[list(range(NCORES))],
                ins=[cc_gam_i.opt()], outs=[cc_gam_o.opt()],
            ).then_inc(scc.h)
            scc.v += 1
        blk.gpsimd(_)
        GAM_AR = scc.v

        def _(sync):
            sync.wait_ge(scc.h, GAM_AR)
            sync.dma_start(out=grow[0:1, 2:4], in_=cc_gam_o).then_inc(sx2[1].h, 16)
            sx2[1].v += 16
        blk.sync(_)
        GAM_LD = sx2[1].v

        def _(v):
            v.wait_ge(sx2[1].h, GAM_LD)
            v.tensor_single_scalar(grow[0:1, 4:6], grow[0:1, 2:4],
                                   1.0 / (512 * NCORES * D), ALU.mult).then_inc(sv.h, 1)
            sv.v += 1
            v.wait_ge(sv.h, sv.v)
            v.tensor_single_scalar(grow[0:1, 4:6], grow[0:1, 4:6], 1e-5,
                                   ALU.add).then_inc(sv.h, 1)
            sv.v += 1
            v.wait_ge(sv.h, sv.v)
            v.reciprocal(grow[0:1, 6:8], grow[0:1, 4:6]).then_inc(sv.h, 1)
            sv.v += 1
        blk.vector(_)
        W6G = sv.v

        def _(t):
            t.wait_ge(sv.h, W6G)
            t.matmul(pb[5][:, 4:6], onef_r[0:1, :], grow[0:1, 6:8],
                     start=True, stop=True).then_inc(sm.h, 1)
            sm.v += 1
        blk.tensor(_)
        W6B = sm.v

        def _(v):
            v.wait_ge(sm.h, W6B)
            v.tensor_copy(scol6[:, 0:2], pb[5][:, 4:6]).then_inc(sv.h, 1)
            sv.v += 1
        blk.vector(_)
        W6S = sv.v

        wqq2 = scr("wqq2", [128, NDC, 512], BF16, 16384)
        wqst2 = at("wqst2", [128, 4, D], BF16, offset=132928)

        for wi in range(2):
            QB = sv.v
            src = wst6[0] if wi == 0 else wst6b
            dstq = wqq if wi == 0 else wqq2

            def _(v, wi=wi, src=src, dstq=dstq, QB=QB, PM=sm.v):
                if wi == 0:
                    v.wait_ge(sv.h, W6S)
                else:
                    v.wait_ge(sv.h, QB)
                    v.wait_ge(sm.h, PM)
                v.tensor_scalar(qge[:, :, :].rearrange("p a b -> p (a b)"),
                                src[:, :, :].rearrange("p a b -> p (a b)"),
                                scol6[:, wi:wi + 1], 0.5, ALU.mult, ALU.is_gt)
                v.tensor_scalar(qle[:, :, :].rearrange("p a b -> p (a b)"),
                                src[:, :, :].rearrange("p a b -> p (a b)"),
                                scol6[:, wi:wi + 1], -0.5, ALU.mult,
                                ALU.is_lt).then_inc(sv.h, 1)
                sv.v += 1
                v.wait_ge(sv.h, sv.v)
                v.tensor_tensor(dstq[:, :, :].rearrange("p a b -> p (a b)"),
                                qge[:, :, :].rearrange("p a b -> p (a b)"),
                                qle[:, :, :].rearrange("p a b -> p (a b)"),
                                ALU.subtract).then_inc(sv.h, 1)
                sv.v += 1
            blk.vector(_)
            QV = sv.v
            TBASE_M, TBASE_A = sm.v, sa.v
            blocks = []
            if wi == 0:
                for c in range(4):
                    for d8 in range(NDC):
                        blocks.append((wqq[:, c, 128 * d8:128 * (d8 + 1)],
                                       wqst[:, d8, 128 * c:128 * (c + 1)]))
            else:
                for c in range(NDC):
                    for f8 in range(4):
                        blocks.append((wqq2[:, c, 128 * f8:128 * (f8 + 1)],
                                       wqst2[:, f8, 128 * c:128 * (c + 1)]))

            def _(t, QV=QV, TBASE_A=TBASE_A, blocks=blocks):
                t.wait_ge(sv.h, QV)
                for bi, (inap, outap) in enumerate(blocks):
                    t.wait_ge(sa.h, TBASE_A + bi)
                    t.transpose(pbt[:, 128 * (bi % 2):128 * (bi % 2) + 128],
                                inap, ident_sb[:, :]).then_inc(sm.h, 1)
                    sm.v += 1
            blk.tensor(_)

            def _(s, TBASE_M=TBASE_M, blocks=blocks, wi=wi, CIV=sci.v):
                for bi, (inap, outap) in enumerate(blocks):
                    if bi == 0 and wi == 1:
                        s.wait_ge(sci.h, CIV)
                    s.wait_ge(sm.h, TBASE_M + bi + 1)
                    s.activation(outap, pbt[:, 128 * (bi % 2):128 * (bi % 2) + 128],
                                 AF.Copy).then_inc(sa.h, 1)
                    sa.v += 1
            blk.scalar(_)

            WQA = sa.v

            def _(sync, wi=wi, WQA=WQA):
                sync.wait_ge(sa.h, WQA)
                if wi == 0:
                    sync.dma_start(
                        out=cc_wq_i[0:1024, :].rearrange("(c p) n -> p c n", p=128),
                        in_=wqst[:, :, :]).then_inc(sci.h, 16)
                else:
                    sync.dma_start(
                        out=cc_wq_i.rearrange("a b -> (a b)")[524288:1048576]
                        .rearrange("(c p n) -> p c n", p=128, n=1024),
                        in_=wqst2[:, :, :]).then_inc(sci.h, 16)
                sci.v += 16
            blk.sync(_)

        WQ_CI = sci.v

        def _(g):
            g.wait_ge(sci.h, WQ_CI)
            g.collective_compute(
                "AllGather", ALU.bypass, replica_groups=[list(range(NCORES))],
                ins=[cc_wq_i.opt()], outs=[cc_wq_o.opt()],
            ).then_inc(scc.h)
            scc.v += 1
        blk.gpsimd(_)
        WQ_AG = scc.v

        wq1r = at("wq1r", [128, 32, NDC, 128], BF16, offset=65536)
        wq2rA = at("wq2rA", [128, 16, NDC, 128], BF16, offset=131072)
        wq2rB = scr("wq2rB", [128, 16, NDC, 128], BF16, 0)
        ccwq_flat = cc_wq_o.rearrange("a b -> (a b)")

        def _(sync):
            sync.wait_ge(scc.h, WQ_AG)
            for r in range(NCORES):
                w1v = ccwq_flat[r * 2048 * 512: r * 2048 * 512 + 524288] \
                    .rearrange("(dch p fl col) -> p fl dch col",
                               p=128, fl=4, col=128)
                sync.dma_start(out=wq1r[:, 4 * r:4 * r + 4, :, :],
                               in_=w1v).then_inc(sg.h, 16)
                w2v = ccwq_flat[r * 2048 * 512 + 524288: (r + 1) * 2048 * 512] \
                    .rearrange("(fl p d2ch col) -> p fl d2ch col",
                               p=128, fl=4, col=128)
                w2dst = wq2rA if r < 4 else wq2rB
                sync.dma_start(out=w2dst[:, (4 * r) % 16:(4 * r) % 16 + 4, :, :],
                               in_=w2v).then_inc(sg.h, 16)
                sg.v += 32
        blk.sync(_)
        WQR_LD = sg.v

        x1l6 = at("x1l6", [128, NDC, 256], F32, offset=0)
        yt6 = at("yt6", [128, NDC, 256], F32, offset=8192)
        hT6 = at("hT6", [128, 32, 256], BF16, offset=16384)
        FBASE_V = sv.v
        sx6 = S(nc, "sx6")

        def _(sync):
            sync.dma_start(
                out=x1l6[:, :, :],
                in_=d_x1.rearrange("(c p) t -> p c t", p=128)[:, :, 0:256],
            ).then_inc(sx6.h, 16)
            sx6.v += 16
        blk.sync(_)

        for j8 in range(8):
            TBM, TBA, TBV = sm.v, sa.v, sv.v

            def _(t, j8=j8, TBM=TBM, TBA=TBA, TBV=TBV):
                if j8 == 0:
                    t.wait_ge(sg.h, WQR_LD)
                    t.wait_ge(sv.h, P6V)
                for fc in range(32):
                    if fc >= 2:
                        t.wait_ge(sa.h, TBA + fc - 1)
                    elif fc == 0:
                        t.wait_ge(sa.h, TBA)
                    for dch in range(NDC):
                        mm = t.matmul(pb[4 + fc % 2][:, 0:256],
                                      wq1r[:, fc, dch, :],
                                      ln1T[:, dch, 256 * j8:256 * (j8 + 1)],
                                      start=(dch == 0), stop=(dch == NDC - 1))
                    mm.then_inc(sm.h, 1)
                    sm.v += 1
                t.wait_ge(sa.h, TBA + 32)
                for d2 in range(NDC):
                    if d2 >= 4:
                        t.wait_ge(sv.h, TBV + (d2 - 4) + 1)
                    elif d2 == 0:
                        t.wait_ge(sv.h, TBV)
                    for fc in range(32):
                        w2t = (wq2rA[:, fc, d2, :] if fc < 16
                               else wq2rB[:, fc - 16, d2, :])
                        mm = t.matmul(pb[d2 % 4][:, 0:256], w2t, hT6[:, fc, :],
                                      start=(fc == 0), stop=(fc == 31))
                    mm.then_inc(sm.h, 1)
                    sm.v += 1
            blk.tensor(_)

            def _(s, j8=j8, TBM=TBM):
                for fc in range(32):
                    s.wait_ge(sm.h, TBM + fc + 1)
                    s.activation(hT6[:, fc, :], pb[4 + fc % 2][:, 0:256],
                                 AF.Relu, bias=b1_sb[:, fc:fc + 1]).then_inc(sa.h, 1)
                    sa.v += 1
            blk.scalar(_)

            def _(v, j8=j8, TBM=TBM, STV=sst.v):
                v.wait_ge(sx6.h, 16 * (j8 + 1))
                v.wait_ge(sst.h, STV)
                for d2 in range(NDC):
                    v.wait_ge(sm.h, TBM + 32 + d2 + 1)
                    v.scalar_tensor_tensor(
                        yt6[:, d2, :], pb[d2 % 4][:, 0:256], b2_sb[:, d2:d2 + 1],
                        x1l6[:, d2, :], ALU.add, ALU.add).then_inc(sv.h, 1)
                    sv.v += 1
            blk.vector(_)
            YV = sv.v

            def _(sync, j8=j8, YV=YV):
                sync.wait_ge(sv.h, YV)
                sync.dma_start(
                    out=y.rearrange("(c p) t -> p c t", p=128)[:, :, 256 * j8:256 * (j8 + 1)],
                    in_=yt6[:, :, :]).then_inc(sst.h, 16)
                sst.v += 16
                if j8 + 1 < 8:
                    sync.dma_start(
                        out=x1l6[:, :, :],
                        in_=d_x1.rearrange("(c p) t -> p c t", p=128)
                        [:, :, 256 * (j8 + 1):256 * (j8 + 2)],
                    ).then_inc(sx6.h, 16)
                    sx6.v += 16
            blk.sync(_)

    def _(sync):
        for _sem in S.registry:
            if _sem.v > 0:
                sync.wait_ge(_sem.h, _sem.v)
        if DEBUG:
            if PHASES < 3:
                sync.dma_start(out=dbg["dbg_ln0"][:, 0:NDC, 0:TH],
                               in_=ln0h[:, :, :]).then_inc(sst.h, 16)
                sync.dma_start(out=dbg["dbg_q"], in_=qTs[:, :, :]).then_inc(sst.h, 16)
                sync.dma_start(out=dbg["dbg_k"], in_=kTs[:, :, :]).then_inc(sst.h, 16)
                sync.dma_start(out=dbg["dbg_v"],
                               in_=vtok.rearrange("p a b -> p (a b)")).then_inc(sst.h, 16)
                sst.v += 64
            if PHASES == 2:
                sync.dma_start(out=dbg["dbg_ql"],
                               in_=ql_bf.rearrange("p a b -> p (a b)")).then_inc(sst.h, 16)
                sync.dma_start(out=dbg["dbg_kl"],
                               in_=kl_bf.rearrange("p a b -> p (a b)")).then_inc(sst.h, 16)
                sync.dma_start(out=dbg["dbg_a2"],
                               in_=A2bf4.rearrange("p a b -> p (a b)")).then_inc(sst.h, 16)
                sync.dma_start(out=dbg["dbg_f"],
                               in_=F4bf.rearrange("p a b -> p (a b)")).then_inc(sst.h, 16)
                sync.dma_start(out=dbg["dbg_g"],
                               in_=G4bf.rearrange("p a b -> p (a b)")).then_inc(sst.h, 16)
                sst.v += 80
            if PHASES >= 2:
                sync.dma_start(out=dbg["dbg_scale"][0:1, 0:6], in_=scrow[0:1, 0:6]).then_inc(sst.h, 16)
                sst.v += 16
            if PHASES in (2, 3, 4, 5):
                sync.dma_start(out=dbg["dbg_out"], in_=outTo[:, :, :]).then_inc(sst.h, 16)
                sst.v += 16
            if PHASES in (4, 5):
                sync.dma_start(out=dbg["dbg_enc"], in_=encT[:, :, :]).then_inc(sst.h, 16)
                sync.dma_start(out=dbg["dbg_val"], in_=valT[:, :, :]).then_inc(sst.h, 16)
                sync.dma_start(out=dbg["dbg_a2d"], in_=a2dr[:, :]).then_inc(sst.h, 16)
                sst.v += 48
            if PHASES >= 4:
                sync.dma_start(out=dbg["dbg_alpha"], in_=alpharow[0:1, :]).then_inc(sst.h, 16)
                sst.v += 16
            if PHASES >= 6 and "dbg_ln1" in dbg:
                sync.dma_start(out=dbg["dbg_ln1"], in_=ln1T[:, :, :]).then_inc(sst.h, 16)
                sync.dma_start(out=dbg["dbg_wq1"],
                               in_=wq1r.rearrange("p a b c -> p (a b c)")).then_inc(sst.h, 16)
                sync.dma_start(out=dbg["dbg_wq2a"],
                               in_=wq2rA.rearrange("p a b c -> p (a b c)")).then_inc(sst.h, 16)
                sync.dma_start(out=dbg["dbg_wq2b"],
                               in_=wq2rB.rearrange("p a b c -> p (a b c)")).then_inc(sst.h, 16)
                sync.dma_start(out=dbg["dbg_h"],
                               in_=hT6.rearrange("p a b -> p (a b)")).then_inc(sst.h, 16)
                sync.dma_start(out=dbg["dbg_ccwqi"], in_=cc_wq_i).then_inc(sst.h, 16)
                sync.dma_start(out=dbg["dbg_ccwqo"], in_=cc_wq_o).then_inc(sst.h, 16)
                sst.v += 112
        sync.wait_ge(sst.h, sst.v)
    blk.sync(_)

    blk.__exit__(None, None, None)
    return nc



def _band_matrices(conv_w, parity):
    heads = range(4 * parity, 4 * parity + 4)
    out = np.zeros((128, 4 * 7 * 128), np.float32)
    s_idx = np.arange(128)[:, None]
    p_idx = np.arange(128)[None, :]
    for jloc, h in enumerate(heads):
        cw = np.zeros(300, np.float32)
        cw[:33] = conv_w[h, 0, :, 0]
        jA = s_idx - 128 - p_idx + 16
        jB = s_idx - p_idx + 16
        jC = s_idx + 128 - p_idx + 16
        A = np.where((jA >= 0) & (jA < 33), cw[np.clip(jA, 0, 299)], np.float32(0))
        B = np.where((jB >= 0) & (jB < 33), cw[np.clip(jB, 0, 299)], np.float32(0))
        C = np.where((jC >= 0) & (jC < 33), cw[np.clip(jC, 0, 299)], np.float32(0))
        base = jloc * 7 * 128
        out[:, base: base + 128] = A
        out[:, base + 128: base + 256] = B
        out[:, base + 256: base + 384] = C
        if parity == 1:
            out[:, base + 384: base + 512] = A
            out[:, base + 512: base + 640] = C
        else:
            out[:, base + 640: base + 768] = A
            out[:, base + 768: base + 896] = C
    return out


def _outw_other(out_w, par):
    w = np.array(out_w, copy=True)
    w[256 * par: 256 * par + 256, :] = 0
    return w


def make_in_maps(inputs):
    x0 = inputs["L0_attn_dense"]
    qkv_w = inputs["L0_attn_qkv_w"]
    conv_w = inputs["L0_attn_conv_w"]
    in_maps = []
    iota64 = np.broadcast_to(np.arange(64, dtype=np.float32), (128, 64)).copy()
    ones_bf = np.ones((128, 128), ml_dtypes.bfloat16)
    ident_bf = np.eye(128).astype(ml_dtypes.bfloat16)
    ikc = np.zeros((128, 2 * 768), ml_dtypes.bfloat16)
    for k, val in enumerate([7.0, 15.0, 13.0]):
        for ch in range(2):
            for p in range(128):
                ikc[p, ch * 768 + k * 256 + ch * 128 + p] = val

    for c in range(NCORES):
        b, par = c // 2, c % 2
        perm = (np.r_[2048:4096, 0:2048] if par else np.r_[0:4096])
        x0T = np.ascontiguousarray(x0[b].T[:, perm])
        hsl = slice(256 * par, 256 * par + 256)
        qkvw_c = np.ascontiguousarray(
            np.concatenate([qkv_w[:, 0:512][:, hsl],
                            qkv_w[:, 512:1024][:, hsl],
                            qkv_w[:, 1024:1536][:, hsl]], axis=1))
        sel = np.zeros((64, 32), ml_dtypes.bfloat16)
        for i in range(32):
            sel[32 * par + i, i] = 1
        m = {
            "x0T": x0T,
            "qkvw": qkvw_c,
            "wv": inputs["L0_attn_wv_w"],
            "wvb": np.ascontiguousarray(inputs["L0_attn_wv_b"].reshape(NDC, 128).T),
            "wqk": np.ascontiguousarray(
                np.concatenate([inputs["L0_attn_wq"], inputs["L0_attn_wk"]], axis=1)),
            "outw_o": _outw_other(inputs["L0_attn_out_w"], par),
            "outw_m": np.ascontiguousarray(
                inputs["L0_attn_out_w"][256 * par: 256 * par + 256]),
            "outb": np.ascontiguousarray(inputs["L0_attn_out_b"].reshape(NDC, 128).T),
            "w1s": np.ascontiguousarray(inputs["L0_ffn_W1"][512 * c: 512 * c + 512]),
            "w2s": np.ascontiguousarray(inputs["L0_ffn_W2"][:, 512 * c: 512 * c + 512]),
            "b1c": np.ascontiguousarray(inputs["L0_ffn_b1"].reshape(32, 128).T),
            "b2c": np.ascontiguousarray(inputs["L0_ffn_b2"].reshape(NDC, 128).T),
            "adjr": np.ascontiguousarray(
                inputs["L0_attn_adj_rows"][EPC * c: EPC * (c + 1)].reshape(NET, 128).T),
            "adjc": np.ascontiguousarray(
                inputs["L0_attn_adj_cols"][EPC * c: EPC * (c + 1)].reshape(NET, 128).T),
            "adjv": np.ascontiguousarray(
                inputs["L0_attn_adj_vals"][EPC * c: EPC * (c + 1)].reshape(NET, 128).T),
            "bands": _band_matrices(conv_w, par),
            "iota64": iota64,
            "ones_bf": ones_bf,
            "ident_bf": ident_bf,
            "ikc": ikc,
            "selalpha": sel,
        }
        in_maps.append(m)
    return in_maps


def kernel(**inputs):
    inputs = {k: np.asarray(v) for k, v in inputs.items()}
    in_maps = make_in_maps(inputs)
    nc = build_program()
    res = run_bass_kernel_spmd(nc, in_maps, core_ids=list(range(NCORES)))
    out = np.empty((4, T, D), np.float32)
    for c in range(NCORES):
        b, par = c // 2, c % 2
        out[b, 2048 * par: 2048 * par + 2048, :] = np.asarray(res.results[c]["y"]).T
    return out


if __name__ == "__main__":
    nc = build_program()
    print("build ok, instructions:", nc.n_instructions())


# version 03
# speedup vs baseline: 1.0069x; 1.0069x over previous
import numpy as np
import ml_dtypes
import concourse.bass as bass
import concourse.mybir as mybir
from concourse.bass_core_rust import IndirectOffsetOnAxis
from concourse.bass_utils import run_bass_kernel_spmd

F32 = mybir.dt.float32
BF16 = mybir.dt.bfloat16
I32 = mybir.dt.int32
AF = mybir.ActivationFunctionType
ALU = mybir.AluOpType
AX = mybir.AxisListType

NCORES = 8
D = 1024
NDC = 8
T = 4096
TH = 2048
HN = 4
DH = 64
NL = 256
EPC = 16384
NET = 128
SLAB = 256
NSLAB = T // SLAB

PHASES = 6
DEBUG = False


class S:
    registry = []

    def __init__(self, nc, name):
        self.h = nc.alloc_semaphore(name)
        self.v = 0
        S.registry.append(self)


def build_program():
    S.registry = []
    nc = bass.Bass("TRN2", target_bir_lowering=False, debug=False,
                   num_devices=NCORES, num_swdge_queues=4)

    def inp(name, shape, dt=F32):
        return nc.dram_tensor(name, shape, dt, kind="ExternalInput").ap()

    x0T = inp("x0T", [D, T])
    qkvw = inp("qkvw", [D, 768])
    wv = inp("wv", [D, D])
    wvb = inp("wvb", [128, NDC])
    wqk = inp("wqk", [D, 512])
    outw_o = inp("outw_o", [512, D])
    outw_m = inp("outw_m", [256, D])
    outb = inp("outb", [128, NDC])
    w1s = inp("w1s", [512, D])
    w2s = inp("w2s", [D, 512])
    b1c = inp("b1c", [128, 32])
    b2c = inp("b2c", [128, NDC])
    adjr = inp("adjr", [128, NET], I32)
    adjc = inp("adjc", [128, NET], I32)
    adjv = inp("adjv", [128, NET])
    bands = inp("bands", [128, HN * 7 * 128])
    iota64 = inp("iota64", [128, 64])
    ones_bf = inp("ones_bf", [128, 128], BF16)
    ident_bf = inp("ident_bf", [128, 128], BF16)
    ikc = inp("ikc", [128, 2 * 768], BF16)
    selalpha = inp("selalpha", [64, 32], BF16)

    y = nc.dram_tensor("y", [D, TH], F32, kind="ExternalOutput").ap()
    dbg = {}

    def dbgout(name, shape, dt):
        if DEBUG:
            dbg[name] = nc.dram_tensor(name, shape, dt, kind="ExternalOutput").ap()

    dbgout("dbg_ln0", [128, NDC, T], BF16)
    dbgout("dbg_q", [128, 2, T], BF16)
    dbgout("dbg_k", [128, 2, T], BF16)
    dbgout("dbg_v", [128, 32 * 256], BF16)
    dbgout("dbg_ql", [128, 512], BF16)
    dbgout("dbg_kl", [128, 512], BF16)
    dbgout("dbg_a2", [128, 2048], BF16)
    dbgout("dbg_scale", [1, 8], F32)
    dbgout("dbg_f", [128, 512], BF16)
    dbgout("dbg_g", [128, 512], BF16)
    dbgout("dbg_out", [128, 2, T], BF16)
    dbgout("dbg_enc", [128, NDC, TH], BF16)
    dbgout("dbg_val", [128, NDC, TH], BF16)
    dbgout("dbg_a2d", [64, 64], F32)
    dbgout("dbg_alpha", [1, TH], BF16)
    dbgout("dbg_ln1", [128, NDC, TH], BF16)
    dbgout("dbg_wq1", [128, 32 * NDC * 128], BF16)
    dbgout("dbg_wq2a", [128, 16 * NDC * 128], BF16)
    dbgout("dbg_wq2b", [128, 16 * NDC * 128], BF16)
    dbgout("dbg_h", [128, 32 * 256], BF16)
    dbgout("dbg_ccwqi", [2048, 512], BF16)
    dbgout("dbg_ccwqo", [NCORES * 2048, 512], BF16)

    def dram(name, shape, dt):
        return nc.dram_tensor(name, shape, dt, kind="Internal").ap()

    cc_scale_i = dram("cc_scale_i", [1, 2], F32)
    cc_scale_o = dram("cc_scale_o", [1, 2], F32)
    cc_gam_i = dram("cc_gam_i", [1, 2], F32)
    cc_gam_o = dram("cc_gam_o", [1, 2], F32)
    cc_out_i = dram("cc_out_i", [256, T], BF16)
    cc_out_o = dram("cc_out_o", [512, T], BF16)
    cc_xg_i = dram("cc_xg_i", [D, TH], BF16)
    cc_xg_o = dram("cc_xg_o", [4 * D, TH], BF16)
    cc_tab_i = dram("cc_tab_i", [TH, 512], BF16)
    cc_tab_o = dram("cc_tab_o", [NCORES * TH, 512], BF16)
    cc_wq_i = dram("cc_wq_i", [2048, 512], BF16)
    cc_wq_o = dram("cc_wq_o", [NCORES * 2048, 512], BF16)
    cc_a2d_i = dram("cc_a2d_i", [64, 64], F32)
    cc_a2d_o = dram("cc_a2d_o", [64, 64], F32)
    d_alph = dram("d_alph", [32, 64], BF16)
    d_x1 = nc.dram_tensor("d_x1", [D, TH], F32,
                          kind="ExternalOutput" if DEBUG else "Internal").ap()

    SB0 = 16512

    def at(name, shape, dtt, offset):
        return nc.alloc_sbuf_tensor_at(name, shape, dtt, offset=SB0 + offset)
    encT = at("encT", [128, NDC, TH], BF16, offset=0)
    ln0h = at("ln0h", [128, NDC, TH], BF16, offset=32768)
    valT = at("valT", [128, NDC, TH], BF16, offset=65536)
    qTs = at("qTs", [128, 2, T], BF16, offset=98304)
    kTs = at("kTs", [128, 2, T], BF16, offset=114688)
    vtok = at("vtok", [128, 32, 256], BF16, offset=131072)
    qkvw_sb = at("qkvw_sb", [128, NDC, 768], BF16, offset=147456)
    outTo = at("outTo", [128, 2, T], BF16, offset=147456)

    CW = 163840

    def atw(name, shape, dt):
        nonlocal CW
        t = at(name, shape, dt, offset=CW)
        per = int(np.prod(shape[1:])) * mybir.dt.size(dt)
        CW = (CW + per + 31) & ~31
        return t

    iota_sb = atw("iota_sb", [128, 64], F32)
    ones_sb = atw("ones_sb", [128, 128], BF16)
    ident_sb = atw("ident_sb", [128, 128], BF16)
    sela_sb = atw("sela_sb", [64, 32], BF16)
    wvb_sb = atw("wvb_sb", [128, NDC], F32)
    outb_sb = atw("outb_sb", [128, NDC], F32)
    b1_sb = atw("b1_sb", [128, 32], F32)
    b2_sb = atw("b2_sb", [128, NDC], F32)
    adjr_sb = atw("adjr_sb", [128, NET], I32)
    adjc_sb = atw("adjc_sb", [128, NET], I32)
    adjv_sb = atw("adjv_sb", [128, NET], F32)
    SCR = CW
    SCR_END = 212000
    assert SCR <= 168500, f"const region too big: {SCR}"

    def scr(name, shape, dt, off):
        per = int(np.prod(shape[1:])) * mybir.dt.size(dt)
        assert SCR + off + per <= SCR_END, (name, SCR + off + per - SCR_END)
        return at(name, shape, dt, offset=SCR + off)

    xslab = [scr(f"xslab{i}", [128, NDC, SLAB], F32, 8192 * i) for i in range(2)]
    xb_sb = scr("xb", [128, NDC, SLAB], BF16, 16384)
    sqb_sb = scr("sqb", [128, NDC, SLAB], BF16, 20480)
    lnoth = scr("lnoth", [128, NDC, SLAB], BF16, 24576)
    statrows = scr("statrows", [1, 2048], F32, 28672)
    browbf = scr("browbf", [1, 2 * SLAB], BF16, 35840)
    wstg = scr("wstg", [128, 2048], F32, 0)

    pb = [nc.alloc_psum_tensor(f"pb{i}", [128, 512], F32) for i in range(7)]
    pbt = nc.alloc_psum_tensor("pbt", [128, 1024], BF16)

    sld = S(nc, "sld")
    sx2 = [S(nc, "sx0"), S(nc, "sx1")]
    sci = S(nc, "sci")
    sv = S(nc, "sv")
    sa = S(nc, "sa")
    sp = S(nc, "sp")
    sm = S(nc, "sm")
    sst = S(nc, "sst")
    scc = S(nc, "scc")
    sg = S(nc, "sg")

    blk = nc.Block()
    blk.__enter__()

    def _(sync):
        for ap_in, sb in [(iota64, iota_sb), (ones_bf, ones_sb),
                          (ident_bf, ident_sb), (selalpha, sela_sb),
                          (adjr, adjr_sb), (adjc, adjc_sb), (adjv, adjv_sb),
                          (wvb, wvb_sb), (outb, outb_sb), (b1c, b1_sb),
                          (b2c, b2_sb)]:
            sync.dma_start(out=sb[:, :], in_=ap_in).then_inc(sld.h, 16)
            sld.v += 16
        for r in range(4):
            if r > 0:
                sync.wait_ge(sv.h, r)
            sync.dma_start(
                out=wstg[:, 0:1536].rearrange("p (c n) -> p c n", c=2),
                in_=qkvw.rearrange("(c p) n -> p c n", p=128)[:, 2 * r: 2 * r + 2, :],
            ).then_inc(sld.h, 16)
            sld.v += 16
    blk.sync(_)

    def _(v):
        base = 16 * 11
        for r in range(4):
            v.wait_ge(sld.h, base + 16 * (r + 1))
            wv2 = wstg[:, 0:1536].rearrange("p (c n) -> p c n", c=2)
            v.tensor_single_scalar(qkvw_sb[:, 2 * r: 2 * r + 2, 0:256],
                                   wv2[:, :, 0:256], 0.125, ALU.mult)
            v.tensor_copy(qkvw_sb[:, 2 * r: 2 * r + 2, 256:768],
                          wv2[:, :, 256:768]).then_inc(sv.h, 1)
            sv.v += 1
    blk.vector(_)
    QKVW_CAST = sv.v
    P0_LD = 16 * 12

    BA, BM, BV = sa.v, sm.v, sv.v

    def _(sync):
        for i in range(NSLAB):
            if i >= 2:
                sync.wait_ge(sv.h, BV + 8 * (i - 2) + 7)
            else:
                sync.wait_ge(sv.h, QKVW_CAST)
            sync.dma_start(
                out=xslab[i % 2][:, :, :],
                in_=x0T.rearrange("(c p) t -> p c t", p=128)[:, :, SLAB * i: SLAB * (i + 1)],
            ).then_inc(sx2[i % 2].h, 16)
            sx2[i % 2].v += 16
    blk.sync(_)

    def lnsrc(i, c):
        if i < 8:
            return ln0h[:, c, SLAB * i: SLAB * (i + 1)]
        return lnoth[:, c, :]

    def _(s):
        for i in range(NSLAB):
            xs = xslab[i % 2]
            s.wait_ge(sx2[i % 2].h, 16 * (i // 2 + 1))
            if i >= 1:
                s.wait_ge(sm.h, BM + 8 * (i - 1) + 1)
            s.activation(xb_sb[:, :, :], xs[:, :, :], AF.Copy)
            s.activation(sqb_sb[:, :, :], xs[:, :, :], AF.Square).then_inc(sa.h, 1)
            s.wait_ge(sv.h, BV + 8 * i + 4)
            s.activation(statrows[0:1, 4 * SLAB: 5 * SLAB],
                         statrows[0:1, 3 * SLAB: 4 * SLAB], AF.Sqrt).then_inc(sa.h, 1)
            sa.v += 2
            for g in range(6):
                s.wait_ge(sm.h, BM + 8 * i + 3 + g)
                if g < 2:
                    s.activation(qTs[:, g, SLAB * i: SLAB * (i + 1)],
                                 pb[2 + (g % 2)][:, 0:SLAB], AF.Copy).then_inc(sa.h, 1)
                elif g < 4:
                    s.activation(kTs[:, g - 2, SLAB * i: SLAB * (i + 1)],
                                 pb[2 + (g % 2)][:, 0:SLAB], AF.Copy).then_inc(sa.h, 1)
                else:
                    s.activation(vtok[:, 2 * i + (g - 4), :],
                                 pb[4 + (g - 4)][:, 0:256], AF.Copy).then_inc(sa.h, 1)
                sa.v += 1
    blk.scalar(_)

    def _(v):
        inv = 1.0 / D
        r = statrows
        SL = SLAB
        for i in range(NSLAB):
            v.wait_ge(sm.h, BM + 8 * i + 1)
            v.tensor_single_scalar(r[0:1, 0:SL], pb[0][0:1, 0:SL], -inv, ALU.mult)
            v.tensor_single_scalar(r[0:1, SL:2 * SL], pb[0][0:1, SL:2 * SL], inv,
                                   ALU.mult).then_inc(sv.h, 1)
            v.wait_ge(sv.h, BV + 8 * i + 1)
            v.tensor_tensor(r[0:1, 2 * SL:3 * SL], r[0:1, 0:SL], r[0:1, 0:SL],
                            ALU.mult).then_inc(sv.h, 1)
            v.wait_ge(sv.h, BV + 8 * i + 2)
            v.tensor_tensor(r[0:1, 3 * SL:4 * SL], r[0:1, SL:2 * SL],
                            r[0:1, 2 * SL:3 * SL], ALU.subtract).then_inc(sv.h, 1)
            v.wait_ge(sv.h, BV + 8 * i + 3)
            v.tensor_single_scalar(r[0:1, 3 * SL:4 * SL], r[0:1, 3 * SL:4 * SL],
                                   1e-5, ALU.add).then_inc(sv.h, 1)
            v.wait_ge(sa.h, BA + 8 * i + 2)
            v.reciprocal(r[0:1, 5 * SL:6 * SL], r[0:1, 4 * SL:5 * SL]).then_inc(sv.h, 1)
            v.wait_ge(sv.h, BV + 8 * i + 5)
            v.tensor_copy(browbf[0:1, 0:SL], r[0:1, 0:SL])
            v.tensor_copy(browbf[0:1, SL:2 * SL], r[0:1, 5 * SL:6 * SL]).then_inc(sv.h, 1)
            v.wait_ge(sm.h, BM + 8 * i + 2)
            for c in range(NDC):
                ins = v.tensor_tensor(lnsrc(i, c), xslab[i % 2][:, c, :],
                                      pb[1][:, 0:SL], ALU.add)
            ins.then_inc(sv.h, 1)
            v.wait_ge(sv.h, BV + 8 * i + 7)
            for c in range(NDC):
                ins = v.tensor_tensor(lnsrc(i, c), lnsrc(i, c),
                                      pb[1][:, SL:2 * SL], ALU.mult)
            ins.then_inc(sv.h, 1)
            sv.v += 8
    blk.vector(_)

    def _(t):
        onecol = ones_sb[:, 0:1]
        onerow = ones_sb[0:1, :]
        SL = SLAB
        for i in range(NSLAB):
            t.wait_ge(sa.h, BA + 8 * i + 1)
            if i >= 1:
                t.wait_ge(sv.h, BV + 8 * (i - 1) + 1)
            for c in range(NDC):
                t.matmul(pb[0][0:1, 0:SL], onecol, xb_sb[:, c, :],
                         start=(c == 0), stop=(c == NDC - 1))
            for c in range(NDC):
                mm = t.matmul(pb[0][0:1, SL:2 * SL], onecol, sqb_sb[:, c, :],
                              start=(c == 0), stop=(c == NDC - 1))
            mm.then_inc(sm.h, 1)
            sm.v += 1
            t.wait_ge(sv.h, BV + 8 * i + 6)
            t.matmul(pb[1][:, 0:SL], onerow, browbf[0:1, 0:SL], start=True, stop=True)
            t.matmul(pb[1][:, SL:2 * SL], onerow, browbf[0:1, SL:2 * SL],
                     start=True, stop=True).then_inc(sm.h, 1)
            sm.v += 1
            t.wait_ge(sv.h, BV + 8 * i + 8)
            for g in range(6):
                if g < 4:
                    bank = pb[2 + (g % 2)]
                    if g >= 2:
                        t.wait_ge(sa.h, BA + 8 * i + 3 + (g - 2))
                    elif i >= 1:
                        t.wait_ge(sa.h, BA + 8 * (i - 1) + 3 + (g + 2))
                    for c in range(NDC):
                        mm = t.matmul(bank[:, 0:SL],
                                      qkvw_sb[:, c, 128 * g: 128 * (g + 1)],
                                      lnsrc(i, c), start=(c == 0), stop=(c == NDC - 1))
                else:
                    bank = pb[4 + (g - 4)]
                    if i >= 1:
                        t.wait_ge(sa.h, BA + 8 * (i - 1) + 3 + g)
                    sub = g - 4
                    for c in range(NDC):
                        mm = t.matmul(bank[:, 0:256],
                                      lnsrc(i, c)[:, 128 * sub: 128 * (sub + 1)],
                                      qkvw_sb[:, c, 512:768],
                                      start=(c == 0), stop=(c == NDC - 1))
                mm.then_inc(sm.h, 1)
                sm.v += 1
    blk.tensor(_)

    P1_SA, P1_SM, P1_SV = sa.v, sm.v, sv.v

    qlf = scr("qlf", [128, 2, 256], F32, 0)
    klf = scr("klf", [128, 2, 256], F32, 2048)
    ql_bf = scr("ql_bf", [128, 2, 256], BF16, 4096)
    kl_bf = scr("kl_bf", [128, 2, 256], BF16, 5120)
    E2f = scr("E2f", [128, 2, 256], F32, 6144)
    nmcol = scr("nmcol", [128, 2], F32, 8192)
    sums2 = scr("sums2", [128, 2], F32, 8256)
    rs2 = scr("rs2", [128, 2], F32, 8320)
    RS = scr("RS", [128, 8], F32, 8384)
    RSbf = scr("RSbf", [128, 8], BF16, 8448)
    rm8 = scr("rm8", [128, 1], F32, 8480)
    rm8bf = scr("rm8bf", [128, 1], BF16, 8512)
    csr = scr("csr", [1, 8], F32, 8544)
    scrow = scr("scrow", [1, 8], F32, 45248)
    scbf = scr("scbf", [1, 2], BF16, 8608)
    rsc = scr("rsc", [128, 1], F32, 8640)
    r3row = scr("r3row", [1, 256], F32, 8704)
    r3bf = scr("r3bf", [1, 256], BF16, 9728)
    r3c = scr("r3c", [128, 2], F32, 10240)
    ikc_sb = scr("ikc_sb", [128, 2, 768], BF16, 10304)
    r1row = scr("r1row", [1, 512], F32, 10304)
    r1bf = scr("r1bf", [1, 512], BF16, 12352)
    A2bf4 = scr("A2bf4", [128, 2, 1024], BF16, 13376)
    A2T4 = scr("A2T4", [128, 2, 1024], BF16, 17472)
    zb = [scr(f"zb{i}", [128, 2, 256], BF16, 21568 + 1024 * i) for i in range(2)]
    zTb = [scr(f"zTb{i}", [128, 2, 256], BF16, 23616 + 1024 * i) for i in range(2)]
    xzTbf = scr("xzTbf", [128, 2, 256], BF16, 25664)
    w1bf = scr("w1bf", [128, 2, 256], BF16, 26688)
    w1Tbf = scr("w1Tbf", [128, 2, 256], BF16, 27712)
    w2bf = scr("w2bf", [128, 2, 256], BF16, 28736)
    w2Tbf = scr("w2Tbf", [128, 2, 256], BF16, 29760)
    w4bf = scr("w4bf", [128, 2, 256], BF16, 30784)
    w4Tbf = scr("w4Tbf", [128, 2, 256], BF16, 31808)
    F4bf = scr("F4bf", [128, 2, 256], BF16, 32832)
    G4bf = scr("G4bf", [128, 2, 256], BF16, 33856)
    E3b = scr("E3b", [128, 2, 256], BF16, 34880)
    E1b = scr("E1b", [128, 2, 512], BF16, 35904)
    band_sb = scr("band_sb", [128, HN * 7 * 128], BF16, 37952)

    if PHASES >= 2:
        def _(sync):
            sync.wait_ge(sv.h, P1_SV)
            sync.dma_start(out=ikc_sb[:, :, :],
                           in_=ikc.rearrange("p (c n) -> p c n", c=2)).then_inc(sld.h, 16)
            sld.v += 16
            for r4 in range(4):
                if r4 > 0:
                    sync.wait_ge(sv.h, P1_SV + r4)
                w = 896
                sync.dma_start(out=wstg[:, 0:w],
                               in_=bands[:, w * r4: w * (r4 + 1)]).then_inc(sld.h, 16)
                sld.v += 16
        blk.sync(_)
        BLD = sld.v

        def _(v):
            for r4 in range(4):
                v.wait_ge(sld.h, BLD - 16 * (3 - r4))
                w = 896
                ins = v.tensor_copy(band_sb[:, w * r4: w * (r4 + 1)], wstg[:, 0:w])
                ins.then_inc(sv.h, 1)
                sv.v += 1
        blk.vector(_)
        BCAST2 = sv.v

        def _(v):
            v.wait_ge(sa.h, P1_SA)
            v.wait_ge(sm.h, P1_SM)
            v.wait_ge(sv.h, BCAST2)
            for hc in range(2):
                v.tensor_reduce(qlf[:, hc, :],
                                qTs[:, hc, :].rearrange("p (l g) -> p l g", g=16),
                                AX.X, ALU.add)
                ins = v.tensor_reduce(klf[:, hc, :],
                                      kTs[:, hc, :].rearrange("p (l g) -> p l g", g=16),
                                      AX.X, ALU.add)
            ins.then_inc(sv.h, 1)
            sv.v += 1
            v.wait_ge(sv.h, sv.v)
            for hc in range(2):
                v.tensor_single_scalar(ql_bf[:, hc, :], qlf[:, hc, :], 1.0 / 16, ALU.mult)
                ins = v.tensor_single_scalar(kl_bf[:, hc, :], klf[:, hc, :], 1.0 / 16, ALU.mult)
            ins.then_inc(sv.h, 1)
            sv.v += 1
        blk.vector(_)
        LM = sv.v

        B2M, B2V, B2A = sm.v, sv.v, sa.v

        def _(t):
            for h in range(4):
                hc, off = h // 2, 64 * (h % 2)
                if h == 0:
                    t.wait_ge(sv.h, LM)
                else:
                    t.wait_ge(sa.h, B2A + 2 * (h - 1) + 1)
                for lc in range(2):
                    mm = t.matmul(pb[lc][:, 0:256],
                                  ql_bf[off:off + 64, hc, 128 * lc:128 * (lc + 1)],
                                  kl_bf[off:off + 64, hc, :],
                                  start=True, stop=True)
                mm.then_inc(sm.h, 1)
                sm.v += 1
                t.wait_ge(sv.h, B2V + 4 * h + 3)
                if h >= 1:
                    t.wait_ge(sv.h, B2V + 4 * (h - 1) + 4)
                for lc in range(2):
                    mm = t.matmul(pb[2][0:1, 0:256], ones_sb[:, 0:1],
                                  A2bf4[:, lc, 256 * h:256 * (h + 1)],
                                  start=(lc == 0), stop=(lc == 1))
                mm.then_inc(sm.h, 1)
                sm.v += 1
                if h >= 1:
                    t.wait_ge(sa.h, B2A + 2 * (h - 1) + 2)
                for lc in range(2):
                    for cc in range(2):
                        ins = t.transpose(
                            pbt[:, 256 * lc + 128 * cc: 256 * lc + 128 * (cc + 1)],
                            A2bf4[:, lc, 256 * h + 128 * cc: 256 * h + 128 * (cc + 1)],
                            ident_sb[:, :])
                ins.then_inc(sm.h, 1)
                sm.v += 1
        blk.tensor(_)

        def _(v):
            for h in range(4):
                v.wait_ge(sm.h, B2M + 3 * h + 1)
                for lc in range(2):
                    ins = v.tensor_reduce(nmcol[:, lc:lc + 1], pb[lc][:, 0:256], AX.X,
                                          ALU.max, negate=True)
                ins.then_inc(sv.h, 1)
                v.wait_ge(sa.h, B2A + 2 * h + 1)
                v.reciprocal(rs2[:, 0:2], sums2[:, 0:2]).then_inc(sv.h, 1)
                v.wait_ge(sv.h, B2V + 4 * h + 2)
                for lc in range(2):
                    ins = v.tensor_single_scalar(A2bf4[:, lc, 256 * h:256 * (h + 1)],
                                                 E2f[:, lc, :], rs2[:, lc:lc + 1], ALU.mult)
                ins.then_inc(sv.h, 1)
                v.wait_ge(sv.h, B2V + 4 * h + 3)
                for lc in range(2):
                    v.tensor_reduce(RS[:, 2 * h + lc: 2 * h + lc + 1],
                                    A2bf4[:, lc, 256 * h:256 * (h + 1)], AX.X, ALU.add)
                v.wait_ge(sm.h, B2M + 3 * h + 2)
                ins = v.tensor_reduce(csr[0:1, h:h + 1], pb[2][0:1, 0:256], AX.X, ALU.max)
                ins.then_inc(sv.h, 1)
                sv.v += 4
        blk.vector(_)

        def _(s):
            for h in range(4):
                s.wait_ge(sv.h, B2V + 4 * h + 1)
                for lc in range(2):
                    s.activation(E2f[:, lc, :], pb[lc][:, 0:256], AF.Exp,
                                 bias=nmcol[:, lc:lc + 1],
                                 accum_out=sums2[:, lc:lc + 1]).then_inc(
                                     sa.h, (1 if lc == 1 else 0))
                sa.v += 1
                s.wait_ge(sm.h, B2M + 3 * h + 3)
                for lc in range(2):
                    for cc in range(2):
                        ins = s.activation(
                            A2T4[:, cc, 256 * h + 128 * lc: 256 * h + 128 * (lc + 1)],
                            pbt[:, 256 * lc + 128 * cc: 256 * lc + 128 * (cc + 1)],
                            AF.Copy)
                ins.then_inc(sa.h, 1)
                sa.v += 1
        blk.scalar(_)
        S2M, S2V, S2A = sm.v, sv.v, sa.v

        def _(v):
            v.wait_ge(sv.h, S2V)
            v.tensor_copy(RSbf[:, :], RS[:, :]).then_inc(sv.h, 1)
            sv.v += 1
        blk.vector(_)

        def _(t):
            t.wait_ge(sv.h, sv.v)
            t.wait_ge(sa.h, S2A)
            t.transpose(pbt[0:8, 0:128], RSbf[:, :], ident_sb[:, :]).then_inc(sm.h, 1)
            sm.v += 1
        blk.tensor(_)
        TR1 = sm.v

        def _(v):
            v.wait_ge(sm.h, TR1)
            v.tensor_reduce(rm8[0:8, 0:1], pbt[0:8, 0:128], AX.X,
                            ALU.max).then_inc(sv.h, 1)
            sv.v += 1
            v.wait_ge(sv.h, sv.v)
            v.tensor_copy(rm8bf[0:8, 0:1], rm8[0:8, 0:1]).then_inc(sv.h, 1)
            sv.v += 1
        blk.vector(_)

        def _(t):
            t.wait_ge(sv.h, sv.v)
            t.transpose(pbt[0:1, 128:136], rm8bf[0:8, 0:1], ident_sb[0:8, 0:8]).then_inc(sm.h, 1)
            sm.v += 1
        blk.tensor(_)
        TR2 = sm.v

        def _(v):
            v.wait_ge(sm.h, TR2)
            v.tensor_reduce(scrow[0:1, 0:1], pbt[0:1, 128:136], AX.X, ALU.max)
            v.tensor_reduce(scrow[0:1, 1:2], csr[0:1, 0:4], AX.X,
                            ALU.max).then_inc(sv.h, 1)
            sv.v += 1
        blk.vector(_)
        SCROW = sv.v

        def _(sync):
            sync.wait_ge(sv.h, SCROW)
            sync.dma_start(out=cc_scale_i, in_=scrow[0:1, 0:2]).then_inc(sci.h, 16)
            sci.v += 16
        blk.sync(_)
        SCW = sci.v

        def _(g):
            g.wait_ge(sci.h, SCW)
            g.collective_compute(
                "AllReduce", ALU.max, replica_groups=[list(range(NCORES))],
                ins=[cc_scale_i.opt()], outs=[cc_scale_o.opt()],
            ).then_inc(scc.h)
            scc.v += 1
        blk.gpsimd(_)

        def _(sync):
            sync.wait_ge(scc.h, scc.v)
            sync.dma_start(out=scrow[0:1, 2:4], in_=cc_scale_o).then_inc(sst.h, 16)
            sst.v += 16
        blk.sync(_)
        SCR2 = sst.v

        def _(v):
            v.wait_ge(sst.h, SCR2)
            v.tensor_tensor(scrow[0:1, 4:5], scrow[0:1, 2:3], scrow[0:1, 3:4],
                            ALU.mult).then_inc(sv.h, 1)
            sv.v += 1
            v.wait_ge(sv.h, sv.v)
            v.reciprocal(scrow[0:1, 5:6], scrow[0:1, 4:5]).then_inc(sv.h, 1)
            sv.v += 1
            v.wait_ge(sv.h, sv.v)
            v.tensor_copy(scbf[0:1, 0:1], scrow[0:1, 5:6]).then_inc(sv.h, 1)
            sv.v += 1
        blk.vector(_)
        RSCBF = sv.v

        def _(t):
            t.wait_ge(sv.h, RSCBF)
            t.matmul(pb[5][:, 0:1], ones_sb[0:1, :], scbf[0:1, 0:1],
                     start=True, stop=True).then_inc(sm.h, 1)
            sm.v += 1
        blk.tensor(_)
        RSCMM = sm.v

        def _(v):
            v.wait_ge(sm.h, RSCMM)
            v.tensor_copy(rsc[:, 0:1], pb[5][:, 0:1]).then_inc(sv.h, 1)
            sv.v += 1
        blk.vector(_)
        RSC = sv.v

        for h in range(HN):
            hc, off = h // 2, 64 * (h % 2)
            B5M, B5A, B5V = sm.v, sa.v, sv.v

            def _(t, h=h, hc=hc, off=off, B5M=B5M, B5A=B5A, B5V=B5V):
                t.wait_ge(sv.h, B5V)
                for tc in range(32):
                    if tc >= 1:
                        t.wait_ge(sa.h, B5A + tc)
                    elif h == 0:
                        t.wait_ge(sv.h, LM)
                    t.matmul(pb[0][:, 0:256],
                             kTs[off:off + 64, hc, 128 * tc:128 * (tc + 1)],
                             ql_bf[off:off + 64, hc, :],
                             start=True, stop=True).then_inc(sm.h, 1)
                    sm.v += 1
                    t.wait_ge(sa.h, B5A + tc + 1)
                    t.matmul(pb[3][0:1, 0:256], ones_sb[:, 0:1], E3b[:, tc % 2, :],
                             start=(tc == 0), stop=(tc == 31))
                    for lc in range(2):
                        mm = t.matmul(pb[1 + lc][:, 0:64],
                                      E3b[:, tc % 2, 128 * lc:128 * (lc + 1)],
                                      vtok[:, tc, 64 * h:64 * (h + 1)],
                                      start=(tc == 0), stop=(tc == 31))
                    mm.then_inc(sm.h, 1)
                    sm.v += 1
            blk.tensor(_)

            def _(s, B5M=B5M):
                for tc in range(32):
                    s.wait_ge(sm.h, B5M + 2 * tc + 1)
                    if tc >= 2:
                        s.wait_ge(sm.h, B5M + 2 * (tc - 2) + 2)
                    s.activation(E3b[:, tc % 2, :], pb[0][:, 0:256],
                                 AF.Exp).then_inc(sa.h, 1)
                    sa.v += 1
            blk.scalar(_)
            E3M = sm.v

            def _(v, h=h, E3M=E3M):
                v.wait_ge(sm.h, E3M)
                v.reciprocal(r3row[0:1, 0:256], pb[3][0:1, 0:256]).then_inc(sv.h, 1)
                sv.v += 1
                v.wait_ge(sv.h, sv.v)
                v.tensor_copy(r3bf[0:1, 0:256], r3row[0:1, 0:256]).then_inc(sv.h, 1)
                sv.v += 1
            blk.vector(_)
            R3V = sv.v

            def _(t, B2A_l=sa.v, R3V=R3V):
                t.wait_ge(sv.h, R3V)
                if h > 0:
                    t.wait_ge(sa.h, B2A_l)
                for lc in range(2):
                    ins = t.transpose(pbt[:, 2 * lc:2 * lc + 1],
                                      r3bf[0:1, 128 * lc:128 * (lc + 1)],
                                      ident_sb[0:1, 0:1])
                ins.then_inc(sm.h, 1)
                sm.v += 1
            blk.tensor(_)
            R3T = sm.v

            def _(s, R3T=R3T):
                s.wait_ge(sm.h, R3T)
                for lc in range(2):
                    ins = s.activation(r3c[:, lc:lc + 1], pbt[:, 2 * lc:2 * lc + 1],
                                       AF.Copy)
                ins.then_inc(sa.h, 1)
                sa.v += 1
            blk.scalar(_)
            R3C = sa.v

            def _(v, h=h, R3C=R3C):
                v.wait_ge(sa.h, R3C)
                for lc in range(2):
                    ins = v.tensor_single_scalar(F4bf[:, lc, 64 * h:64 * (h + 1)],
                                                 pb[1 + lc][:, 0:64],
                                                 r3c[:, lc:lc + 1], ALU.mult)
                ins.then_inc(sv.h, 1)
                sv.v += 1
            blk.vector(_)

        IK = {7.0: 0, 15.0: 1, 13.0: 2}

        for h in range(HN):
            B4V = sv.v

            def _(v, h=h, B4V=B4V, GPREV=sm.v, ILD=sld.v):
                v.wait_ge(sv.h, RSC)
                v.wait_ge(sm.h, GPREV)
                v.wait_ge(sld.h, ILD)
                for c in range(2):
                    v.tensor_single_scalar(zb[0][:, c, :],
                                           A2T4[:, c, 256 * h:256 * (h + 1)],
                                           rsc[:, 0:1], ALU.mult)
                    ins = v.tensor_single_scalar(zTb[0][:, c, :],
                                                 A2bf4[:, c, 256 * h:256 * (h + 1)],
                                                 rsc[:, 0:1], ALU.mult)
                ins.then_inc(sv.h, 1)
                sv.v += 1
            blk.vector(_)

            cur, curT = zb[0], zTb[0]
            nxt, nxtT = zb[1], zTb[1]
            A2T_h = lambda kc, oc, h=h: A2T4[:, kc, 256 * h + 128 * oc: 256 * h + 128 * (oc + 1)]

            for it in range(6):
                prods = [
                    (lambda kc, oc: A2T_h(kc, oc), cur, [(w1bf, ("isub", 0))]),
                    (lambda kc, oc, cur=cur: cur[:, kc, 128 * oc:128 * (oc + 1)],
                     A2T4[:, :, 256 * h:256 * (h + 1)],
                     [(xzTbf, ("copy",)), (w1Tbf, ("isub", 0))]),
                    (lambda kc, oc: xzTbf[:, kc, 128 * oc:128 * (oc + 1)], w1bf,
                     [(w2bf, ("isub", 1))]),
                    (lambda kc, oc: w1bf[:, kc, 128 * oc:128 * (oc + 1)], xzTbf,
                     [(w2Tbf, ("isub", 1))]),
                    (lambda kc, oc: xzTbf[:, kc, 128 * oc:128 * (oc + 1)], w2bf,
                     [(w4bf, ("isub", 2))]),
                    (lambda kc, oc: w2bf[:, kc, 128 * oc:128 * (oc + 1)], xzTbf,
                     [(w4Tbf, ("isub", 2))]),
                    (lambda kc, oc, curT=curT: curT[:, kc, 128 * oc:128 * (oc + 1)],
                     w4bf, [(nxt, ("scale", 0.25))]),
                    (lambda kc, oc, cur=cur: w4bf[:, kc, 128 * oc:128 * (oc + 1)],
                     curT, [(nxtT, ("scale", 0.25))]),
                ]
                for pi, (lf, rhs_t, evacs) in enumerate(prods):
                    bank = 2 * (pi % 2)
                    PV = sv.v

                    def _(t, lf=lf, rhs_t=rhs_t, bank=bank, PV=PV):
                        t.wait_ge(sv.h, PV)
                        for oc in range(2):
                            for kc in range(2):
                                mm = t.matmul(pb[bank + oc][:, 0:256], lf(kc, oc),
                                              rhs_t[:, kc, :],
                                              start=(kc == 0), stop=(kc == 1))
                        mm.then_inc(sm.h, 1)
                        sm.v += 1
                    blk.tensor(_)
                    PM = sm.v

                    def _(v, evacs=evacs, bank=bank, PM=PM):
                        v.wait_ge(sm.h, PM)
                        for dst, mode in evacs:
                            for oc in range(2):
                                if mode[0] == "isub":
                                    ins = v.tensor_tensor(
                                        dst[:, oc, :],
                                        ikc_sb[:, oc, 256 * mode[1]:256 * (mode[1] + 1)],
                                        pb[bank + oc][:, 0:256], ALU.subtract)
                                elif mode[0] == "copy":
                                    ins = v.tensor_copy(dst[:, oc, :], pb[bank + oc][:, 0:256])
                                else:
                                    ins = v.tensor_single_scalar(
                                        dst[:, oc, :], pb[bank + oc][:, 0:256],
                                        mode[1], ALU.mult)
                        ins.then_inc(sv.h, 1)
                        sv.v += 1
                    blk.vector(_)
                cur, nxt = nxt, cur
                curT, nxtT = nxtT, curT

            GV = sv.v

            def _(t, h=h, curT=curT, GV=GV):
                t.wait_ge(sv.h, GV)
                for oc in range(2):
                    for kc in range(2):
                        mm = t.matmul(pb[4 + oc][:, 0:64],
                                      curT[:, kc, 128 * oc:128 * (oc + 1)],
                                      F4bf[:, kc, 64 * h:64 * (h + 1)],
                                      start=(kc == 0), stop=(kc == 1))
                mm.then_inc(sm.h, 1)
                sm.v += 1
            blk.tensor(_)
            GM = sm.v

            def _(v, h=h, GM=GM):
                v.wait_ge(sm.h, GM)
                for oc in range(2):
                    ins = v.tensor_copy(G4bf[:, oc, 64 * h:64 * (h + 1)],
                                        pb[4 + oc][:, 0:64])
                ins.then_inc(sv.h, 1)
                sv.v += 1
            blk.vector(_)

        for h in range(HN):
            hc, off = h // 2, 64 * (h % 2)
            for tt in range(8):
                B6V = sv.v
                B6A = sa.v

                def _(t, hc=hc, off=off, tt=tt, B6V=B6V, B6A=B6A):
                    t.wait_ge(sv.h, B6V)
                    for lc in range(2):
                        if lc == 1 or tt > 0 or True:
                            t.wait_ge(sa.h, B6A + lc)
                        t.matmul(pb[0][:, 0:512],
                                 kl_bf[off:off + 64, hc, 128 * lc:128 * (lc + 1)],
                                 qTs[off:off + 64, hc, 512 * tt:512 * (tt + 1)],
                                 start=True, stop=True).then_inc(sm.h, 1)
                        sm.v += 1
                        t.wait_ge(sa.h, B6A + lc + 1)
                        t.matmul(pb[1][0:64, 0:512], G4bf[:, lc, 64 * h:64 * (h + 1)],
                                 E1b[:, lc, :], start=(lc == 0), stop=(lc == 1))
                        mm = t.matmul(pb[2][0:1, 0:512], ones_sb[:, 0:1],
                                      E1b[:, lc, :], start=(lc == 0), stop=(lc == 1))
                        mm.then_inc(sm.h, 1)
                        sm.v += 1
                blk.tensor(_)
                E1M0 = sm.v - 4

                def _(s, E1M0=E1M0):
                    for lc in range(2):
                        s.wait_ge(sm.h, E1M0 + 2 * lc + 1)
                        s.activation(E1b[:, lc, :], pb[0][:, 0:512],
                                     AF.Exp).then_inc(sa.h, 1)
                        sa.v += 1
                blk.scalar(_)
                E1M = sm.v

                def _(v, E1M=E1M):
                    v.wait_ge(sm.h, E1M)
                    v.reciprocal(r1row[0:1, 0:512], pb[2][0:1, 0:512]).then_inc(sv.h, 1)
                    sv.v += 1
                    v.wait_ge(sv.h, sv.v)
                    v.tensor_copy(r1bf[0:1, 0:512], r1row[0:1, 0:512]).then_inc(sv.h, 1)
                    sv.v += 1
                blk.vector(_)
                R1V = sv.v

                def _(t, R1V=R1V):
                    t.wait_ge(sv.h, R1V)
                    t.matmul(pb[3][0:64, 0:512], ones_sb[0:1, 0:64],
                             r1bf[0:1, 0:512], start=True, stop=True).then_inc(sm.h, 1)
                    sm.v += 1
                blk.tensor(_)
                R1M = sm.v

                def _(s, hc=hc, off=off, tt=tt, E1M=E1M):
                    s.wait_ge(sm.h, E1M)
                    s.activation(outTo[off:off + 64, hc, 512 * tt:512 * (tt + 1)],
                                 pb[1][0:64, 0:512], AF.Copy).then_inc(sa.h, 1)
                    sa.v += 1
                blk.scalar(_)
                OA = sa.v

                def _(v, hc=hc, off=off, tt=tt, OA=OA, R1M=R1M):
                    v.wait_ge(sa.h, OA)
                    v.wait_ge(sm.h, R1M)
                    v.tensor_tensor(outTo[off:off + 64, hc, 512 * tt:512 * (tt + 1)],
                                    outTo[off:off + 64, hc, 512 * tt:512 * (tt + 1)],
                                    pb[3][0:64, 0:512], ALU.mult).then_inc(sv.h, 1)
                    sv.v += 1
                blk.vector(_)

            CB = sv.v

            def _(t, h=h, CB=CB):
                t.wait_ge(sv.h, CB)
                for tc in range(32):
                    if tc >= 2:
                        t.wait_ge(sv.h, CB + tc - 1)
                    srcs = []
                    if tc == 0:
                        srcs.append((31, 3))
                    elif tc == 16:
                        srcs.append((15, 5))
                    else:
                        srcs.append((tc - 1, 0))
                    srcs.append((tc, 1))
                    if tc == 31:
                        srcs.append((0, 4))
                    elif tc == 15:
                        srcs.append((16, 6))
                    else:
                        srcs.append((tc + 1, 2))
                    for si, (sc_, bi) in enumerate(srcs):
                        mm = t.matmul(pb[4 + tc % 2][0:64, 0:128],
                                      vtok[:, sc_, 64 * h:64 * (h + 1)],
                                      band_sb[:, (h * 7 + bi) * 128:(h * 7 + bi + 1) * 128],
                                      start=(si == 0), stop=(si == 2))
                    mm.then_inc(sm.h, 1)
                    sm.v += 1
            blk.tensor(_)
            CM0 = sm.v - 32

            def _(v, h=h, hc=hc, off=off, CM0=CM0, CBV=sv.v):
                v.wait_ge(sv.h, CBV)
                for tc in range(32):
                    v.wait_ge(sm.h, CM0 + tc + 1)
                    v.tensor_tensor(outTo[off:off + 64, hc, 128 * tc:128 * (tc + 1)],
                                    outTo[off:off + 64, hc, 128 * tc:128 * (tc + 1)],
                                    pb[4 + tc % 2][0:64, 0:128], ALU.add).then_inc(sv.h, 1)
                    sv.v += 1
            blk.vector(_)

    readbuf = at("readbuf", [128, 4, TH], BF16, offset=98304)
    wv_bf = at("wv_bf", [128, NDC, D], BF16, offset=114688)
    wvstage = at("wvstage", [128, 2, D], F32, offset=131072)
    xgstage = at("xgstage", [128, NDC, 512], BF16, offset=139264)
    owo_bf = scr("owo_bf", [128, 4, D], BF16, 8192)
    owm_bf = scr("owm_bf", [128, 2, D], BF16, 16384)
    wqk_bf = scr("wqk_bf", [128, NDC, 512], BF16, 20480)
    q0stage = scr("q0stage", [128, 16, 512], BF16, 28672)
    xg0buf = [at(f"xg0buf{i}", [128, 1, TH], BF16, offset=98304 + 4096 * i)
              for i in range(2)]

    if PHASES >= 3:
        P2_SV, P2_SA, P2_SM = sv.v, sa.v, sm.v

        def _(sync):
            sync.wait_ge(sv.h, P2_SV)
            sync.dma_start(out=cc_out_i.rearrange("(c p) t -> p c t", p=128),
                           in_=outTo[:, :, :]).then_inc(sci.h, 16)
            sci.v += 16
        blk.sync(_)
        OUTW_CI = sci.v

        def _(g):
            g.wait_ge(sci.h, OUTW_CI)
            g.collective_compute(
                "AllGather", ALU.bypass,
                replica_groups=[[0, 1], [2, 3], [4, 5], [6, 7]],
                ins=[cc_out_i.opt()], outs=[cc_out_o.opt()],
            ).then_inc(scc.h)
            scc.v += 1
        blk.gpsimd(_)
        OUT_AG = scc.v

        def _(sync):
            sync.wait_ge(scc.h, OUT_AG)
            sync.dma_start(
                out=readbuf[:, :, :],
                in_=cc_out_o.rearrange("(c p) t -> p c t", p=128)[:, :, TH:T],
            ).then_inc(sx2[0].h, 16)
            sx2[0].v += 16
            sync.dma_start(out=wstg[:, 0:2048].rearrange("p (c n) -> p c n", c=2),
                           in_=outw_o.rearrange("(c p) n -> p c n", p=128)[:, 0:2, :]
                           ).then_inc(sld.h, 16)
            sld.v += 16
        blk.sync(_)
        RB_LD = sx2[0].v

        def _(sync):
            base = sld.v

            seq = [
                (outw_o.rearrange("(c p) n -> p c n", p=128)[:, 2:4, :], 2),
                (outw_m.rearrange("(c p) n -> p c n", p=128)[:, 0:2, :], 2),
                (wqk.rearrange("(c p) n -> p c n", p=128)[:, 0:4, :], 4),
                (wqk.rearrange("(c p) n -> p c n", p=128)[:, 4:8, :], 4),
            ]
            for i, (srcv, nch) in enumerate(seq):
                sync.wait_ge(sv.h, P2_SV + i + 1)
                sync.dma_start(out=wstg[:, 0:2048].rearrange("p (c n) -> p c n", c=nch),
                               in_=srcv).then_inc(sld.h, 16)
                sld.v += 16
            for r in range(4):
                sync.wait_ge(sv.h, P2_SV + 5 + r)
                sync.dma_start(out=wvstage[:, :, :],
                               in_=wv.rearrange("(c p) n -> p c n", p=128)[:, 2 * r:2 * r + 2, :]
                               ).then_inc(sld.h, 16)
                sld.v += 16
        blk.sync(_)

        def _(v):
            base = sld.v - 16 * 9
            v.wait_ge(sld.h, base + 16)
            v.tensor_copy(owo_bf[:, 0:2, :],
                          wstg[:, 0:2048].rearrange("p (c n) -> p c n", c=2)).then_inc(sv.h, 1)
            v.wait_ge(sld.h, base + 32)
            v.tensor_copy(owo_bf[:, 2:4, :],
                          wstg[:, 0:2048].rearrange("p (c n) -> p c n", c=2)).then_inc(sv.h, 1)
            v.wait_ge(sld.h, base + 48)
            v.tensor_copy(owm_bf[:, :, :],
                          wstg[:, 0:2048].rearrange("p (c n) -> p c n", c=2)).then_inc(sv.h, 1)
            v.wait_ge(sld.h, base + 64)
            v.tensor_copy(wqk_bf[:, 0:4, :],
                          wstg[:, 0:2048].rearrange("p (c n) -> p c n", c=4)).then_inc(sv.h, 1)
            v.wait_ge(sld.h, base + 80)
            v.tensor_copy(wqk_bf[:, 4:8, :],
                          wstg[:, 0:2048].rearrange("p (c n) -> p c n", c=4)).then_inc(sv.h, 1)
            for r in range(4):
                v.wait_ge(sld.h, base + 96 + 16 * r)
                v.tensor_copy(wv_bf[:, 2 * r:2 * r + 2, :],
                              wvstage[:, :, :]).then_inc(sv.h, 1)
            sv.v += 9
        blk.vector(_)
        WCAST3 = sv.v

        B3M, B3A = sm.v, sa.v

        def _(t):
            t.wait_ge(sv.h, WCAST3)
            t.wait_ge(sx2[0].h, RB_LD)
            for jt in range(4):
                for d2 in range(NDC):
                    bank = pb[d2 % 2]
                    idx = jt * NDC + d2
                    if idx >= 2:
                        t.wait_ge(sa.h, B3A + idx - 1)
                    first = True
                    for cch in range(4):
                        t.matmul(bank[:, 0:512], owo_bf[:, cch, 128 * d2:128 * (d2 + 1)],
                                 readbuf[:, cch, 512 * jt:512 * (jt + 1)],
                                 start=first, stop=False)
                        first = False
                    for hc in range(2):
                        mm = t.matmul(bank[:, 0:512],
                                      owm_bf[:, hc, 128 * d2:128 * (d2 + 1)],
                                      outTo[:, hc, 512 * jt:512 * (jt + 1)],
                                      start=False, stop=(hc == 1))
                    mm.then_inc(sm.h, 1)
                    sm.v += 1
        blk.tensor(_)

        def _(s, B3CI=sci.v):
            for jt in range(4):
                for d2 in range(NDC):
                    idx = jt * NDC + d2
                    s.wait_ge(sm.h, B3M + idx + 1)
                    if d2 == 0 and jt >= 1:
                        s.wait_ge(sci.h, B3CI + 16 * jt)
                    s.activation(xgstage[:, d2, :], pb[d2 % 2][:, 0:512], AF.Identity,
                                 bias=outb_sb[:, d2:d2 + 1]).then_inc(sa.h, 1)
                    sa.v += 1
        blk.scalar(_)
        XGA = sa.v

        def _(sync):
            for jt in range(4):
                sync.wait_ge(sa.h, B3A + NDC * (jt + 1))
                sync.dma_start(
                    out=cc_xg_i.rearrange("(c p) t -> p c t", p=128)[:, :, 512 * jt:512 * (jt + 1)],
                    in_=xgstage[:, :, :]).then_inc(sci.h, 16)
                sci.v += 16
        blk.sync(_)
        XG_CI = sci.v

        def _(g):
            g.wait_ge(sci.h, XG_CI)
            g.collective_compute(
                "AllGather", ALU.bypass,
                replica_groups=[[0, 2, 4, 6], [1, 3, 5, 7]],
                ins=[cc_xg_i.opt()], outs=[cc_xg_o.opt()],
            ).then_inc(scc.h)
            scc.v += 1
        blk.gpsimd(_)
        XG_AG = scc.v

        sxg = [S(nc, "sxg0"), S(nc, "sxg1")]
        ENC0 = sv.v

        def _(sync):
            sync.wait_ge(scc.h, XG_AG)
            for ch in range(NDC):
                if ch >= 2:
                    sync.wait_ge(sv.h, ENC0 + ch - 1)
                sync.dma_start(
                    out=xg0buf[ch % 2][:, 0, :],
                    in_=cc_xg_o.rearrange("(c p) t -> p c t", p=128)[:, ch, :],
                ).then_inc(sxg[ch % 2].h, 16)
                sxg[ch % 2].v += 16
        blk.sync(_)

        def _(v):
            for ch in range(NDC):
                v.wait_ge(sxg[ch % 2].h, 16 * (ch // 2 + 1))
                v.tensor_tensor(encT[:, ch, :], xg0buf[ch % 2][:, 0, :],
                                ln0h[:, ch, :], ALU.add).then_inc(sv.h, 1)
                sv.v += 1
        blk.vector(_)
        ENCV = sv.v

        B3M2, B3A2 = sm.v, sa.v

        def _(t):
            t.wait_ge(sv.h, WCAST3)
            for jt in range(4):
                for d2 in range(NDC):
                    bank = pb[2 + d2 % 2]
                    idx = jt * NDC + d2
                    if idx >= 2:
                        t.wait_ge(sa.h, B3A2 + idx - 1)
                    for dch in range(NDC):
                        mm = t.matmul(bank[:, 0:512],
                                      wv_bf[:, dch, 128 * d2:128 * (d2 + 1)],
                                      ln0h[:, dch, 512 * jt:512 * (jt + 1)],
                                      start=(dch == 0), stop=(dch == NDC - 1))
                    mm.then_inc(sm.h, 1)
                    sm.v += 1
        blk.tensor(_)

        def _(s):
            for jt in range(4):
                for d2 in range(NDC):
                    idx = jt * NDC + d2
                    s.wait_ge(sm.h, B3M2 + idx + 1)
                    s.activation(valT[:, d2, 512 * jt:512 * (jt + 1)],
                                 pb[2 + d2 % 2][:, 0:512], AF.Identity,
                                 bias=wvb_sb[:, d2:d2 + 1]).then_inc(sa.h, 1)
                    sa.v += 1
        blk.scalar(_)
        VALA = sa.v

    Qg = [at(f"Qg{i}", [128, 256], BF16, offset=32768 + 512 * i) for i in range(2)]
    Kg = [at(f"Kg{i}", [128, 256], BF16, offset=33792 + 512 * i) for i in range(2)]
    prod4 = at("prod4", [128, 256], F32, offset=34816)
    junk4 = at("junk4", [128, 256], F32, offset=35840)
    ctb = at("ctb", [128, NET], F32, offset=36864)
    av16 = at("av16", [128, NET], F32, offset=37376)
    hi_f = at("hi_f", [128, NET], F32, offset=37888)
    lo_f = at("lo_f", [128, NET], F32, offset=38400)
    hii4 = at("hii4", [128, NET], I32, offset=38912)
    loi4 = at("loi4", [128, NET], I32, offset=39424)
    adjc_cl = at("adjc_cl", [128, NET], I32, offset=46592)
    Ub = [at(f"Ub{i}", [128, 64], BF16, offset=39936 + 128 * i) for i in range(2)]
    Vb = [at(f"Vb{i}", [128, 64], BF16, offset=40192 + 128 * i) for i in range(2)]
    a2dr = at("a2dr", [64, 64], F32, offset=40448)
    acols = at("acols", [64, 8], F32, offset=40704)
    arow1 = at("arow1", [1, 64], F32, offset=40736)
    alpha2 = at("alpha2", [64, 64], F32, offset=41024)
    alpha2b = at("alpha2b", [64, 72], BF16, offset=41280)
    selout = at("selout", [32, 64], BF16, offset=41440)
    alpharow = scr("alpharow", [1, TH], BF16, 36992)
    a2dsb = at("a2dsb", [64, 64], F32, offset=46080)

    if PHASES >= 4:
        B4M, B4A = sm.v, sa.v

        def _(t):
            t.wait_ge(sv.h, ENCV)
            for tch in range(16):
                bank = pb[4 + tch % 2]
                if tch >= 2:
                    t.wait_ge(sa.h, B4A + tch - 1)
                for dch in range(NDC):
                    mm = t.matmul(bank[:, 0:512],
                                  encT[:, dch, 128 * tch:128 * (tch + 1)],
                                  wqk_bf[:, dch, :],
                                  start=(dch == 0), stop=(dch == NDC - 1))
                mm.then_inc(sm.h, 1)
                sm.v += 1
        blk.tensor(_)

        def _(s):
            for tch in range(16):
                s.wait_ge(sm.h, B4M + tch + 1)
                s.activation(q0stage[:, tch, :], pb[4 + tch % 2][:, 0:512],
                             AF.Copy).then_inc(sa.h, 1)
                sa.v += 1
        blk.scalar(_)
        Q0A = sa.v

        def _(sync):
            sync.wait_ge(sa.h, Q0A)
            sync.dma_start(out=cc_tab_i.rearrange("(c p) t -> p c t", p=128),
                           in_=q0stage[:, :, :]).then_inc(sci.h, 16)
            sci.v += 16
        blk.sync(_)
        TAB_CI = sci.v

        def _(g):
            g.wait_ge(sci.h, TAB_CI)
            g.collective_compute(
                "AllGather", ALU.bypass, replica_groups=[list(range(NCORES))],
                ins=[cc_tab_i.opt()], outs=[cc_tab_o.opt()],
            ).then_inc(scc.h)
            scc.v += 1
        blk.gpsimd(_)
        TAB_AG = scc.v

        def _(v):
            v.wait_ge(sld.h, P0_LD)
            v.wait_ge(sa.h, VALA)
            v.wait_ge(sv.h, ENCV)
            v.tensor_single_scalar(av16[:, :], adjv_sb[:, :], 1.0 / 16.0, ALU.mult)
            v.tensor_single_scalar(hii4[:, :], adjr_sb[:, :], 6, ALU.logical_shift_right)
            v.tensor_single_scalar(loi4[:, :], adjr_sb[:, :], 63, ALU.bitwise_and)
            v.tensor_single_scalar(adjc_cl[:, :], adjc_sb[:, :], 4095,
                                   ALU.min).then_inc(sv.h, 1)
            sv.v += 1
            v.wait_ge(sv.h, sv.v)
            v.tensor_copy(hi_f[:, :], hii4[:, :])
            v.tensor_copy(lo_f[:, :], loi4[:, :]).then_inc(sv.h, 1)
            sv.v += 1
        blk.vector(_)
        EPREP = sv.v

        BGV, BGA, BGM = sv.v, sa.v, sm.v
        sg2 = [S(nc, "sg20"), S(nc, "sg21")]

        def _(g):
            g.wait_ge(scc.h, TAB_AG)
            g.wait_ge(sv.h, EPREP)
            for et in range(NET):
                if et >= 2:
                    g.wait_ge(sv.h, BGV + 2 * (et - 2) + 1)
                g.indirect_dma_start(
                    Qg[et % 2][:, :], None, cc_tab_o,
                    IndirectOffsetOnAxis(adjr_sb[:, et:et + 1], 0),
                ).then_inc(sg2[et % 2].h, 16)
                g.indirect_dma_start(
                    Kg[et % 2][:, :], None, cc_tab_o,
                    IndirectOffsetOnAxis(adjc_cl[:, et:et + 1], 0),
                    element_offset=256,
                ).then_inc(sg2[et % 2].h, 16)
                sg2[et % 2].v += 32
        blk.gpsimd(_)

        def _(v):
            for et in range(NET):
                v.wait_ge(sg2[et % 2].h, 32 * (et // 2 + 1))
                if et >= 1:
                    v.wait_ge(sa.h, BGA + et)
                v.tensor_tensor(prod4[:, :], Qg[et % 2][:, :], Kg[et % 2][:, :],
                                ALU.mult).then_inc(sv.h, 1)
                v.wait_ge(sa.h, BGA + et + 1)
                if et >= 2:
                    v.wait_ge(sm.h, BGM + et - 1)
                v.tensor_scalar(Ub[et % 2][:, :], iota_sb[:, :], hi_f[:, et:et + 1],
                                ctb[:, et:et + 1], ALU.is_equal, ALU.mult)
                v.tensor_single_scalar(Vb[et % 2][:, :], iota_sb[:, :],
                                       lo_f[:, et:et + 1],
                                       ALU.is_equal).then_inc(sv.h, 1)
                sv.v += 2
        blk.vector(_)

        def _(s):
            for et in range(NET):
                s.wait_ge(sv.h, BGV + 2 * et + 1)
                s.activation(junk4[:, :], prod4[:, :], AF.Copy,
                             scale=av16[:, et:et + 1],
                             accum_out=ctb[:, et:et + 1]).then_inc(sa.h, 1)
                sa.v += 1
        blk.scalar(_)

        def _(t):
            for et in range(NET):
                t.wait_ge(sv.h, BGV + 2 * et + 2)
                t.matmul(pb[6][0:64, 0:64], Ub[et % 2][:, :], Vb[et % 2][:, :],
                         start=(et == 0), stop=(et == NET - 1)).then_inc(sm.h, 1)
                sm.v += 1
        blk.tensor(_)
        A2DM = sm.v

        def _(v):
            v.wait_ge(sm.h, A2DM)
            v.tensor_copy(a2dsb[:, :], pb[6][0:64, 0:64]).then_inc(sv.h, 1)
            sv.v += 1
        blk.vector(_)
        A2DV = sv.v

        def _(sync):
            sync.wait_ge(sv.h, A2DV)
            sync.dma_start(out=cc_a2d_i, in_=a2dsb[:, :]).then_inc(sci.h, 16)
            sci.v += 16
        blk.sync(_)
        A2D_CI = sci.v

        def _(g):
            g.wait_ge(sci.h, A2D_CI)
            g.collective_compute(
                "AllReduce", ALU.add, replica_groups=[list(range(NCORES))],
                ins=[cc_a2d_i.opt()], outs=[cc_a2d_o.opt()],
            ).then_inc(scc.h)
            scc.v += 1
        blk.gpsimd(_)
        A2D_AR = scc.v

        def _(sync):
            sync.wait_ge(scc.h, A2D_AR)
            sync.dma_start(out=a2dr[:, :], in_=cc_a2d_o).then_inc(sx2[0].h, 16)
            sx2[0].v += 16
        blk.sync(_)
        A2DR_LD = sx2[0].v

        def _(v):
            v.wait_ge(sx2[0].h, A2DR_LD)
            v.tensor_reduce(acols[:, 0:1], a2dr[:, :], AX.X, ALU.max).then_inc(sv.h, 1)
            sv.v += 1
        blk.vector(_)
        AM1 = sv.v

        def _(v):
            v.wait_ge(sv.h, AM1)
            v.tensor_copy(alpha2b[:, 0:1], acols[0:64, 0:1]).then_inc(sv.h, 1)
            sv.v += 1
        blk.vector(_)
        AM2 = sv.v

        def _(t):
            t.wait_ge(sv.h, AM2)
            t.transpose(pbt[0:1, 256:320], alpha2b[0:64, 0:1],
                        ident_sb[0:64, 0:64]).then_inc(sm.h, 1)
            sm.v += 1
        blk.tensor(_)
        AM3 = sm.v

        def _(v):
            v.wait_ge(sm.h, AM3)
            v.tensor_reduce(arow1[0:1, 0:1], pbt[0:1, 256:320], AX.X,
                            ALU.max).then_inc(sv.h, 1)
            sv.v += 1
            v.wait_ge(sv.h, sv.v)
            v.tensor_single_scalar(arow1[0:1, 1:2], arow1[0:1, 0:1], -1.0,
                                   ALU.mult).then_inc(sv.h, 1)
            sv.v += 1
            v.wait_ge(sv.h, sv.v)
            v.tensor_copy(alpha2b[0:1, 1:2], arow1[0:1, 1:2]).then_inc(sv.h, 1)
            sv.v += 1
        blk.vector(_)
        AM4 = sv.v

        def _(t):
            t.wait_ge(sv.h, AM4)
            t.matmul(pb[6][0:64, 64:65], ones_sb[0:1, 0:64], alpha2b[0:1, 1:2],
                     start=True, stop=True).then_inc(sm.h, 1)
            sm.v += 1
        blk.tensor(_)
        AM5 = sm.v

        def _(v):
            v.wait_ge(sm.h, AM5)
            v.tensor_copy(acols[:, 1:2], pb[6][0:64, 64:65]).then_inc(sv.h, 1)
            sv.v += 1
        blk.vector(_)
        AM6 = sv.v

        def _(s):
            s.wait_ge(sv.h, AM6)
            s.activation(alpha2[:, :], a2dr[:, :], AF.Exp, bias=acols[:, 1:2],
                         accum_out=acols[:, 2:3]).then_inc(sa.h, 1)
            sa.v += 1
        blk.scalar(_)
        AM7 = sa.v

        def _(v):
            v.wait_ge(sa.h, AM7)
            v.tensor_copy(alpha2b[:, 2:3], acols[0:64, 2:3]).then_inc(sv.h, 1)
            sv.v += 1
        blk.vector(_)
        AM8 = sv.v

        def _(t):
            t.wait_ge(sv.h, AM8)
            t.transpose(pbt[0:1, 320:384], alpha2b[0:64, 2:3],
                        ident_sb[0:64, 0:64]).then_inc(sm.h, 1)
            sm.v += 1
        blk.tensor(_)
        AM9 = sm.v

        def _(v):
            v.wait_ge(sm.h, AM9)
            v.tensor_reduce(arow1[0:1, 2:3], pbt[0:1, 320:384], AX.X,
                            ALU.add).then_inc(sv.h, 1)
            sv.v += 1
            v.wait_ge(sv.h, sv.v)
            v.reciprocal(arow1[0:1, 3:4], arow1[0:1, 2:3]).then_inc(sv.h, 1)
            sv.v += 1
            v.wait_ge(sv.h, sv.v)
            v.tensor_copy(alpha2b[0:1, 3:4], arow1[0:1, 3:4]).then_inc(sv.h, 1)
            sv.v += 1
        blk.vector(_)
        AM10 = sv.v

        def _(t):
            t.wait_ge(sv.h, AM10)
            t.matmul(pb[6][0:64, 66:67], ones_sb[0:1, 0:64], alpha2b[0:1, 3:4],
                     start=True, stop=True).then_inc(sm.h, 1)
            sm.v += 1
        blk.tensor(_)
        AM11 = sm.v

        def _(v):
            v.wait_ge(sm.h, AM11)
            v.tensor_copy(acols[:, 3:4], pb[6][0:64, 66:67]).then_inc(sv.h, 1)
            sv.v += 1
            v.wait_ge(sv.h, sv.v)
            v.tensor_single_scalar(alpha2b[:, 4:68], alpha2[:, :],
                                   acols[:, 3:4], ALU.mult).then_inc(sv.h, 1)
            sv.v += 1
        blk.vector(_)
        AM12 = sv.v

        def _(t):
            t.wait_ge(sv.h, AM12)
            t.matmul(pb[6][0:32, 128:192], sela_sb[:, :], alpha2b[:, 4:68],
                     start=True, stop=True).then_inc(sm.h, 1)
            sm.v += 1
        blk.tensor(_)
        AM13 = sm.v

        def _(v):
            v.wait_ge(sm.h, AM13)
            v.tensor_copy(selout[:, :], pb[6][0:32, 128:192]).then_inc(sv.h, 1)
            sv.v += 1
        blk.vector(_)
        AM14 = sv.v

        def _(sync):
            sync.wait_ge(sv.h, AM14)
            sync.dma_start(out=d_alph, in_=selout[:, :]).then_inc(sst.h, 16)
            sst.v += 16
            sync.wait_ge(sst.h, sst.v)
            sync.dma_start(out=alpharow[0:1, :],
                           in_=d_alph.rearrange("a b -> (a b)").unsqueeze(0)).then_inc(sx2[1].h, 16)
            sx2[1].v += 16
        blk.sync(_)
        ALPHA_LD = sx2[1].v

    ln1T = at("ln1T", [128, NDC, TH], BF16, offset=32768)
    x0s5 = [at("x0s5", [128, NDC, 512], F32, offset=98304)] * 2
    xl5 = at("xl5", [128, NDC, 512], F32, offset=131072)
    sg5 = at("sg5", [128, NDC, 512], F32, offset=147456)
    sw5 = scr("sw5", [128, NDC, 512], F32, 0)
    x1b5 = at("x1b5", [128, NDC, 512], F32, offset=114688)
    x1bb = at("x1bb", [128, NDC, 512], BF16, offset=147456)
    sq15 = at("sq15", [128, NDC, 512], BF16, offset=155648)
    alpbr = scr("alpbr", [1, 2 * 512], BF16, 41216)

    if PHASES >= 5:
        sxw = S(nc, "sxw")
        sx5 = [S(nc, "sx50"), S(nc, "sx51")]
        GBASE_V, GBASE_A, GBASE_M = sv.v, sa.v, sm.v
        for jt in range(4):
            TV, TA, TM = sv.v, sa.v, sm.v

            def _(sync, jt=jt, TV=TV):
                if jt >= 1:
                    sync.wait_ge(sv.h, TV)
                sync.dma_start(
                    out=x0s5[0][:, :, :],
                    in_=x0T.rearrange("(c p) t -> p c t", p=128)[:, :, 512 * jt:512 * (jt + 1)],
                ).then_inc(sx5[0].h, 16)
                sx5[0].v += 16
            blk.sync(_)

            def _(t, jt=jt, TV=TV, TA=TA):
                if jt == 0:
                    t.wait_ge(sx2[1].h, sx2[1].v)
                if jt >= 1:
                    t.wait_ge(sv.h, TV)
                t.matmul(pb[5][:, 0:512], ones_sb[0:1, :],
                         alpharow[0:1, 512 * jt:512 * (jt + 1)],
                         start=True, stop=True).then_inc(sm.h, 1)
                sm.v += 1
            blk.tensor(_)
            ALB = sm.v

            def _(v, jt=jt, ALB=ALB, TA=TA):
                v.wait_ge(sm.h, ALB)
                for ch in range(NDC):
                    ins = v.tensor_tensor(xl5[:, ch, :],
                                          valT[:, ch, 512 * jt:512 * (jt + 1)],
                                          pb[5][:, 0:512], ALU.mult)
                ins.then_inc(sv.h, 1)
                sv.v += 1
            blk.vector(_)
            G1 = sv.v

            def _(s, jt=jt, G1=G1):
                s.wait_ge(sv.h, G1)
                for ch in range(NDC):
                    ins = s.activation(sg5[:, ch, :], xl5[:, ch, :], AF.Sigmoid,
                                       scale=-1.0)
                ins.then_inc(sa.h, 1)
                sa.v += 1
                s.wait_ge(sa.h, sa.v)
                for ch in range(NDC):
                    ins = s.activation(sw5[:, ch, :], sg5[:, ch, :], AF.Square)
                ins.then_inc(sa.h, 1)
                sa.v += 1
            blk.scalar(_)
            G3 = sa.v

            def _(v, jt=jt, G3=G3):
                v.wait_ge(sa.h, G3)
                for ch in range(NDC):
                    ins = v.tensor_tensor(sg5[:, ch, :], xl5[:, ch, :],
                                          encT[:, ch, 512 * jt:512 * (jt + 1)],
                                          ALU.subtract)
                ins.then_inc(sv.h, 1)
                sv.v += 1
                v.wait_ge(sv.h, sv.v)
                for ch in range(NDC):
                    ins = v.tensor_tensor(xl5[:, ch, :], sw5[:, ch, :],
                                          sg5[:, ch, :], ALU.mult)
                ins.then_inc(sv.h, 1)
                sv.v += 1
                v.wait_ge(sv.h, sv.v)
                v.wait_ge(sx5[0].h, 16 * (jt + 1))
                for ch in range(NDC):
                    ins = v.scalar_tensor_tensor(sg5[:, ch, :], xl5[:, ch, :], 2.0,
                                                 x0s5[0][:, ch, :],
                                                 ALU.mult, ALU.add)
                ins.then_inc(sv.h, 1)
                sv.v += 1
                v.wait_ge(sv.h, sv.v)
                v.wait_ge(sxw.h, 16 * jt)
                for ch in range(NDC):
                    ins = v.scalar_tensor_tensor(x1b5[:, ch, :],
                                                 encT[:, ch, 512 * jt:512 * (jt + 1)],
                                                 2.0, sg5[:, ch, :],
                                                 ALU.mult, ALU.add)
                ins.then_inc(sv.h, 1)
                sv.v += 1
            blk.vector(_)
            G7 = sv.v

            def _(sync, jt=jt, G7=G7):
                sync.wait_ge(sv.h, G7)
                sync.dma_start(out=d_x1.rearrange("(c p) t -> p c t", p=128)
                               [:, :, 512 * jt:512 * (jt + 1)],
                               in_=x1b5[:, :, :]).then_inc(sxw.h, 16)
                sxw.v += 16
            blk.sync(_)

            def _(s, jt=jt, G7=G7):
                s.wait_ge(sv.h, G7)
                s.activation(x1bb[:, :, :], x1b5[:, :, :], AF.Copy)
                s.activation(sq15[:, :, :], x1b5[:, :, :], AF.Square).then_inc(sa.h, 1)
                sa.v += 1
            blk.scalar(_)
            L1A = sa.v

            def _(t, jt=jt, L1A=L1A):
                t.wait_ge(sa.h, L1A)
                onecol = ones_sb[:, 0:1]
                for c in range(NDC):
                    t.matmul(pb[0][0:1, 0:512], onecol, x1bb[:, c, :],
                             start=(c == 0), stop=(c == NDC - 1))
                for c in range(NDC):
                    mm = t.matmul(pb[1][0:1, 0:512], onecol, sq15[:, c, :],
                                  start=(c == 0), stop=(c == NDC - 1))
                mm.then_inc(sm.h, 1)
                sm.v += 1
            blk.tensor(_)
            L1M = sm.v

            def _(v, jt=jt, L1M=L1M):
                r = statrows
                inv = 1.0 / D
                v.wait_ge(sm.h, L1M)
                v.tensor_single_scalar(r[0:1, 0:512], pb[0][0:1, 0:512], -inv, ALU.mult)
                v.tensor_single_scalar(r[0:1, 512:1024], pb[1][0:1, 0:512], inv,
                                       ALU.mult).then_inc(sv.h, 1)
                sv.v += 1
                v.wait_ge(sv.h, sv.v)
                v.tensor_tensor(r[0:1, 1024:1536], r[0:1, 0:512], r[0:1, 0:512],
                                ALU.mult).then_inc(sv.h, 1)
                sv.v += 1
                v.wait_ge(sv.h, sv.v)
                v.tensor_tensor(r[0:1, 1024:1536], r[0:1, 512:1024], r[0:1, 1024:1536],
                                ALU.subtract).then_inc(sv.h, 1)
                sv.v += 1
                v.wait_ge(sv.h, sv.v)
                v.tensor_single_scalar(r[0:1, 1024:1536], r[0:1, 1024:1536], 1e-5,
                                       ALU.add).then_inc(sv.h, 1)
                sv.v += 1
            blk.vector(_)
            L1V = sv.v

            def _(s, jt=jt, L1V=L1V):
                s.wait_ge(sv.h, L1V)
                s.activation(statrows[0:1, 1536:2 * 1024], statrows[0:1, 1024:1536],
                             AF.Sqrt).then_inc(sa.h, 1)
                sa.v += 1
            blk.scalar(_)
            L1S = sa.v

            def _(v, jt=jt, L1S=L1S):
                r = statrows
                v.wait_ge(sa.h, L1S)
                v.reciprocal(r[0:1, 1024:1536], r[0:1, 1536:2048]).then_inc(sv.h, 1)
                sv.v += 1
                v.wait_ge(sv.h, sv.v)
                v.tensor_copy(alpbr[0:1, 0:512], r[0:1, 0:512])
                v.tensor_copy(alpbr[0:1, 512:1024], r[0:1, 1024:1536]).then_inc(sv.h, 1)
                sv.v += 1
            blk.vector(_)
            L1R = sv.v

            def _(t, jt=jt, L1R=L1R):
                t.wait_ge(sv.h, L1R)
                t.matmul(pb[2][:, 0:512], ones_sb[0:1, :], alpbr[0:1, 0:512],
                         start=True, stop=True)
                t.matmul(pb[3][:, 0:512], ones_sb[0:1, :], alpbr[0:1, 512:1024],
                         start=True, stop=True).then_inc(sm.h, 1)
                sm.v += 1
            blk.tensor(_)
            L1B = sm.v

            def _(v, jt=jt, L1B=L1B):
                v.wait_ge(sm.h, L1B)
                for c in range(NDC):
                    ins = v.tensor_tensor(ln1T[:, c, 512 * jt:512 * (jt + 1)],
                                          x1b5[:, c, :], pb[2][:, 0:512], ALU.add)
                ins.then_inc(sv.h, 1)
                sv.v += 1
                v.wait_ge(sv.h, sv.v)
                for c in range(NDC):
                    ins = v.tensor_tensor(ln1T[:, c, 512 * jt:512 * (jt + 1)],
                                          ln1T[:, c, 512 * jt:512 * (jt + 1)],
                                          pb[3][:, 0:512], ALU.mult)
                ins.then_inc(sv.h, 1)
                sv.v += 1
            blk.vector(_)

    wst6 = [at(f"wst6{i}", [128, NDC // 2, D], F32, offset=98304 + 16384 * i)
            for i in range(2)]
    gs6 = at("gs6", [128, 2], F32, offset=131072)
    onef_c = at("onef_c", [128, 1], F32, offset=131136)
    onef_r = at("onef_r", [1, 128], F32, offset=131200)
    grow = at("grow", [1, 8], F32, offset=131776)
    scol6 = at("scol6", [128, 2], F32, offset=132352)
    qge = scr("qge", [128, 4, D], BF16, 0)
    qle = scr("qle", [128, 4, D], BF16, 8192)
    wqq = scr("wqq", [128, 4, D], BF16, 16384)
    wqst = at("wqst", [128, NDC, 512], BF16, offset=132928)

    wst6b = at("wst6b", [128, NDC, 512], F32, offset=114688)
    if PHASES >= 6:
        P6V, P6A, P6M = sv.v, sa.v, sm.v

        def _(sync):
            sync.wait_ge(sv.h, P6V)
            sync.wait_ge(sxw.h, sxw.v)
            sync.dma_start(out=wst6[0][:, :, :],
                           in_=w1s.rearrange("(c p) n -> p c n", p=128)).then_inc(sx2[0].h, 16)
            sx2[0].v += 16
            sync.dma_start(out=wst6b[:, :, :],
                           in_=w2s.rearrange("(c p) n -> p c n", p=128)).then_inc(sx2[0].h, 16)
            sx2[0].v += 16
        blk.sync(_)
        W6L = sx2[0].v

        def _(v):
            v.wait_ge(sx2[0].h, W6L)
            v.memset(onef_c[:, 0:1], 1.0)
            v.memset(onef_r[0:1, :], 1.0)
            v.tensor_reduce(gs6[:, 0:1], wst6[0][:, :, :], AX.XY, ALU.add,
                            apply_absolute_value=True)
            v.tensor_reduce(gs6[:, 1:2], wst6b[:, :, :], AX.XY, ALU.add,
                            apply_absolute_value=True).then_inc(sv.h, 1)
            sv.v += 1
        blk.vector(_)
        W6R = sv.v

        def _(t):
            t.wait_ge(sv.h, W6R)
            t.matmul(pb[5][0:1, 0:2], onef_c[:, 0:1], gs6[:, 0:2],
                     start=True, stop=True).then_inc(sm.h, 1)
            sm.v += 1
        blk.tensor(_)
        W6M = sm.v

        def _(v):
            v.wait_ge(sm.h, W6M)
            v.tensor_copy(grow[0:1, 0:2], pb[5][0:1, 0:2]).then_inc(sv.h, 1)
            sv.v += 1
        blk.vector(_)
        W6C = sv.v

        def _(sync):
            sync.wait_ge(sv.h, W6C)
            sync.dma_start(out=cc_gam_i, in_=grow[0:1, 0:2]).then_inc(sci.h, 16)
            sci.v += 16
        blk.sync(_)
        GAM_CI = sci.v

        def _(g):
            g.wait_ge(sci.h, GAM_CI)
            g.collective_compute(
                "AllReduce", ALU.add, replica_groups=[list(range(NCORES))],
                ins=[cc_gam_i.opt()], outs=[cc_gam_o.opt()],
            ).then_inc(scc.h)
            scc.v += 1
        blk.gpsimd(_)
        GAM_AR = scc.v

        def _(sync):
            sync.wait_ge(scc.h, GAM_AR)
            sync.dma_start(out=grow[0:1, 2:4], in_=cc_gam_o).then_inc(sx2[1].h, 16)
            sx2[1].v += 16
        blk.sync(_)
        GAM_LD = sx2[1].v

        def _(v):
            v.wait_ge(sx2[1].h, GAM_LD)
            v.tensor_single_scalar(grow[0:1, 4:6], grow[0:1, 2:4],
                                   1.0 / (512 * NCORES * D), ALU.mult).then_inc(sv.h, 1)
            sv.v += 1
            v.wait_ge(sv.h, sv.v)
            v.tensor_single_scalar(grow[0:1, 4:6], grow[0:1, 4:6], 1e-5,
                                   ALU.add).then_inc(sv.h, 1)
            sv.v += 1
            v.wait_ge(sv.h, sv.v)
            v.reciprocal(grow[0:1, 6:8], grow[0:1, 4:6]).then_inc(sv.h, 1)
            sv.v += 1
        blk.vector(_)
        W6G = sv.v

        def _(t):
            t.wait_ge(sv.h, W6G)
            t.matmul(pb[5][:, 4:6], onef_r[0:1, :], grow[0:1, 6:8],
                     start=True, stop=True).then_inc(sm.h, 1)
            sm.v += 1
        blk.tensor(_)
        W6B = sm.v

        def _(v):
            v.wait_ge(sm.h, W6B)
            v.tensor_copy(scol6[:, 0:2], pb[5][:, 4:6]).then_inc(sv.h, 1)
            sv.v += 1
        blk.vector(_)
        W6S = sv.v

        wqq2 = scr("wqq2", [128, NDC, 512], BF16, 16384)
        wqst2 = at("wqst2", [128, 4, D], BF16, offset=132928)

        for wi in range(2):
            QB = sv.v
            src = wst6[0] if wi == 0 else wst6b
            dstq = wqq if wi == 0 else wqq2

            def _(v, wi=wi, src=src, dstq=dstq, QB=QB, PM=sm.v):
                if wi == 0:
                    v.wait_ge(sv.h, W6S)
                else:
                    v.wait_ge(sv.h, QB)
                    v.wait_ge(sm.h, PM)
                v.tensor_scalar(qge[:, :, :].rearrange("p a b -> p (a b)"),
                                src[:, :, :].rearrange("p a b -> p (a b)"),
                                scol6[:, wi:wi + 1], 0.5, ALU.mult, ALU.is_gt)
                v.tensor_scalar(qle[:, :, :].rearrange("p a b -> p (a b)"),
                                src[:, :, :].rearrange("p a b -> p (a b)"),
                                scol6[:, wi:wi + 1], -0.5, ALU.mult,
                                ALU.is_lt).then_inc(sv.h, 1)
                sv.v += 1
                v.wait_ge(sv.h, sv.v)
                v.tensor_tensor(dstq[:, :, :].rearrange("p a b -> p (a b)"),
                                qge[:, :, :].rearrange("p a b -> p (a b)"),
                                qle[:, :, :].rearrange("p a b -> p (a b)"),
                                ALU.subtract).then_inc(sv.h, 1)
                sv.v += 1
            blk.vector(_)
            QV = sv.v
            TBASE_M, TBASE_A = sm.v, sa.v
            blocks = []
            if wi == 0:
                for c in range(4):
                    for d8 in range(NDC):
                        blocks.append((wqq[:, c, 128 * d8:128 * (d8 + 1)],
                                       wqst[:, d8, 128 * c:128 * (c + 1)]))
            else:
                for c in range(NDC):
                    for f8 in range(4):
                        blocks.append((wqq2[:, c, 128 * f8:128 * (f8 + 1)],
                                       wqst2[:, f8, 128 * c:128 * (c + 1)]))

            def _(t, QV=QV, TBASE_A=TBASE_A, blocks=blocks):
                t.wait_ge(sv.h, QV)
                for bi, (inap, outap) in enumerate(blocks):
                    t.wait_ge(sa.h, TBASE_A + bi)
                    t.transpose(pbt[:, 128 * (bi % 2):128 * (bi % 2) + 128],
                                inap, ident_sb[:, :]).then_inc(sm.h, 1)
                    sm.v += 1
            blk.tensor(_)

            def _(s, TBASE_M=TBASE_M, blocks=blocks, wi=wi, CIV=sci.v):
                for bi, (inap, outap) in enumerate(blocks):
                    if bi == 0 and wi == 1:
                        s.wait_ge(sci.h, CIV)
                    s.wait_ge(sm.h, TBASE_M + bi + 1)
                    s.activation(outap, pbt[:, 128 * (bi % 2):128 * (bi % 2) + 128],
                                 AF.Copy).then_inc(sa.h, 1)
                    sa.v += 1
            blk.scalar(_)

            WQA = sa.v

            def _(sync, wi=wi, WQA=WQA):
                sync.wait_ge(sa.h, WQA)
                if wi == 0:
                    sync.dma_start(
                        out=cc_wq_i[0:1024, :].rearrange("(c p) n -> p c n", p=128),
                        in_=wqst[:, :, :]).then_inc(sci.h, 16)
                else:
                    sync.dma_start(
                        out=cc_wq_i.rearrange("a b -> (a b)")[524288:1048576]
                        .rearrange("(c p n) -> p c n", p=128, n=1024),
                        in_=wqst2[:, :, :]).then_inc(sci.h, 16)
                sci.v += 16
            blk.sync(_)

        WQ_CI = sci.v

        def _(g):
            g.wait_ge(sci.h, WQ_CI)
            g.collective_compute(
                "AllGather", ALU.bypass, replica_groups=[list(range(NCORES))],
                ins=[cc_wq_i.opt()], outs=[cc_wq_o.opt()],
            ).then_inc(scc.h)
            scc.v += 1
        blk.gpsimd(_)
        WQ_AG = scc.v

        wq1r = at("wq1r", [128, 32, NDC, 128], BF16, offset=65536)
        wq2rA = at("wq2rA", [128, 16, NDC, 128], BF16, offset=131072)
        wq2rB = scr("wq2rB", [128, 16, NDC, 128], BF16, 0)
        ccwq_flat = cc_wq_o.rearrange("a b -> (a b)")

        def _(sync):
            sync.wait_ge(scc.h, WQ_AG)
            for r in range(NCORES):
                w1v = ccwq_flat[r * 2048 * 512: r * 2048 * 512 + 524288] \
                    .rearrange("(dch p fl col) -> p fl dch col",
                               p=128, fl=4, col=128)
                sync.dma_start(out=wq1r[:, 4 * r:4 * r + 4, :, :],
                               in_=w1v).then_inc(sg.h, 16)
                w2v = ccwq_flat[r * 2048 * 512 + 524288: (r + 1) * 2048 * 512] \
                    .rearrange("(fl p d2ch col) -> p fl d2ch col",
                               p=128, fl=4, col=128)
                w2dst = wq2rA if r < 4 else wq2rB
                sync.dma_start(out=w2dst[:, (4 * r) % 16:(4 * r) % 16 + 4, :, :],
                               in_=w2v).then_inc(sg.h, 16)
                sg.v += 32
        blk.sync(_)
        WQR_LD = sg.v

        hT6 = at("hT6", [128, 32, 512], BF16, offset=0)
        x1pp = at("x1pp", [128, 2, 512], F32, offset=199200)
        ybuf = at("ybuf", [128, 2, 512], F32, offset=203296)
        sx6 = S(nc, "sx6")
        d_x1v = d_x1.rearrange("(c p) t -> p c t", p=128)
        y_v = y.rearrange("(c p) t -> p c t", p=128)

        for jt in range(4):
            TBM, TBA, EVJ, STJ, LDJ = sm.v, sa.v, sv.v, sst.v, sx6.v

            def _(t, jt=jt, TBM=TBM, TBA=TBA, EVJ=EVJ):
                if jt == 0:
                    t.wait_ge(sg.h, WQR_LD)
                    t.wait_ge(sv.h, P6V)
                for fc in range(32):
                    if fc >= 2:
                        t.wait_ge(sa.h, TBA + fc - 1)
                    elif fc == 0:
                        t.wait_ge(sa.h, TBA)
                    for dch in range(NDC):
                        mm = t.matmul(pb[4 + fc % 2][:, 0:512],
                                      wq1r[:, fc, dch, :],
                                      ln1T[:, dch, 512 * jt:512 * (jt + 1)],
                                      start=(dch == 0), stop=(dch == NDC - 1))
                    mm.then_inc(sm.h, 1)
                    sm.v += 1
                t.wait_ge(sa.h, TBA + 32)
                for d2 in range(NDC):
                    if d2 >= 4:
                        t.wait_ge(sv.h, EVJ + d2 - 3)
                    elif d2 == 0:
                        t.wait_ge(sv.h, EVJ)
                    for fc in range(32):
                        w2t = (wq2rA[:, fc, d2, :] if fc < 16
                               else wq2rB[:, fc - 16, d2, :])
                        mm = t.matmul(pb[d2 % 4][:, 0:512], w2t, hT6[:, fc, :],
                                      start=(fc == 0), stop=(fc == 31))
                    mm.then_inc(sm.h, 1)
                    sm.v += 1
            blk.tensor(_)

            def _(s, jt=jt, TBM=TBM):
                for fc in range(32):
                    s.wait_ge(sm.h, TBM + fc + 1)
                    s.activation(hT6[:, fc, :], pb[4 + fc % 2][:, 0:512],
                                 AF.Relu, bias=b1_sb[:, fc:fc + 1]).then_inc(sa.h, 1)
                    sa.v += 1
            blk.scalar(_)

            def _(v, jt=jt, TBM=TBM, EVJ=EVJ, STJ=STJ, LDJ=LDJ):
                for d2 in range(NDC):
                    v.wait_ge(sm.h, TBM + 32 + d2 + 1)
                    v.wait_ge(sx6.h, LDJ + 16 * (d2 + 1))
                    if d2 >= 2:
                        v.wait_ge(sst.h, STJ + 16 * (d2 - 1))
                    v.scalar_tensor_tensor(
                        ybuf[:, d2 % 2, :], pb[d2 % 4][:, 0:512],
                        b2_sb[:, d2:d2 + 1], x1pp[:, d2 % 2, :],
                        ALU.add, ALU.add).then_inc(sv.h, 1)
                    sv.v += 1
            blk.vector(_)

            def _(sync, jt=jt, EVJ=EVJ):
                for d2 in range(NDC):
                    if d2 >= 2:
                        sync.wait_ge(sv.h, EVJ + d2 - 1)
                    sync.dma_start(
                        out=x1pp[:, d2 % 2:d2 % 2 + 1, :],
                        in_=d_x1v[:, d2:d2 + 1, 512 * jt:512 * (jt + 1)],
                    ).then_inc(sx6.h, 16)
                    sx6.v += 16
                    if d2 >= 2:
                        sync.wait_ge(sv.h, EVJ + d2 - 1)
                        sync.dma_start(
                            out=y_v[:, d2 - 2:d2 - 1, 512 * jt:512 * (jt + 1)],
                            in_=ybuf[:, d2 % 2:d2 % 2 + 1, :],
                        ).then_inc(sst.h, 16)
                        sst.v += 16
                for d2 in (NDC - 2, NDC - 1):
                    sync.wait_ge(sv.h, EVJ + d2 + 1)
                    sync.dma_start(
                        out=y_v[:, d2:d2 + 1, 512 * jt:512 * (jt + 1)],
                        in_=ybuf[:, d2 % 2:d2 % 2 + 1, :],
                    ).then_inc(sst.h, 16)
                    sst.v += 16
            blk.sync(_)

    def _(sync):
        for _sem in S.registry:
            if _sem.v > 0:
                sync.wait_ge(_sem.h, _sem.v)
        if DEBUG:
            if PHASES < 3:
                sync.dma_start(out=dbg["dbg_ln0"][:, 0:NDC, 0:TH],
                               in_=ln0h[:, :, :]).then_inc(sst.h, 16)
                sync.dma_start(out=dbg["dbg_q"], in_=qTs[:, :, :]).then_inc(sst.h, 16)
                sync.dma_start(out=dbg["dbg_k"], in_=kTs[:, :, :]).then_inc(sst.h, 16)
                sync.dma_start(out=dbg["dbg_v"],
                               in_=vtok.rearrange("p a b -> p (a b)")).then_inc(sst.h, 16)
                sst.v += 64
            if PHASES == 2:
                sync.dma_start(out=dbg["dbg_ql"],
                               in_=ql_bf.rearrange("p a b -> p (a b)")).then_inc(sst.h, 16)
                sync.dma_start(out=dbg["dbg_kl"],
                               in_=kl_bf.rearrange("p a b -> p (a b)")).then_inc(sst.h, 16)
                sync.dma_start(out=dbg["dbg_a2"],
                               in_=A2bf4.rearrange("p a b -> p (a b)")).then_inc(sst.h, 16)
                sync.dma_start(out=dbg["dbg_f"],
                               in_=F4bf.rearrange("p a b -> p (a b)")).then_inc(sst.h, 16)
                sync.dma_start(out=dbg["dbg_g"],
                               in_=G4bf.rearrange("p a b -> p (a b)")).then_inc(sst.h, 16)
                sst.v += 80
            if PHASES >= 2:
                sync.dma_start(out=dbg["dbg_scale"][0:1, 0:6], in_=scrow[0:1, 0:6]).then_inc(sst.h, 16)
                sst.v += 16
            if PHASES in (2, 3, 4, 5):
                sync.dma_start(out=dbg["dbg_out"], in_=outTo[:, :, :]).then_inc(sst.h, 16)
                sst.v += 16
            if PHASES in (4, 5):
                sync.dma_start(out=dbg["dbg_enc"], in_=encT[:, :, :]).then_inc(sst.h, 16)
                sync.dma_start(out=dbg["dbg_val"], in_=valT[:, :, :]).then_inc(sst.h, 16)
                sync.dma_start(out=dbg["dbg_a2d"], in_=a2dr[:, :]).then_inc(sst.h, 16)
                sst.v += 48
            if PHASES >= 4:
                sync.dma_start(out=dbg["dbg_alpha"], in_=alpharow[0:1, :]).then_inc(sst.h, 16)
                sst.v += 16
            if PHASES >= 6 and "dbg_ln1" in dbg:
                sync.dma_start(out=dbg["dbg_ln1"], in_=ln1T[:, :, :]).then_inc(sst.h, 16)
                sync.dma_start(out=dbg["dbg_wq1"],
                               in_=wq1r.rearrange("p a b c -> p (a b c)")).then_inc(sst.h, 16)
                sync.dma_start(out=dbg["dbg_wq2a"],
                               in_=wq2rA.rearrange("p a b c -> p (a b c)")).then_inc(sst.h, 16)
                sync.dma_start(out=dbg["dbg_wq2b"],
                               in_=wq2rB.rearrange("p a b c -> p (a b c)")).then_inc(sst.h, 16)
                sync.dma_start(out=dbg["dbg_h"],
                               in_=hT6.rearrange("p a b -> p (a b)")).then_inc(sst.h, 16)
                sync.dma_start(out=dbg["dbg_ccwqi"], in_=cc_wq_i).then_inc(sst.h, 16)
                sync.dma_start(out=dbg["dbg_ccwqo"], in_=cc_wq_o).then_inc(sst.h, 16)
                sst.v += 112
        sync.wait_ge(sst.h, sst.v)
    blk.sync(_)

    blk.__exit__(None, None, None)
    return nc



def _band_matrices(conv_w, parity):
    heads = range(4 * parity, 4 * parity + 4)
    out = np.zeros((128, 4 * 7 * 128), np.float32)
    s_idx = np.arange(128)[:, None]
    p_idx = np.arange(128)[None, :]
    for jloc, h in enumerate(heads):
        cw = np.zeros(300, np.float32)
        cw[:33] = conv_w[h, 0, :, 0]
        jA = s_idx - 128 - p_idx + 16
        jB = s_idx - p_idx + 16
        jC = s_idx + 128 - p_idx + 16
        A = np.where((jA >= 0) & (jA < 33), cw[np.clip(jA, 0, 299)], np.float32(0))
        B = np.where((jB >= 0) & (jB < 33), cw[np.clip(jB, 0, 299)], np.float32(0))
        C = np.where((jC >= 0) & (jC < 33), cw[np.clip(jC, 0, 299)], np.float32(0))
        base = jloc * 7 * 128
        out[:, base: base + 128] = A
        out[:, base + 128: base + 256] = B
        out[:, base + 256: base + 384] = C
        if parity == 1:
            out[:, base + 384: base + 512] = A
            out[:, base + 512: base + 640] = C
        else:
            out[:, base + 640: base + 768] = A
            out[:, base + 768: base + 896] = C
    return out


def _outw_other(out_w, par):
    w = np.array(out_w, copy=True)
    w[256 * par: 256 * par + 256, :] = 0
    return w


def make_in_maps(inputs):
    x0 = inputs["L0_attn_dense"]
    qkv_w = inputs["L0_attn_qkv_w"]
    conv_w = inputs["L0_attn_conv_w"]
    in_maps = []
    iota64 = np.broadcast_to(np.arange(64, dtype=np.float32), (128, 64)).copy()
    ones_bf = np.ones((128, 128), ml_dtypes.bfloat16)
    ident_bf = np.eye(128).astype(ml_dtypes.bfloat16)
    ikc = np.zeros((128, 2 * 768), ml_dtypes.bfloat16)
    for k, val in enumerate([7.0, 15.0, 13.0]):
        for ch in range(2):
            for p in range(128):
                ikc[p, ch * 768 + k * 256 + ch * 128 + p] = val

    for c in range(NCORES):
        b, par = c // 2, c % 2
        perm = (np.r_[2048:4096, 0:2048] if par else np.r_[0:4096])
        x0T = np.ascontiguousarray(x0[b].T[:, perm])
        hsl = slice(256 * par, 256 * par + 256)
        qkvw_c = np.ascontiguousarray(
            np.concatenate([qkv_w[:, 0:512][:, hsl],
                            qkv_w[:, 512:1024][:, hsl],
                            qkv_w[:, 1024:1536][:, hsl]], axis=1))
        sel = np.zeros((64, 32), ml_dtypes.bfloat16)
        for i in range(32):
            sel[32 * par + i, i] = 1
        m = {
            "x0T": x0T,
            "qkvw": qkvw_c,
            "wv": inputs["L0_attn_wv_w"],
            "wvb": np.ascontiguousarray(inputs["L0_attn_wv_b"].reshape(NDC, 128).T),
            "wqk": np.ascontiguousarray(
                np.concatenate([inputs["L0_attn_wq"], inputs["L0_attn_wk"]], axis=1)),
            "outw_o": _outw_other(inputs["L0_attn_out_w"], par),
            "outw_m": np.ascontiguousarray(
                inputs["L0_attn_out_w"][256 * par: 256 * par + 256]),
            "outb": np.ascontiguousarray(inputs["L0_attn_out_b"].reshape(NDC, 128).T),
            "w1s": np.ascontiguousarray(inputs["L0_ffn_W1"][512 * c: 512 * c + 512]),
            "w2s": np.ascontiguousarray(inputs["L0_ffn_W2"][:, 512 * c: 512 * c + 512]),
            "b1c": np.ascontiguousarray(inputs["L0_ffn_b1"].reshape(32, 128).T),
            "b2c": np.ascontiguousarray(inputs["L0_ffn_b2"].reshape(NDC, 128).T),
            "adjr": np.ascontiguousarray(
                inputs["L0_attn_adj_rows"][EPC * c: EPC * (c + 1)].reshape(NET, 128).T),
            "adjc": np.ascontiguousarray(
                inputs["L0_attn_adj_cols"][EPC * c: EPC * (c + 1)].reshape(NET, 128).T),
            "adjv": np.ascontiguousarray(
                inputs["L0_attn_adj_vals"][EPC * c: EPC * (c + 1)].reshape(NET, 128).T),
            "bands": _band_matrices(conv_w, par),
            "iota64": iota64,
            "ones_bf": ones_bf,
            "ident_bf": ident_bf,
            "ikc": ikc,
            "selalpha": sel,
        }
        in_maps.append(m)
    return in_maps


def kernel(**inputs):
    inputs = {k: np.asarray(v) for k, v in inputs.items()}
    in_maps = make_in_maps(inputs)
    nc = build_program()
    res = run_bass_kernel_spmd(nc, in_maps, core_ids=list(range(NCORES)))
    out = np.empty((4, T, D), np.float32)
    for c in range(NCORES):
        b, par = c // 2, c % 2
        out[b, 2048 * par: 2048 * par + 2048, :] = np.asarray(res.results[c]["y"]).T
    return out


if __name__ == "__main__":
    nc = build_program()
    print("build ok, instructions:", nc.n_instructions())


# version 04
# speedup vs baseline: 1.0224x; 1.0154x over previous
import numpy as np
import ml_dtypes
import concourse.bass as bass
import concourse.mybir as mybir
from concourse.bass_core_rust import IndirectOffsetOnAxis
from concourse.bass_utils import run_bass_kernel_spmd

F32 = mybir.dt.float32
BF16 = mybir.dt.bfloat16
I32 = mybir.dt.int32
AF = mybir.ActivationFunctionType
ALU = mybir.AluOpType
AX = mybir.AxisListType

NCORES = 8
D = 1024
NDC = 8
T = 4096
TH = 2048
HN = 4
DH = 64
NL = 256
EPC = 16384
NET = 128
SLAB = 256
NSLAB = T // SLAB

PHASES = 6
DEBUG = False


class S:
    registry = []

    def __init__(self, nc, name):
        self.h = nc.alloc_semaphore(name)
        self.v = 0
        S.registry.append(self)


def build_program():
    S.registry = []
    nc = bass.Bass("TRN2", target_bir_lowering=False, debug=False,
                   num_devices=NCORES, num_swdge_queues=4)

    def inp(name, shape, dt=F32):
        return nc.dram_tensor(name, shape, dt, kind="ExternalInput").ap()

    x0T = inp("x0T", [D, T])
    qkvw = inp("qkvw", [D, 768])
    wv = inp("wv", [D, D])
    wvb = inp("wvb", [128, NDC])
    wqk = inp("wqk", [D, 512])
    outw_o = inp("outw_o", [512, D])
    outw_m = inp("outw_m", [256, D])
    outb = inp("outb", [128, NDC])
    w1s = inp("w1s", [512, D])
    w2s = inp("w2s", [D, 512])
    b1c = inp("b1c", [128, 32])
    b2c = inp("b2c", [128, NDC])
    adjr = inp("adjr", [128, NET], I32)
    adjc = inp("adjc", [128, NET], I32)
    adjv = inp("adjv", [128, NET])
    bands = inp("bands", [128, HN * 7 * 128])
    iota64 = inp("iota64", [128, 64])
    ones_bf = inp("ones_bf", [128, 128], BF16)
    ident_bf = inp("ident_bf", [128, 128], BF16)
    ikc = inp("ikc", [128, 2 * 768], BF16)
    selalpha = inp("selalpha", [64, 32], BF16)

    y = nc.dram_tensor("y", [D, TH], F32, kind="ExternalOutput").ap()
    dbg = {}

    def dbgout(name, shape, dt):
        if DEBUG:
            dbg[name] = nc.dram_tensor(name, shape, dt, kind="ExternalOutput").ap()

    dbgout("dbg_ln0", [128, NDC, T], BF16)
    dbgout("dbg_q", [128, 2, T], BF16)
    dbgout("dbg_k", [128, 2, T], BF16)
    dbgout("dbg_v", [128, 32 * 256], BF16)
    dbgout("dbg_ql", [128, 512], BF16)
    dbgout("dbg_kl", [128, 512], BF16)
    dbgout("dbg_a2", [128, 2048], BF16)
    dbgout("dbg_scale", [1, 8], F32)
    dbgout("dbg_f", [128, 512], BF16)
    dbgout("dbg_g", [128, 512], BF16)
    dbgout("dbg_out", [128, 2, T], BF16)
    dbgout("dbg_enc", [128, NDC, TH], BF16)
    dbgout("dbg_val", [128, NDC, TH], BF16)
    dbgout("dbg_a2d", [64, 64], F32)
    dbgout("dbg_alpha", [1, TH], BF16)
    dbgout("dbg_ln1", [128, NDC, TH], BF16)
    dbgout("dbg_wq1", [128, 32 * NDC * 128], BF16)
    dbgout("dbg_wq2a", [128, 16 * NDC * 128], BF16)
    dbgout("dbg_wq2b", [128, 16 * NDC * 128], BF16)
    dbgout("dbg_h", [128, 32 * 256], BF16)
    dbgout("dbg_ccwqi", [2048, 512], BF16)
    dbgout("dbg_ccwqo", [NCORES * 2048, 512], BF16)

    def dram(name, shape, dt):
        return nc.dram_tensor(name, shape, dt, kind="Internal").ap()

    cc_scale_i = dram("cc_scale_i", [1, 2], F32)
    cc_scale_o = dram("cc_scale_o", [1, 2], F32)
    cc_gam_i = dram("cc_gam_i", [1, 2], F32)
    cc_gam_o = dram("cc_gam_o", [1, 2], F32)
    cc_out_i = dram("cc_out_i", [256, T], BF16)
    cc_out_o = dram("cc_out_o", [512, T], BF16)
    cc_xg_i = dram("cc_xg_i", [D, TH], BF16)
    cc_xg_o = dram("cc_xg_o", [4 * D, TH], BF16)
    cc_tab_i = dram("cc_tab_i", [TH, 512], BF16)
    cc_tab_o = dram("cc_tab_o", [NCORES * TH, 512], BF16)
    cc_wq_i = dram("cc_wq_i", [2048, 512], BF16)
    cc_wq_o = dram("cc_wq_o", [NCORES * 2048, 512], BF16)
    cc_a2d_i = dram("cc_a2d_i", [64, 64], F32)
    cc_a2d_o = dram("cc_a2d_o", [64, 64], F32)
    d_alph = dram("d_alph", [32, 64], BF16)
    d_x1 = nc.dram_tensor("d_x1", [D, TH], F32,
                          kind="ExternalOutput" if DEBUG else "Internal").ap()

    SB0 = 16512

    def at(name, shape, dtt, offset):
        return nc.alloc_sbuf_tensor_at(name, shape, dtt, offset=SB0 + offset)
    encT = at("encT", [128, NDC, TH], BF16, offset=0)
    ln0h = at("ln0h", [128, NDC, TH], BF16, offset=32768)
    valT = at("valT", [128, NDC, TH], BF16, offset=65536)
    qTs = at("qTs", [128, 2, T], BF16, offset=98304)
    kTs = at("kTs", [128, 2, T], BF16, offset=114688)
    vtok = at("vtok", [128, 32, 256], BF16, offset=131072)
    qkvw_sb = at("qkvw_sb", [128, NDC, 768], BF16, offset=147456)
    outTo = at("outTo", [128, 2, T], BF16, offset=147456)

    CW = 163840

    def atw(name, shape, dt):
        nonlocal CW
        t = at(name, shape, dt, offset=CW)
        per = int(np.prod(shape[1:])) * mybir.dt.size(dt)
        CW = (CW + per + 31) & ~31
        return t

    iota_sb = atw("iota_sb", [128, 64], F32)
    ones_sb = atw("ones_sb", [128, 128], BF16)
    ident_sb = atw("ident_sb", [128, 128], BF16)
    sela_sb = atw("sela_sb", [64, 32], BF16)
    wvb_sb = atw("wvb_sb", [128, NDC], F32)
    outb_sb = atw("outb_sb", [128, NDC], F32)
    b1_sb = atw("b1_sb", [128, 32], F32)
    b2_sb = atw("b2_sb", [128, NDC], F32)
    adjr_sb = atw("adjr_sb", [128, NET], I32)
    adjc_sb = atw("adjc_sb", [128, NET], I32)
    adjv_sb = atw("adjv_sb", [128, NET], F32)
    SCR = CW
    SCR_END = 212000
    assert SCR <= 168500, f"const region too big: {SCR}"

    def scr(name, shape, dt, off):
        per = int(np.prod(shape[1:])) * mybir.dt.size(dt)
        assert SCR + off + per <= SCR_END, (name, SCR + off + per - SCR_END)
        return at(name, shape, dt, offset=SCR + off)

    xslab = [scr(f"xslab{i}", [128, NDC, SLAB], F32, 8192 * i) for i in range(2)]
    xb_sb = scr("xb", [128, NDC, SLAB], BF16, 16384)
    sqb_sb = scr("sqb", [128, NDC, SLAB], BF16, 20480)
    lnoth = scr("lnoth", [128, NDC, SLAB], BF16, 24576)
    statrows = scr("statrows", [1, 2048], F32, 28672)
    browbf = scr("browbf", [1, 2 * SLAB], BF16, 35840)
    wstg = scr("wstg", [128, 2048], F32, 0)

    pb = [nc.alloc_psum_tensor(f"pb{i}", [128, 512], F32) for i in range(7)]
    pbt = nc.alloc_psum_tensor("pbt", [128, 1024], BF16)

    sld = S(nc, "sld")
    sx2 = [S(nc, "sx0"), S(nc, "sx1")]
    sci = S(nc, "sci")
    sv = S(nc, "sv")
    sa = S(nc, "sa")
    sp = S(nc, "sp")
    sm = S(nc, "sm")
    sst = S(nc, "sst")
    scc = S(nc, "scc")
    sg = S(nc, "sg")

    blk = nc.Block()
    blk.__enter__()

    def _(sync):
        for ap_in, sb in [(iota64, iota_sb), (ones_bf, ones_sb),
                          (ident_bf, ident_sb), (selalpha, sela_sb),
                          (adjr, adjr_sb), (adjc, adjc_sb), (adjv, adjv_sb),
                          (wvb, wvb_sb), (outb, outb_sb), (b1c, b1_sb),
                          (b2c, b2_sb)]:
            sync.dma_start(out=sb[:, :], in_=ap_in).then_inc(sld.h, 16)
            sld.v += 16
        for r in range(4):
            if r > 0:
                sync.wait_ge(sv.h, r)
            sync.dma_start(
                out=wstg[:, 0:1536].rearrange("p (c n) -> p c n", c=2),
                in_=qkvw.rearrange("(c p) n -> p c n", p=128)[:, 2 * r: 2 * r + 2, :],
            ).then_inc(sld.h, 16)
            sld.v += 16
    blk.sync(_)

    def _(v):
        base = 16 * 11
        for r in range(4):
            v.wait_ge(sld.h, base + 16 * (r + 1))
            wv2 = wstg[:, 0:1536].rearrange("p (c n) -> p c n", c=2)
            v.tensor_single_scalar(qkvw_sb[:, 2 * r: 2 * r + 2, 0:256],
                                   wv2[:, :, 0:256], 0.125, ALU.mult)
            v.tensor_copy(qkvw_sb[:, 2 * r: 2 * r + 2, 256:768],
                          wv2[:, :, 256:768]).then_inc(sv.h, 1)
            sv.v += 1
    blk.vector(_)
    QKVW_CAST = sv.v
    P0_LD = 16 * 12

    BA, BM, BV = sa.v, sm.v, sv.v

    def _(sync):
        for i in range(NSLAB):
            if i >= 2:
                sync.wait_ge(sv.h, BV + 8 * (i - 2) + 7)
            else:
                sync.wait_ge(sv.h, QKVW_CAST)
            sync.dma_start(
                out=xslab[i % 2][:, :, :],
                in_=x0T.rearrange("(c p) t -> p c t", p=128)[:, :, SLAB * i: SLAB * (i + 1)],
            ).then_inc(sx2[i % 2].h, 16)
            sx2[i % 2].v += 16
    blk.sync(_)

    def lnsrc(i, c):
        if i < 8:
            return ln0h[:, c, SLAB * i: SLAB * (i + 1)]
        return lnoth[:, c, :]

    def _(s):
        for i in range(NSLAB):
            xs = xslab[i % 2]
            s.wait_ge(sx2[i % 2].h, 16 * (i // 2 + 1))
            if i >= 1:
                s.wait_ge(sm.h, BM + 8 * (i - 1) + 1)
            s.activation(xb_sb[:, :, :], xs[:, :, :], AF.Copy)
            s.activation(sqb_sb[:, :, :], xs[:, :, :], AF.Square).then_inc(sa.h, 1)
            s.wait_ge(sv.h, BV + 8 * i + 4)
            s.activation(statrows[0:1, 4 * SLAB: 5 * SLAB],
                         statrows[0:1, 3 * SLAB: 4 * SLAB], AF.Sqrt).then_inc(sa.h, 1)
            sa.v += 2
            for g in range(6):
                s.wait_ge(sm.h, BM + 8 * i + 3 + g)
                if g < 2:
                    s.activation(qTs[:, g, SLAB * i: SLAB * (i + 1)],
                                 pb[2 + (g % 2)][:, 0:SLAB], AF.Copy).then_inc(sa.h, 1)
                elif g < 4:
                    s.activation(kTs[:, g - 2, SLAB * i: SLAB * (i + 1)],
                                 pb[2 + (g % 2)][:, 0:SLAB], AF.Copy).then_inc(sa.h, 1)
                else:
                    s.activation(vtok[:, 2 * i + (g - 4), :],
                                 pb[4 + (g - 4)][:, 0:256], AF.Copy).then_inc(sa.h, 1)
                sa.v += 1
    blk.scalar(_)

    def _(v):
        inv = 1.0 / D
        r = statrows
        SL = SLAB
        for i in range(NSLAB):
            v.wait_ge(sm.h, BM + 8 * i + 1)
            v.tensor_single_scalar(r[0:1, 0:SL], pb[0][0:1, 0:SL], -inv, ALU.mult)
            v.tensor_single_scalar(r[0:1, SL:2 * SL], pb[0][0:1, SL:2 * SL], inv,
                                   ALU.mult).then_inc(sv.h, 1)
            v.wait_ge(sv.h, BV + 8 * i + 1)
            v.tensor_tensor(r[0:1, 2 * SL:3 * SL], r[0:1, 0:SL], r[0:1, 0:SL],
                            ALU.mult).then_inc(sv.h, 1)
            v.wait_ge(sv.h, BV + 8 * i + 2)
            v.tensor_tensor(r[0:1, 3 * SL:4 * SL], r[0:1, SL:2 * SL],
                            r[0:1, 2 * SL:3 * SL], ALU.subtract).then_inc(sv.h, 1)
            v.wait_ge(sv.h, BV + 8 * i + 3)
            v.tensor_single_scalar(r[0:1, 3 * SL:4 * SL], r[0:1, 3 * SL:4 * SL],
                                   1e-5, ALU.add).then_inc(sv.h, 1)
            v.wait_ge(sa.h, BA + 8 * i + 2)
            v.reciprocal(r[0:1, 5 * SL:6 * SL], r[0:1, 4 * SL:5 * SL]).then_inc(sv.h, 1)
            v.wait_ge(sv.h, BV + 8 * i + 5)
            v.tensor_copy(browbf[0:1, 0:SL], r[0:1, 0:SL])
            v.tensor_copy(browbf[0:1, SL:2 * SL], r[0:1, 5 * SL:6 * SL]).then_inc(sv.h, 1)
            v.wait_ge(sm.h, BM + 8 * i + 2)
            for c in range(NDC):
                ins = v.tensor_tensor(lnsrc(i, c), xslab[i % 2][:, c, :],
                                      pb[1][:, 0:SL], ALU.add)
            ins.then_inc(sv.h, 1)
            v.wait_ge(sv.h, BV + 8 * i + 7)
            for c in range(NDC):
                ins = v.tensor_tensor(lnsrc(i, c), lnsrc(i, c),
                                      pb[1][:, SL:2 * SL], ALU.mult)
            ins.then_inc(sv.h, 1)
            sv.v += 8
    blk.vector(_)

    def _(t):
        onecol = ones_sb[:, 0:1]
        onerow = ones_sb[0:1, :]
        SL = SLAB
        for i in range(NSLAB):
            t.wait_ge(sa.h, BA + 8 * i + 1)
            if i >= 1:
                t.wait_ge(sv.h, BV + 8 * (i - 1) + 1)
            for c in range(NDC):
                t.matmul(pb[0][0:1, 0:SL], onecol, xb_sb[:, c, :],
                         start=(c == 0), stop=(c == NDC - 1))
            for c in range(NDC):
                mm = t.matmul(pb[0][0:1, SL:2 * SL], onecol, sqb_sb[:, c, :],
                              start=(c == 0), stop=(c == NDC - 1))
            mm.then_inc(sm.h, 1)
            sm.v += 1
            t.wait_ge(sv.h, BV + 8 * i + 6)
            t.matmul(pb[1][:, 0:SL], onerow, browbf[0:1, 0:SL], start=True, stop=True)
            t.matmul(pb[1][:, SL:2 * SL], onerow, browbf[0:1, SL:2 * SL],
                     start=True, stop=True).then_inc(sm.h, 1)
            sm.v += 1
            t.wait_ge(sv.h, BV + 8 * i + 8)
            for g in range(6):
                if g < 4:
                    bank = pb[2 + (g % 2)]
                    if g >= 2:
                        t.wait_ge(sa.h, BA + 8 * i + 3 + (g - 2))
                    elif i >= 1:
                        t.wait_ge(sa.h, BA + 8 * (i - 1) + 3 + (g + 2))
                    for c in range(NDC):
                        mm = t.matmul(bank[:, 0:SL],
                                      qkvw_sb[:, c, 128 * g: 128 * (g + 1)],
                                      lnsrc(i, c), start=(c == 0), stop=(c == NDC - 1))
                else:
                    bank = pb[4 + (g - 4)]
                    if i >= 1:
                        t.wait_ge(sa.h, BA + 8 * (i - 1) + 3 + g)
                    sub = g - 4
                    for c in range(NDC):
                        mm = t.matmul(bank[:, 0:256],
                                      lnsrc(i, c)[:, 128 * sub: 128 * (sub + 1)],
                                      qkvw_sb[:, c, 512:768],
                                      start=(c == 0), stop=(c == NDC - 1))
                mm.then_inc(sm.h, 1)
                sm.v += 1
    blk.tensor(_)

    P1_SA, P1_SM, P1_SV = sa.v, sm.v, sv.v

    qlf = scr("qlf", [128, 2, 256], F32, 0)
    klf = scr("klf", [128, 2, 256], F32, 2048)
    ql_bf = scr("ql_bf", [128, 2, 256], BF16, 4096)
    kl_bf = scr("kl_bf", [128, 2, 256], BF16, 5120)
    E2f = scr("E2f", [128, 2, 256], F32, 6144)
    nmcol = scr("nmcol", [128, 2], F32, 8192)
    sums2 = scr("sums2", [128, 2], F32, 8256)
    rs2 = scr("rs2", [128, 2], F32, 8320)
    RS = scr("RS", [128, 8], F32, 8384)
    RSbf = scr("RSbf", [128, 8], BF16, 8448)
    rm8 = scr("rm8", [128, 1], F32, 8480)
    rm8bf = scr("rm8bf", [128, 1], BF16, 8512)
    csr = scr("csr", [1, 8], F32, 8544)
    scrow = scr("scrow", [1, 8], F32, 45248)
    scbf = scr("scbf", [1, 2], BF16, 8608)
    rsc = scr("rsc", [128, 1], F32, 8640)
    r3row = scr("r3row", [1, 256], F32, 8704)
    r3bf = scr("r3bf", [1, 256], BF16, 9728)
    r3c = scr("r3c", [128, 2], F32, 10240)
    ikc_sb = scr("ikc_sb", [128, 2, 768], BF16, 10304)
    r1row = scr("r1row", [1, 512], F32, 10304)
    r1bf = scr("r1bf", [1, 512], BF16, 12352)
    A2bf4 = scr("A2bf4", [128, 2, 1024], BF16, 13376)
    A2T4 = scr("A2T4", [128, 2, 1024], BF16, 17472)
    zb = [scr(f"zb{i}", [128, 2, 256], BF16, 21568 + 1024 * i) for i in range(2)]
    zTb = [scr(f"zTb{i}", [128, 2, 256], BF16, 23616 + 1024 * i) for i in range(2)]
    xzTbf = scr("xzTbf", [128, 2, 256], BF16, 25664)
    w1bf = scr("w1bf", [128, 2, 256], BF16, 26688)
    w1Tbf = scr("w1Tbf", [128, 2, 256], BF16, 27712)
    w2bf = scr("w2bf", [128, 2, 256], BF16, 28736)
    w2Tbf = scr("w2Tbf", [128, 2, 256], BF16, 29760)
    w4bf = scr("w4bf", [128, 2, 256], BF16, 30784)
    w4Tbf = scr("w4Tbf", [128, 2, 256], BF16, 31808)
    F4bf = scr("F4bf", [128, 2, 256], BF16, 32832)
    G4bf = scr("G4bf", [128, 2, 256], BF16, 33856)
    E3b = scr("E3b", [128, 2, 256], BF16, 34880)
    E1b = scr("E1b", [128, 2, 512], BF16, 35904)
    band_sb = scr("band_sb", [128, HN * 7 * 128], BF16, 37952)

    if PHASES >= 2:
        def _(sync):
            sync.wait_ge(sv.h, P1_SV)
            sync.dma_start(out=ikc_sb[:, :, :],
                           in_=ikc.rearrange("p (c n) -> p c n", c=2)).then_inc(sld.h, 16)
            sld.v += 16
            for r4 in range(4):
                if r4 > 0:
                    sync.wait_ge(sv.h, P1_SV + r4)
                w = 896
                sync.dma_start(out=wstg[:, 0:w],
                               in_=bands[:, w * r4: w * (r4 + 1)]).then_inc(sld.h, 16)
                sld.v += 16
        blk.sync(_)
        BLD = sld.v

        def _(v):
            for r4 in range(4):
                v.wait_ge(sld.h, BLD - 16 * (3 - r4))
                w = 896
                ins = v.tensor_copy(band_sb[:, w * r4: w * (r4 + 1)], wstg[:, 0:w])
                ins.then_inc(sv.h, 1)
                sv.v += 1
        blk.vector(_)
        BCAST2 = sv.v

        def _(v):
            v.wait_ge(sa.h, P1_SA)
            v.wait_ge(sm.h, P1_SM)
            v.wait_ge(sv.h, BCAST2)
            for hc in range(2):
                v.tensor_reduce(qlf[:, hc, :],
                                qTs[:, hc, :].rearrange("p (l g) -> p l g", g=16),
                                AX.X, ALU.add)
                ins = v.tensor_reduce(klf[:, hc, :],
                                      kTs[:, hc, :].rearrange("p (l g) -> p l g", g=16),
                                      AX.X, ALU.add)
            ins.then_inc(sv.h, 1)
            sv.v += 1
            v.wait_ge(sv.h, sv.v)
            for hc in range(2):
                v.tensor_single_scalar(ql_bf[:, hc, :], qlf[:, hc, :], 1.0 / 16, ALU.mult)
                ins = v.tensor_single_scalar(kl_bf[:, hc, :], klf[:, hc, :], 1.0 / 16, ALU.mult)
            ins.then_inc(sv.h, 1)
            sv.v += 1
        blk.vector(_)
        LM = sv.v

        B2M, B2V, B2A = sm.v, sv.v, sa.v

        def _(t):
            for h in range(4):
                hc, off = h // 2, 64 * (h % 2)
                if h == 0:
                    t.wait_ge(sv.h, LM)
                else:
                    t.wait_ge(sa.h, B2A + 2 * (h - 1) + 1)
                for lc in range(2):
                    mm = t.matmul(pb[lc][:, 0:256],
                                  ql_bf[off:off + 64, hc, 128 * lc:128 * (lc + 1)],
                                  kl_bf[off:off + 64, hc, :],
                                  start=True, stop=True)
                mm.then_inc(sm.h, 1)
                sm.v += 1
                t.wait_ge(sv.h, B2V + 4 * h + 3)
                if h >= 1:
                    t.wait_ge(sv.h, B2V + 4 * (h - 1) + 4)
                for lc in range(2):
                    mm = t.matmul(pb[2][0:1, 0:256], ones_sb[:, 0:1],
                                  A2bf4[:, lc, 256 * h:256 * (h + 1)],
                                  start=(lc == 0), stop=(lc == 1))
                mm.then_inc(sm.h, 1)
                sm.v += 1
                if h >= 1:
                    t.wait_ge(sa.h, B2A + 2 * (h - 1) + 2)
                for lc in range(2):
                    for cc in range(2):
                        ins = t.transpose(
                            pbt[:, 256 * lc + 128 * cc: 256 * lc + 128 * (cc + 1)],
                            A2bf4[:, lc, 256 * h + 128 * cc: 256 * h + 128 * (cc + 1)],
                            ident_sb[:, :])
                ins.then_inc(sm.h, 1)
                sm.v += 1
        blk.tensor(_)

        def _(v):
            for h in range(4):
                v.wait_ge(sm.h, B2M + 3 * h + 1)
                for lc in range(2):
                    ins = v.tensor_reduce(nmcol[:, lc:lc + 1], pb[lc][:, 0:256], AX.X,
                                          ALU.max, negate=True)
                ins.then_inc(sv.h, 1)
                v.wait_ge(sa.h, B2A + 2 * h + 1)
                v.reciprocal(rs2[:, 0:2], sums2[:, 0:2]).then_inc(sv.h, 1)
                v.wait_ge(sv.h, B2V + 4 * h + 2)
                for lc in range(2):
                    ins = v.tensor_single_scalar(A2bf4[:, lc, 256 * h:256 * (h + 1)],
                                                 E2f[:, lc, :], rs2[:, lc:lc + 1], ALU.mult)
                ins.then_inc(sv.h, 1)
                v.wait_ge(sv.h, B2V + 4 * h + 3)
                for lc in range(2):
                    v.tensor_reduce(RS[:, 2 * h + lc: 2 * h + lc + 1],
                                    A2bf4[:, lc, 256 * h:256 * (h + 1)], AX.X, ALU.add)
                v.wait_ge(sm.h, B2M + 3 * h + 2)
                ins = v.tensor_reduce(csr[0:1, h:h + 1], pb[2][0:1, 0:256], AX.X, ALU.max)
                ins.then_inc(sv.h, 1)
                sv.v += 4
        blk.vector(_)

        def _(s):
            for h in range(4):
                s.wait_ge(sv.h, B2V + 4 * h + 1)
                for lc in range(2):
                    s.activation(E2f[:, lc, :], pb[lc][:, 0:256], AF.Exp,
                                 bias=nmcol[:, lc:lc + 1],
                                 accum_out=sums2[:, lc:lc + 1]).then_inc(
                                     sa.h, (1 if lc == 1 else 0))
                sa.v += 1
                s.wait_ge(sm.h, B2M + 3 * h + 3)
                for lc in range(2):
                    for cc in range(2):
                        ins = s.activation(
                            A2T4[:, cc, 256 * h + 128 * lc: 256 * h + 128 * (lc + 1)],
                            pbt[:, 256 * lc + 128 * cc: 256 * lc + 128 * (cc + 1)],
                            AF.Copy)
                ins.then_inc(sa.h, 1)
                sa.v += 1
        blk.scalar(_)
        S2M, S2V, S2A = sm.v, sv.v, sa.v

        def _(v):
            v.wait_ge(sv.h, S2V)
            v.tensor_copy(RSbf[:, :], RS[:, :]).then_inc(sv.h, 1)
            sv.v += 1
        blk.vector(_)

        def _(t):
            t.wait_ge(sv.h, sv.v)
            t.wait_ge(sa.h, S2A)
            t.transpose(pbt[0:8, 0:128], RSbf[:, :], ident_sb[:, :]).then_inc(sm.h, 1)
            sm.v += 1
        blk.tensor(_)
        TR1 = sm.v

        def _(v):
            v.wait_ge(sm.h, TR1)
            v.tensor_reduce(rm8[0:8, 0:1], pbt[0:8, 0:128], AX.X,
                            ALU.max).then_inc(sv.h, 1)
            sv.v += 1
            v.wait_ge(sv.h, sv.v)
            v.tensor_copy(rm8bf[0:8, 0:1], rm8[0:8, 0:1]).then_inc(sv.h, 1)
            sv.v += 1
        blk.vector(_)

        def _(t):
            t.wait_ge(sv.h, sv.v)
            t.transpose(pbt[0:1, 128:136], rm8bf[0:8, 0:1], ident_sb[0:8, 0:8]).then_inc(sm.h, 1)
            sm.v += 1
        blk.tensor(_)
        TR2 = sm.v

        def _(v):
            v.wait_ge(sm.h, TR2)
            v.tensor_reduce(scrow[0:1, 0:1], pbt[0:1, 128:136], AX.X, ALU.max)
            v.tensor_reduce(scrow[0:1, 1:2], csr[0:1, 0:4], AX.X,
                            ALU.max).then_inc(sv.h, 1)
            sv.v += 1
        blk.vector(_)
        SCROW = sv.v

        def _(sync):
            sync.wait_ge(sv.h, SCROW)
            sync.dma_start(out=cc_scale_i, in_=scrow[0:1, 0:2]).then_inc(sci.h, 16)
            sci.v += 16
        blk.sync(_)
        SCW = sci.v

        def _(g):
            g.wait_ge(sci.h, SCW)
            g.collective_compute(
                "AllReduce", ALU.max, replica_groups=[list(range(NCORES))],
                ins=[cc_scale_i.opt()], outs=[cc_scale_o.opt()],
            ).then_inc(scc.h)
            scc.v += 1
        blk.gpsimd(_)

        def _(sync):
            sync.wait_ge(scc.h, scc.v)
            sync.dma_start(out=scrow[0:1, 2:4], in_=cc_scale_o).then_inc(sst.h, 16)
            sst.v += 16
        blk.sync(_)
        SCR2 = sst.v

        def _(v):
            v.wait_ge(sst.h, SCR2)
            v.tensor_tensor(scrow[0:1, 4:5], scrow[0:1, 2:3], scrow[0:1, 3:4],
                            ALU.mult).then_inc(sv.h, 1)
            sv.v += 1
            v.wait_ge(sv.h, sv.v)
            v.reciprocal(scrow[0:1, 5:6], scrow[0:1, 4:5]).then_inc(sv.h, 1)
            sv.v += 1
            v.wait_ge(sv.h, sv.v)
            v.tensor_copy(scbf[0:1, 0:1], scrow[0:1, 5:6]).then_inc(sv.h, 1)
            sv.v += 1
        blk.vector(_)
        RSCBF = sv.v

        def _(t):
            t.wait_ge(sv.h, RSCBF)
            t.matmul(pb[5][:, 0:1], ones_sb[0:1, :], scbf[0:1, 0:1],
                     start=True, stop=True).then_inc(sm.h, 1)
            sm.v += 1
        blk.tensor(_)
        RSCMM = sm.v

        def _(v):
            v.wait_ge(sm.h, RSCMM)
            v.tensor_copy(rsc[:, 0:1], pb[5][:, 0:1]).then_inc(sv.h, 1)
            sv.v += 1
        blk.vector(_)
        RSC = sv.v

        for h in range(HN):
            hc, off = h // 2, 64 * (h % 2)
            B5M, B5A, B5V = sm.v, sa.v, sv.v

            def _(t, h=h, hc=hc, off=off, B5M=B5M, B5A=B5A, B5V=B5V):
                t.wait_ge(sv.h, B5V)
                for tc in range(32):
                    if tc >= 1:
                        t.wait_ge(sa.h, B5A + tc)
                    elif h == 0:
                        t.wait_ge(sv.h, LM)
                    t.matmul(pb[0][:, 0:256],
                             kTs[off:off + 64, hc, 128 * tc:128 * (tc + 1)],
                             ql_bf[off:off + 64, hc, :],
                             start=True, stop=True).then_inc(sm.h, 1)
                    sm.v += 1
                    t.wait_ge(sa.h, B5A + tc + 1)
                    t.matmul(pb[3][0:1, 0:256], ones_sb[:, 0:1], E3b[:, tc % 2, :],
                             start=(tc == 0), stop=(tc == 31))
                    for lc in range(2):
                        mm = t.matmul(pb[1 + lc][:, 0:64],
                                      E3b[:, tc % 2, 128 * lc:128 * (lc + 1)],
                                      vtok[:, tc, 64 * h:64 * (h + 1)],
                                      start=(tc == 0), stop=(tc == 31))
                    mm.then_inc(sm.h, 1)
                    sm.v += 1
            blk.tensor(_)

            def _(s, B5M=B5M):
                for tc in range(32):
                    s.wait_ge(sm.h, B5M + 2 * tc + 1)
                    if tc >= 2:
                        s.wait_ge(sm.h, B5M + 2 * (tc - 2) + 2)
                    s.activation(E3b[:, tc % 2, :], pb[0][:, 0:256],
                                 AF.Exp).then_inc(sa.h, 1)
                    sa.v += 1
            blk.scalar(_)
            E3M = sm.v

            def _(v, h=h, E3M=E3M):
                v.wait_ge(sm.h, E3M)
                v.reciprocal(r3row[0:1, 0:256], pb[3][0:1, 0:256]).then_inc(sv.h, 1)
                sv.v += 1
                v.wait_ge(sv.h, sv.v)
                v.tensor_copy(r3bf[0:1, 0:256], r3row[0:1, 0:256]).then_inc(sv.h, 1)
                sv.v += 1
            blk.vector(_)
            R3V = sv.v

            def _(t, B2A_l=sa.v, R3V=R3V):
                t.wait_ge(sv.h, R3V)
                if h > 0:
                    t.wait_ge(sa.h, B2A_l)
                for lc in range(2):
                    ins = t.transpose(pbt[:, 2 * lc:2 * lc + 1],
                                      r3bf[0:1, 128 * lc:128 * (lc + 1)],
                                      ident_sb[0:1, 0:1])
                ins.then_inc(sm.h, 1)
                sm.v += 1
            blk.tensor(_)
            R3T = sm.v

            def _(s, R3T=R3T):
                s.wait_ge(sm.h, R3T)
                for lc in range(2):
                    ins = s.activation(r3c[:, lc:lc + 1], pbt[:, 2 * lc:2 * lc + 1],
                                       AF.Copy)
                ins.then_inc(sa.h, 1)
                sa.v += 1
            blk.scalar(_)
            R3C = sa.v

            def _(v, h=h, R3C=R3C):
                v.wait_ge(sa.h, R3C)
                for lc in range(2):
                    ins = v.tensor_single_scalar(F4bf[:, lc, 64 * h:64 * (h + 1)],
                                                 pb[1 + lc][:, 0:64],
                                                 r3c[:, lc:lc + 1], ALU.mult)
                ins.then_inc(sv.h, 1)
                sv.v += 1
            blk.vector(_)

        IK = {7.0: 0, 15.0: 1, 13.0: 2}

        for h in range(HN):
            B4V = sv.v

            def _(v, h=h, B4V=B4V, GPREV=sm.v, ILD=sld.v):
                v.wait_ge(sv.h, RSC)
                v.wait_ge(sm.h, GPREV)
                v.wait_ge(sld.h, ILD)
                for c in range(2):
                    v.tensor_single_scalar(zb[0][:, c, :],
                                           A2T4[:, c, 256 * h:256 * (h + 1)],
                                           rsc[:, 0:1], ALU.mult)
                    ins = v.tensor_single_scalar(zTb[0][:, c, :],
                                                 A2bf4[:, c, 256 * h:256 * (h + 1)],
                                                 rsc[:, 0:1], ALU.mult)
                ins.then_inc(sv.h, 1)
                sv.v += 1
            blk.vector(_)

            cur, curT = zb[0], zTb[0]
            nxt, nxtT = zb[1], zTb[1]
            A2T_h = lambda kc, oc, h=h: A2T4[:, kc, 256 * h + 128 * oc: 256 * h + 128 * (oc + 1)]

            for it in range(6):
                prods = [
                    (lambda kc, oc: A2T_h(kc, oc), cur, [(w1bf, ("isub", 0))]),
                    (lambda kc, oc, cur=cur: cur[:, kc, 128 * oc:128 * (oc + 1)],
                     A2T4[:, :, 256 * h:256 * (h + 1)],
                     [(xzTbf, ("copy",)), (w1Tbf, ("isub", 0))]),
                    (lambda kc, oc: xzTbf[:, kc, 128 * oc:128 * (oc + 1)], w1bf,
                     [(w2bf, ("isub", 1))]),
                    (lambda kc, oc: w1bf[:, kc, 128 * oc:128 * (oc + 1)], xzTbf,
                     [(w2Tbf, ("isub", 1))]),
                    (lambda kc, oc: xzTbf[:, kc, 128 * oc:128 * (oc + 1)], w2bf,
                     [(w4bf, ("isub", 2))]),
                    (lambda kc, oc: w2bf[:, kc, 128 * oc:128 * (oc + 1)], xzTbf,
                     [(w4Tbf, ("isub", 2))]),
                    (lambda kc, oc, curT=curT: curT[:, kc, 128 * oc:128 * (oc + 1)],
                     w4bf, [(nxt, ("scale", 0.25))]),
                    (lambda kc, oc, cur=cur: w4bf[:, kc, 128 * oc:128 * (oc + 1)],
                     curT, [(nxtT, ("scale", 0.25))]),
                ]
                for pi, (lf, rhs_t, evacs) in enumerate(prods):
                    bank = 2 * (pi % 2)
                    PV = sv.v

                    def _(t, lf=lf, rhs_t=rhs_t, bank=bank, PV=PV):
                        t.wait_ge(sv.h, PV)
                        for oc in range(2):
                            for kc in range(2):
                                mm = t.matmul(pb[bank + oc][:, 0:256], lf(kc, oc),
                                              rhs_t[:, kc, :],
                                              start=(kc == 0), stop=(kc == 1))
                        mm.then_inc(sm.h, 1)
                        sm.v += 1
                    blk.tensor(_)
                    PM = sm.v

                    def _(v, evacs=evacs, bank=bank, PM=PM):
                        v.wait_ge(sm.h, PM)
                        for dst, mode in evacs:
                            for oc in range(2):
                                if mode[0] == "isub":
                                    ins = v.tensor_tensor(
                                        dst[:, oc, :],
                                        ikc_sb[:, oc, 256 * mode[1]:256 * (mode[1] + 1)],
                                        pb[bank + oc][:, 0:256], ALU.subtract)
                                elif mode[0] == "copy":
                                    ins = v.tensor_copy(dst[:, oc, :], pb[bank + oc][:, 0:256])
                                else:
                                    ins = v.tensor_single_scalar(
                                        dst[:, oc, :], pb[bank + oc][:, 0:256],
                                        mode[1], ALU.mult)
                        ins.then_inc(sv.h, 1)
                        sv.v += 1
                    blk.vector(_)
                cur, nxt = nxt, cur
                curT, nxtT = nxtT, curT

            GV = sv.v

            def _(t, h=h, curT=curT, GV=GV):
                t.wait_ge(sv.h, GV)
                for oc in range(2):
                    for kc in range(2):
                        mm = t.matmul(pb[4 + oc][:, 0:64],
                                      curT[:, kc, 128 * oc:128 * (oc + 1)],
                                      F4bf[:, kc, 64 * h:64 * (h + 1)],
                                      start=(kc == 0), stop=(kc == 1))
                mm.then_inc(sm.h, 1)
                sm.v += 1
            blk.tensor(_)
            GM = sm.v

            def _(v, h=h, GM=GM):
                v.wait_ge(sm.h, GM)
                for oc in range(2):
                    ins = v.tensor_copy(G4bf[:, oc, 64 * h:64 * (h + 1)],
                                        pb[4 + oc][:, 0:64])
                ins.then_inc(sv.h, 1)
                sv.v += 1
            blk.vector(_)

        for h in range(HN):
            hc, off = h // 2, 64 * (h % 2)
            for tt in range(8):
                B6V = sv.v
                B6A = sa.v

                def _(t, hc=hc, off=off, tt=tt, B6V=B6V, B6A=B6A):
                    t.wait_ge(sv.h, B6V)
                    for lc in range(2):
                        if lc == 1 or tt > 0 or True:
                            t.wait_ge(sa.h, B6A + lc)
                        t.matmul(pb[0][:, 0:512],
                                 kl_bf[off:off + 64, hc, 128 * lc:128 * (lc + 1)],
                                 qTs[off:off + 64, hc, 512 * tt:512 * (tt + 1)],
                                 start=True, stop=True).then_inc(sm.h, 1)
                        sm.v += 1
                        t.wait_ge(sa.h, B6A + lc + 1)
                        t.matmul(pb[1][0:64, 0:512], G4bf[:, lc, 64 * h:64 * (h + 1)],
                                 E1b[:, lc, :], start=(lc == 0), stop=(lc == 1))
                        mm = t.matmul(pb[2][0:1, 0:512], ones_sb[:, 0:1],
                                      E1b[:, lc, :], start=(lc == 0), stop=(lc == 1))
                        mm.then_inc(sm.h, 1)
                        sm.v += 1
                blk.tensor(_)
                E1M0 = sm.v - 4

                def _(s, E1M0=E1M0):
                    for lc in range(2):
                        s.wait_ge(sm.h, E1M0 + 2 * lc + 1)
                        s.activation(E1b[:, lc, :], pb[0][:, 0:512],
                                     AF.Exp).then_inc(sa.h, 1)
                        sa.v += 1
                blk.scalar(_)
                E1M = sm.v

                def _(v, E1M=E1M):
                    v.wait_ge(sm.h, E1M)
                    v.reciprocal(r1row[0:1, 0:512], pb[2][0:1, 0:512]).then_inc(sv.h, 1)
                    sv.v += 1
                    v.wait_ge(sv.h, sv.v)
                    v.tensor_copy(r1bf[0:1, 0:512], r1row[0:1, 0:512]).then_inc(sv.h, 1)
                    sv.v += 1
                blk.vector(_)
                R1V = sv.v

                def _(t, R1V=R1V):
                    t.wait_ge(sv.h, R1V)
                    t.matmul(pb[3][0:64, 0:512], ones_sb[0:1, 0:64],
                             r1bf[0:1, 0:512], start=True, stop=True).then_inc(sm.h, 1)
                    sm.v += 1
                blk.tensor(_)
                R1M = sm.v

                def _(s, hc=hc, off=off, tt=tt, E1M=E1M):
                    s.wait_ge(sm.h, E1M)
                    s.activation(outTo[off:off + 64, hc, 512 * tt:512 * (tt + 1)],
                                 pb[1][0:64, 0:512], AF.Copy).then_inc(sa.h, 1)
                    sa.v += 1
                blk.scalar(_)
                OA = sa.v

                def _(v, hc=hc, off=off, tt=tt, OA=OA, R1M=R1M):
                    v.wait_ge(sa.h, OA)
                    v.wait_ge(sm.h, R1M)
                    v.tensor_tensor(outTo[off:off + 64, hc, 512 * tt:512 * (tt + 1)],
                                    outTo[off:off + 64, hc, 512 * tt:512 * (tt + 1)],
                                    pb[3][0:64, 0:512], ALU.mult).then_inc(sv.h, 1)
                    sv.v += 1
                blk.vector(_)

            CB = sv.v

            def _(t, h=h, CB=CB):
                t.wait_ge(sv.h, CB)
                for tc in range(32):
                    if tc >= 2:
                        t.wait_ge(sv.h, CB + tc - 1)
                    srcs = []
                    if tc == 0:
                        srcs.append((31, 3))
                    elif tc == 16:
                        srcs.append((15, 5))
                    else:
                        srcs.append((tc - 1, 0))
                    srcs.append((tc, 1))
                    if tc == 31:
                        srcs.append((0, 4))
                    elif tc == 15:
                        srcs.append((16, 6))
                    else:
                        srcs.append((tc + 1, 2))
                    for si, (sc_, bi) in enumerate(srcs):
                        mm = t.matmul(pb[4 + tc % 2][0:64, 0:128],
                                      vtok[:, sc_, 64 * h:64 * (h + 1)],
                                      band_sb[:, (h * 7 + bi) * 128:(h * 7 + bi + 1) * 128],
                                      start=(si == 0), stop=(si == 2))
                    mm.then_inc(sm.h, 1)
                    sm.v += 1
            blk.tensor(_)
            CM0 = sm.v - 32

            def _(v, h=h, hc=hc, off=off, CM0=CM0, CBV=sv.v):
                v.wait_ge(sv.h, CBV)
                for tc in range(32):
                    v.wait_ge(sm.h, CM0 + tc + 1)
                    v.tensor_tensor(outTo[off:off + 64, hc, 128 * tc:128 * (tc + 1)],
                                    outTo[off:off + 64, hc, 128 * tc:128 * (tc + 1)],
                                    pb[4 + tc % 2][0:64, 0:128], ALU.add).then_inc(sv.h, 1)
                    sv.v += 1
            blk.vector(_)

    readbuf = at("readbuf", [128, 4, TH], BF16, offset=98304)
    wv_bf = at("wv_bf", [128, NDC, D], BF16, offset=114688)
    wvstage = at("wvstage", [128, 2, D], F32, offset=131072)
    xgstage = at("xgstage", [128, NDC, 512], BF16, offset=139264)
    owo_bf = scr("owo_bf", [128, 4, D], BF16, 8192)
    owm_bf = scr("owm_bf", [128, 2, D], BF16, 16384)
    wqk_bf = scr("wqk_bf", [128, NDC, 512], BF16, 20480)
    q0stage = scr("q0stage", [128, 16, 512], BF16, 28672)
    xg0buf = [at(f"xg0buf{i}", [128, 1, TH], BF16, offset=98304 + 4096 * i)
              for i in range(2)]

    if PHASES >= 3:
        P2_SV, P2_SA, P2_SM = sv.v, sa.v, sm.v

        def _(sync):
            sync.wait_ge(sv.h, P2_SV)
            sync.dma_start(out=cc_out_i.rearrange("(c p) t -> p c t", p=128),
                           in_=outTo[:, :, :]).then_inc(sci.h, 16)
            sci.v += 16
        blk.sync(_)
        OUTW_CI = sci.v

        def _(g):
            g.wait_ge(sci.h, OUTW_CI)
            g.collective_compute(
                "AllGather", ALU.bypass,
                replica_groups=[[0, 1], [2, 3], [4, 5], [6, 7]],
                ins=[cc_out_i.opt()], outs=[cc_out_o.opt()],
            ).then_inc(scc.h)
            scc.v += 1
        blk.gpsimd(_)
        OUT_AG = scc.v

        def _(sync):
            sync.wait_ge(scc.h, OUT_AG)
            sync.dma_start(
                out=readbuf[:, :, :],
                in_=cc_out_o.rearrange("(c p) t -> p c t", p=128)[:, :, TH:T],
            ).then_inc(sx2[0].h, 16)
            sx2[0].v += 16
            sync.dma_start(out=wstg[:, 0:2048].rearrange("p (c n) -> p c n", c=2),
                           in_=outw_o.rearrange("(c p) n -> p c n", p=128)[:, 0:2, :]
                           ).then_inc(sld.h, 16)
            sld.v += 16
        blk.sync(_)
        RB_LD = sx2[0].v

        def _(sync):
            base = sld.v

            seq = [
                (outw_o.rearrange("(c p) n -> p c n", p=128)[:, 2:4, :], 2),
                (outw_m.rearrange("(c p) n -> p c n", p=128)[:, 0:2, :], 2),
                (wqk.rearrange("(c p) n -> p c n", p=128)[:, 0:4, :], 4),
                (wqk.rearrange("(c p) n -> p c n", p=128)[:, 4:8, :], 4),
            ]
            for i, (srcv, nch) in enumerate(seq):
                sync.wait_ge(sv.h, P2_SV + i + 1)
                sync.dma_start(out=wstg[:, 0:2048].rearrange("p (c n) -> p c n", c=nch),
                               in_=srcv).then_inc(sld.h, 16)
                sld.v += 16
            for r in range(4):
                sync.wait_ge(sv.h, P2_SV + 5 + r)
                sync.dma_start(out=wvstage[:, :, :],
                               in_=wv.rearrange("(c p) n -> p c n", p=128)[:, 2 * r:2 * r + 2, :]
                               ).then_inc(sld.h, 16)
                sld.v += 16
        blk.sync(_)

        def _(v):
            base = sld.v - 16 * 9
            v.wait_ge(sld.h, base + 16)
            v.tensor_copy(owo_bf[:, 0:2, :],
                          wstg[:, 0:2048].rearrange("p (c n) -> p c n", c=2)).then_inc(sv.h, 1)
            v.wait_ge(sld.h, base + 32)
            v.tensor_copy(owo_bf[:, 2:4, :],
                          wstg[:, 0:2048].rearrange("p (c n) -> p c n", c=2)).then_inc(sv.h, 1)
            v.wait_ge(sld.h, base + 48)
            v.tensor_copy(owm_bf[:, :, :],
                          wstg[:, 0:2048].rearrange("p (c n) -> p c n", c=2)).then_inc(sv.h, 1)
            v.wait_ge(sld.h, base + 64)
            v.tensor_copy(wqk_bf[:, 0:4, :],
                          wstg[:, 0:2048].rearrange("p (c n) -> p c n", c=4)).then_inc(sv.h, 1)
            v.wait_ge(sld.h, base + 80)
            v.tensor_copy(wqk_bf[:, 4:8, :],
                          wstg[:, 0:2048].rearrange("p (c n) -> p c n", c=4)).then_inc(sv.h, 1)
            for r in range(4):
                v.wait_ge(sld.h, base + 96 + 16 * r)
                v.tensor_copy(wv_bf[:, 2 * r:2 * r + 2, :],
                              wvstage[:, :, :]).then_inc(sv.h, 1)
            sv.v += 9
        blk.vector(_)
        WCAST3 = sv.v

        B3M, B3A = sm.v, sa.v

        def _(t):
            t.wait_ge(sv.h, WCAST3)
            t.wait_ge(sx2[0].h, RB_LD)
            for jt in range(4):
                for d2 in range(NDC):
                    bank = pb[d2 % 2]
                    idx = jt * NDC + d2
                    if idx >= 2:
                        t.wait_ge(sa.h, B3A + idx - 1)
                    first = True
                    for cch in range(4):
                        t.matmul(bank[:, 0:512], owo_bf[:, cch, 128 * d2:128 * (d2 + 1)],
                                 readbuf[:, cch, 512 * jt:512 * (jt + 1)],
                                 start=first, stop=False)
                        first = False
                    for hc in range(2):
                        mm = t.matmul(bank[:, 0:512],
                                      owm_bf[:, hc, 128 * d2:128 * (d2 + 1)],
                                      outTo[:, hc, 512 * jt:512 * (jt + 1)],
                                      start=False, stop=(hc == 1))
                    mm.then_inc(sm.h, 1)
                    sm.v += 1
        blk.tensor(_)

        def _(s, B3CI=sci.v):
            for jt in range(4):
                for d2 in range(NDC):
                    idx = jt * NDC + d2
                    s.wait_ge(sm.h, B3M + idx + 1)
                    if d2 == 0 and jt >= 1:
                        s.wait_ge(sci.h, B3CI + 16 * jt)
                    s.activation(xgstage[:, d2, :], pb[d2 % 2][:, 0:512], AF.Identity,
                                 bias=outb_sb[:, d2:d2 + 1]).then_inc(sa.h, 1)
                    sa.v += 1
        blk.scalar(_)
        XGA = sa.v

        def _(sync):
            for jt in range(4):
                sync.wait_ge(sa.h, B3A + NDC * (jt + 1))
                sync.dma_start(
                    out=cc_xg_i.rearrange("(c p) t -> p c t", p=128)[:, :, 512 * jt:512 * (jt + 1)],
                    in_=xgstage[:, :, :]).then_inc(sci.h, 16)
                sci.v += 16
        blk.sync(_)
        XG_CI = sci.v

        def _(g):
            g.wait_ge(sci.h, XG_CI)
            g.collective_compute(
                "AllGather", ALU.bypass,
                replica_groups=[[0, 2, 4, 6], [1, 3, 5, 7]],
                ins=[cc_xg_i.opt()], outs=[cc_xg_o.opt()],
            ).then_inc(scc.h)
            scc.v += 1
        blk.gpsimd(_)
        XG_AG = scc.v

        sxg = [S(nc, "sxg0"), S(nc, "sxg1")]
        ENC0 = sv.v

        def _(sync):
            sync.wait_ge(scc.h, XG_AG)
            for ch in range(NDC):
                if ch >= 2:
                    sync.wait_ge(sv.h, ENC0 + ch - 1)
                sync.dma_start(
                    out=xg0buf[ch % 2][:, 0, :],
                    in_=cc_xg_o.rearrange("(c p) t -> p c t", p=128)[:, ch, :],
                ).then_inc(sxg[ch % 2].h, 16)
                sxg[ch % 2].v += 16
        blk.sync(_)

        def _(v):
            for ch in range(NDC):
                v.wait_ge(sxg[ch % 2].h, 16 * (ch // 2 + 1))
                v.tensor_tensor(encT[:, ch, :], xg0buf[ch % 2][:, 0, :],
                                ln0h[:, ch, :], ALU.add).then_inc(sv.h, 1)
                sv.v += 1
        blk.vector(_)
        ENCV = sv.v

        B3M2, B3A2 = sm.v, sa.v

        def _(t):
            t.wait_ge(sv.h, WCAST3)
            for jt in range(4):
                for d2 in range(NDC):
                    bank = pb[2 + d2 % 2]
                    idx = jt * NDC + d2
                    if idx >= 2:
                        t.wait_ge(sa.h, B3A2 + idx - 1)
                    for dch in range(NDC):
                        mm = t.matmul(bank[:, 0:512],
                                      wv_bf[:, dch, 128 * d2:128 * (d2 + 1)],
                                      ln0h[:, dch, 512 * jt:512 * (jt + 1)],
                                      start=(dch == 0), stop=(dch == NDC - 1))
                    mm.then_inc(sm.h, 1)
                    sm.v += 1
        blk.tensor(_)

        def _(s):
            for jt in range(4):
                for d2 in range(NDC):
                    idx = jt * NDC + d2
                    s.wait_ge(sm.h, B3M2 + idx + 1)
                    s.activation(valT[:, d2, 512 * jt:512 * (jt + 1)],
                                 pb[2 + d2 % 2][:, 0:512], AF.Identity,
                                 bias=wvb_sb[:, d2:d2 + 1]).then_inc(sa.h, 1)
                    sa.v += 1
        blk.scalar(_)
        VALA = sa.v

    Qg = [at(f"Qg{i}", [128, 256], BF16, offset=32768 + 512 * i) for i in range(2)]
    Kg = [at(f"Kg{i}", [128, 256], BF16, offset=33792 + 512 * i) for i in range(2)]
    prod4 = at("prod4", [128, 256], F32, offset=34816)
    junk4 = at("junk4", [128, 256], F32, offset=35840)
    ctb = at("ctb", [128, NET], F32, offset=36864)
    av16 = at("av16", [128, NET], F32, offset=37376)
    hi_f = at("hi_f", [128, NET], F32, offset=37888)
    lo_f = at("lo_f", [128, NET], F32, offset=38400)
    hii4 = at("hii4", [128, NET], I32, offset=38912)
    loi4 = at("loi4", [128, NET], I32, offset=39424)
    adjc_cl = at("adjc_cl", [128, NET], I32, offset=46592)
    Ub = [at(f"Ub{i}", [128, 64], BF16, offset=39936 + 128 * i) for i in range(2)]
    Vb = [at(f"Vb{i}", [128, 64], BF16, offset=40192 + 128 * i) for i in range(2)]
    a2dr = at("a2dr", [64, 64], F32, offset=40448)
    acols = at("acols", [64, 8], F32, offset=40704)
    arow1 = at("arow1", [1, 64], F32, offset=40736)
    alpha2 = at("alpha2", [64, 64], F32, offset=41024)
    alpha2b = at("alpha2b", [64, 72], BF16, offset=41280)
    selout = at("selout", [32, 64], BF16, offset=41440)
    alpharow = scr("alpharow", [1, TH], BF16, 36992)
    a2dsb = at("a2dsb", [64, 64], F32, offset=46080)

    if PHASES >= 4:
        B4M, B4A = sm.v, sa.v

        def _(t):
            t.wait_ge(sv.h, ENCV)
            for tch in range(16):
                bank = pb[4 + tch % 2]
                if tch >= 2:
                    t.wait_ge(sa.h, B4A + tch - 1)
                for dch in range(NDC):
                    mm = t.matmul(bank[:, 0:512],
                                  encT[:, dch, 128 * tch:128 * (tch + 1)],
                                  wqk_bf[:, dch, :],
                                  start=(dch == 0), stop=(dch == NDC - 1))
                mm.then_inc(sm.h, 1)
                sm.v += 1
        blk.tensor(_)

        def _(s):
            for tch in range(16):
                s.wait_ge(sm.h, B4M + tch + 1)
                s.activation(q0stage[:, tch, :], pb[4 + tch % 2][:, 0:512],
                             AF.Copy).then_inc(sa.h, 1)
                sa.v += 1
        blk.scalar(_)
        Q0A = sa.v

        def _(sync):
            sync.wait_ge(sa.h, Q0A)
            sync.dma_start(out=cc_tab_i.rearrange("(c p) t -> p c t", p=128),
                           in_=q0stage[:, :, :]).then_inc(sci.h, 16)
            sci.v += 16
        blk.sync(_)
        TAB_CI = sci.v

        def _(g):
            g.wait_ge(sci.h, TAB_CI)
            g.collective_compute(
                "AllGather", ALU.bypass, replica_groups=[list(range(NCORES))],
                ins=[cc_tab_i.opt()], outs=[cc_tab_o.opt()],
            ).then_inc(scc.h)
            scc.v += 1
        blk.gpsimd(_)
        TAB_AG = scc.v

    wst6 = [at(f"wst6{i}", [128, NDC // 2, D], F32, offset=98304 + 16384 * i)
            for i in range(2)]
    gs6 = at("gs6", [128, 2], F32, offset=131072)
    onef_c = at("onef_c", [128, 1], F32, offset=131136)
    onef_r = at("onef_r", [1, 128], F32, offset=131200)
    grow = at("grow", [1, 8], F32, offset=131776)
    scol6 = at("scol6", [128, 2], F32, offset=132352)
    qge = scr("qge", [128, 4, D], BF16, 0)
    qle = scr("qle", [128, 4, D], BF16, 8192)
    wqq = scr("wqq", [128, 4, D], BF16, 16384)
    wqst = at("wqst", [128, NDC, 512], BF16, offset=132928)

    wst6b = at("wst6b", [128, NDC, 512], F32, offset=114688)
    if PHASES >= 4:

        def _(sync):
            sync.wait_ge(sa.h, VALA)
            sync.wait_ge(sv.h, ENCV)
            sync.dma_start(out=wst6[0][:, :, :],
                           in_=w1s.rearrange("(c p) n -> p c n", p=128)).then_inc(sx2[0].h, 16)
            sx2[0].v += 16
            sync.dma_start(out=wst6b[:, :, :],
                           in_=w2s.rearrange("(c p) n -> p c n", p=128)).then_inc(sx2[0].h, 16)
            sx2[0].v += 16
        blk.sync(_)
        W6L = sx2[0].v

        def _(v):
            v.wait_ge(sx2[0].h, W6L)
            v.memset(onef_c[:, 0:1], 1.0)
            v.memset(onef_r[0:1, :], 1.0)
            v.tensor_reduce(gs6[:, 0:1], wst6[0][:, :, :], AX.XY, ALU.add,
                            apply_absolute_value=True)
            v.tensor_reduce(gs6[:, 1:2], wst6b[:, :, :], AX.XY, ALU.add,
                            apply_absolute_value=True).then_inc(sv.h, 1)
            sv.v += 1
        blk.vector(_)
        W6R = sv.v

        def _(t):
            t.wait_ge(sv.h, W6R)
            t.wait_ge(sa.h, Q0A)
            t.matmul(pb[5][0:1, 0:2], onef_c[:, 0:1], gs6[:, 0:2],
                     start=True, stop=True).then_inc(sm.h, 1)
            sm.v += 1
        blk.tensor(_)
        W6M = sm.v

        def _(v):
            v.wait_ge(sm.h, W6M)
            v.tensor_copy(grow[0:1, 0:2], pb[5][0:1, 0:2]).then_inc(sv.h, 1)
            sv.v += 1
        blk.vector(_)
        W6C = sv.v

        def _(sync):
            sync.wait_ge(sv.h, W6C)
            sync.dma_start(out=cc_gam_i, in_=grow[0:1, 0:2]).then_inc(sci.h, 16)
            sci.v += 16
        blk.sync(_)
        GAM_CI = sci.v

        def _(g):
            g.wait_ge(sci.h, GAM_CI)
            g.collective_compute(
                "AllReduce", ALU.add, replica_groups=[list(range(NCORES))],
                ins=[cc_gam_i.opt()], outs=[cc_gam_o.opt()],
            ).then_inc(scc.h)
            scc.v += 1
        blk.gpsimd(_)
        GAM_AR = scc.v

        def _(sync):
            sync.wait_ge(scc.h, GAM_AR)
            sync.dma_start(out=grow[0:1, 2:4], in_=cc_gam_o).then_inc(sx2[1].h, 16)
            sx2[1].v += 16
        blk.sync(_)
        GAM_LD = sx2[1].v

        def _(v):
            v.wait_ge(sx2[1].h, GAM_LD)
            v.tensor_single_scalar(grow[0:1, 4:6], grow[0:1, 2:4],
                                   1.0 / (512 * NCORES * D), ALU.mult).then_inc(sv.h, 1)
            sv.v += 1
            v.wait_ge(sv.h, sv.v)
            v.tensor_single_scalar(grow[0:1, 4:6], grow[0:1, 4:6], 1e-5,
                                   ALU.add).then_inc(sv.h, 1)
            sv.v += 1
            v.wait_ge(sv.h, sv.v)
            v.reciprocal(grow[0:1, 6:8], grow[0:1, 4:6]).then_inc(sv.h, 1)
            sv.v += 1
        blk.vector(_)
        W6G = sv.v

        def _(t):
            t.wait_ge(sv.h, W6G)
            t.matmul(pb[5][:, 4:6], onef_r[0:1, :], grow[0:1, 6:8],
                     start=True, stop=True).then_inc(sm.h, 1)
            sm.v += 1
        blk.tensor(_)
        W6B = sm.v

        def _(v):
            v.wait_ge(sm.h, W6B)
            v.tensor_copy(scol6[:, 0:2], pb[5][:, 4:6]).then_inc(sv.h, 1)
            sv.v += 1
        blk.vector(_)
        W6S = sv.v

        wqq2 = scr("wqq2", [128, NDC, 512], BF16, 16384)
        wqst2 = at("wqst2", [128, 4, D], BF16, offset=132928)

        for wi in range(2):
            QB = sv.v
            src = wst6[0] if wi == 0 else wst6b
            dstq = wqq if wi == 0 else wqq2

            def _(v, wi=wi, src=src, dstq=dstq, QB=QB, PM=sm.v):
                if wi == 0:
                    v.wait_ge(sv.h, W6S)
                else:
                    v.wait_ge(sv.h, QB)
                    v.wait_ge(sm.h, PM)
                v.tensor_scalar(qge[:, :, :].rearrange("p a b -> p (a b)"),
                                src[:, :, :].rearrange("p a b -> p (a b)"),
                                scol6[:, wi:wi + 1], 0.5, ALU.mult, ALU.is_gt)
                v.tensor_scalar(qle[:, :, :].rearrange("p a b -> p (a b)"),
                                src[:, :, :].rearrange("p a b -> p (a b)"),
                                scol6[:, wi:wi + 1], -0.5, ALU.mult,
                                ALU.is_lt).then_inc(sv.h, 1)
                sv.v += 1
                v.wait_ge(sv.h, sv.v)
                v.tensor_tensor(dstq[:, :, :].rearrange("p a b -> p (a b)"),
                                qge[:, :, :].rearrange("p a b -> p (a b)"),
                                qle[:, :, :].rearrange("p a b -> p (a b)"),
                                ALU.subtract).then_inc(sv.h, 1)
                sv.v += 1
            blk.vector(_)
            QV = sv.v
            TBASE_M, TBASE_A = sm.v, sa.v
            blocks = []
            if wi == 0:
                for c in range(4):
                    for d8 in range(NDC):
                        blocks.append((wqq[:, c, 128 * d8:128 * (d8 + 1)],
                                       wqst[:, d8, 128 * c:128 * (c + 1)]))
            else:
                for c in range(NDC):
                    for f8 in range(4):
                        blocks.append((wqq2[:, c, 128 * f8:128 * (f8 + 1)],
                                       wqst2[:, f8, 128 * c:128 * (c + 1)]))

            def _(t, QV=QV, TBASE_A=TBASE_A, blocks=blocks):
                t.wait_ge(sv.h, QV)
                for bi, (inap, outap) in enumerate(blocks):
                    t.wait_ge(sa.h, TBASE_A + bi)
                    t.transpose(pbt[:, 128 * (bi % 2):128 * (bi % 2) + 128],
                                inap, ident_sb[:, :]).then_inc(sm.h, 1)
                    sm.v += 1
            blk.tensor(_)

            def _(s, TBASE_M=TBASE_M, blocks=blocks, wi=wi, CIV=sci.v):
                for bi, (inap, outap) in enumerate(blocks):
                    if bi == 0 and wi == 1:
                        s.wait_ge(sci.h, CIV)
                    s.wait_ge(sm.h, TBASE_M + bi + 1)
                    s.activation(outap, pbt[:, 128 * (bi % 2):128 * (bi % 2) + 128],
                                 AF.Copy).then_inc(sa.h, 1)
                    sa.v += 1
            blk.scalar(_)

            WQA = sa.v

            def _(sync, wi=wi, WQA=WQA):
                sync.wait_ge(sa.h, WQA)
                if wi == 0:
                    sync.dma_start(
                        out=cc_wq_i[0:1024, :].rearrange("(c p) n -> p c n", p=128),
                        in_=wqst[:, :, :]).then_inc(sci.h, 16)
                else:
                    sync.dma_start(
                        out=cc_wq_i.rearrange("a b -> (a b)")[524288:1048576]
                        .rearrange("(c p n) -> p c n", p=128, n=1024),
                        in_=wqst2[:, :, :]).then_inc(sci.h, 16)
                sci.v += 16
            blk.sync(_)

        WQ_CI = sci.v
        WPREP_QV = sv.v


        def _(v):
            v.wait_ge(sld.h, P0_LD)
            v.wait_ge(sa.h, VALA)
            v.wait_ge(sv.h, ENCV)
            v.tensor_single_scalar(av16[:, :], adjv_sb[:, :], 1.0 / 16.0, ALU.mult)
            v.tensor_single_scalar(hii4[:, :], adjr_sb[:, :], 6, ALU.logical_shift_right)
            v.tensor_single_scalar(loi4[:, :], adjr_sb[:, :], 63, ALU.bitwise_and)
            v.tensor_single_scalar(adjc_cl[:, :], adjc_sb[:, :], 4095,
                                   ALU.min).then_inc(sv.h, 1)
            sv.v += 1
            v.wait_ge(sv.h, sv.v)
            v.tensor_copy(hi_f[:, :], hii4[:, :])
            v.tensor_copy(lo_f[:, :], loi4[:, :]).then_inc(sv.h, 1)
            sv.v += 1
        blk.vector(_)
        EPREP = sv.v

        BGV, BGA, BGM = sv.v, sa.v, sm.v
        sg2 = [S(nc, "sg20"), S(nc, "sg21")]

        def _(g):
            g.wait_ge(scc.h, TAB_AG)
            g.wait_ge(sv.h, EPREP)
            for et in range(NET):
                if et == 32:
                    g.wait_ge(sci.h, WQ_CI)
                    g.collective_compute(
                        "AllGather", ALU.bypass,
                        replica_groups=[list(range(NCORES))],
                        ins=[cc_wq_i.opt()], outs=[cc_wq_o.opt()],
                    ).then_inc(scc.h)
                    scc.v += 1
                if et >= 2:
                    g.wait_ge(sv.h, BGV + 2 * (et - 2) + 1)
                g.indirect_dma_start(
                    Qg[et % 2][:, :], None, cc_tab_o,
                    IndirectOffsetOnAxis(adjr_sb[:, et:et + 1], 0),
                ).then_inc(sg2[et % 2].h, 16)
                g.indirect_dma_start(
                    Kg[et % 2][:, :], None, cc_tab_o,
                    IndirectOffsetOnAxis(adjc_cl[:, et:et + 1], 0),
                    element_offset=256,
                ).then_inc(sg2[et % 2].h, 16)
                sg2[et % 2].v += 32
        blk.gpsimd(_)
        WQ_AG = scc.v

        def _(v):
            for et in range(NET):
                v.wait_ge(sg2[et % 2].h, 32 * (et // 2 + 1))
                if et >= 1:
                    v.wait_ge(sa.h, BGA + et)
                v.tensor_tensor(prod4[:, :], Qg[et % 2][:, :], Kg[et % 2][:, :],
                                ALU.mult).then_inc(sv.h, 1)
                v.wait_ge(sa.h, BGA + et + 1)
                if et >= 2:
                    v.wait_ge(sm.h, BGM + et - 1)
                v.tensor_scalar(Ub[et % 2][:, :], iota_sb[:, :], hi_f[:, et:et + 1],
                                ctb[:, et:et + 1], ALU.is_equal, ALU.mult)
                v.tensor_single_scalar(Vb[et % 2][:, :], iota_sb[:, :],
                                       lo_f[:, et:et + 1],
                                       ALU.is_equal).then_inc(sv.h, 1)
                sv.v += 2
        blk.vector(_)

        def _(s):
            for et in range(NET):
                s.wait_ge(sv.h, BGV + 2 * et + 1)
                s.activation(junk4[:, :], prod4[:, :], AF.Copy,
                             scale=av16[:, et:et + 1],
                             accum_out=ctb[:, et:et + 1]).then_inc(sa.h, 1)
                sa.v += 1
        blk.scalar(_)

        def _(t):
            for et in range(NET):
                t.wait_ge(sv.h, BGV + 2 * et + 2)
                t.matmul(pb[6][0:64, 0:64], Ub[et % 2][:, :], Vb[et % 2][:, :],
                         start=(et == 0), stop=(et == NET - 1)).then_inc(sm.h, 1)
                sm.v += 1
        blk.tensor(_)
        A2DM = sm.v

        def _(v):
            v.wait_ge(sm.h, A2DM)
            v.tensor_copy(a2dsb[:, :], pb[6][0:64, 0:64]).then_inc(sv.h, 1)
            sv.v += 1
        blk.vector(_)
        A2DV = sv.v

        def _(sync):
            sync.wait_ge(sv.h, A2DV)
            sync.dma_start(out=cc_a2d_i, in_=a2dsb[:, :]).then_inc(sci.h, 16)
            sci.v += 16
        blk.sync(_)
        A2D_CI = sci.v

        def _(g):
            g.wait_ge(sci.h, A2D_CI)
            g.collective_compute(
                "AllReduce", ALU.add, replica_groups=[list(range(NCORES))],
                ins=[cc_a2d_i.opt()], outs=[cc_a2d_o.opt()],
            ).then_inc(scc.h)
            scc.v += 1
        blk.gpsimd(_)
        A2D_AR = scc.v

        def _(sync):
            sync.wait_ge(scc.h, A2D_AR)
            sync.dma_start(out=a2dr[:, :], in_=cc_a2d_o).then_inc(sx2[0].h, 16)
            sx2[0].v += 16
        blk.sync(_)
        A2DR_LD = sx2[0].v

        def _(v):
            v.wait_ge(sx2[0].h, A2DR_LD)
            v.tensor_reduce(acols[:, 0:1], a2dr[:, :], AX.X, ALU.max).then_inc(sv.h, 1)
            sv.v += 1
        blk.vector(_)
        AM1 = sv.v

        def _(v):
            v.wait_ge(sv.h, AM1)
            v.tensor_copy(alpha2b[:, 0:1], acols[0:64, 0:1]).then_inc(sv.h, 1)
            sv.v += 1
        blk.vector(_)
        AM2 = sv.v

        def _(t):
            t.wait_ge(sv.h, AM2)
            t.transpose(pbt[0:1, 256:320], alpha2b[0:64, 0:1],
                        ident_sb[0:64, 0:64]).then_inc(sm.h, 1)
            sm.v += 1
        blk.tensor(_)
        AM3 = sm.v

        def _(v):
            v.wait_ge(sm.h, AM3)
            v.tensor_reduce(arow1[0:1, 0:1], pbt[0:1, 256:320], AX.X,
                            ALU.max).then_inc(sv.h, 1)
            sv.v += 1
            v.wait_ge(sv.h, sv.v)
            v.tensor_single_scalar(arow1[0:1, 1:2], arow1[0:1, 0:1], -1.0,
                                   ALU.mult).then_inc(sv.h, 1)
            sv.v += 1
            v.wait_ge(sv.h, sv.v)
            v.tensor_copy(alpha2b[0:1, 1:2], arow1[0:1, 1:2]).then_inc(sv.h, 1)
            sv.v += 1
        blk.vector(_)
        AM4 = sv.v

        def _(t):
            t.wait_ge(sv.h, AM4)
            t.matmul(pb[6][0:64, 64:65], ones_sb[0:1, 0:64], alpha2b[0:1, 1:2],
                     start=True, stop=True).then_inc(sm.h, 1)
            sm.v += 1
        blk.tensor(_)
        AM5 = sm.v

        def _(v):
            v.wait_ge(sm.h, AM5)
            v.tensor_copy(acols[:, 1:2], pb[6][0:64, 64:65]).then_inc(sv.h, 1)
            sv.v += 1
        blk.vector(_)
        AM6 = sv.v

        def _(s):
            s.wait_ge(sv.h, AM6)
            s.activation(alpha2[:, :], a2dr[:, :], AF.Exp, bias=acols[:, 1:2],
                         accum_out=acols[:, 2:3]).then_inc(sa.h, 1)
            sa.v += 1
        blk.scalar(_)
        AM7 = sa.v

        def _(v):
            v.wait_ge(sa.h, AM7)
            v.tensor_copy(alpha2b[:, 2:3], acols[0:64, 2:3]).then_inc(sv.h, 1)
            sv.v += 1
        blk.vector(_)
        AM8 = sv.v

        def _(t):
            t.wait_ge(sv.h, AM8)
            t.transpose(pbt[0:1, 320:384], alpha2b[0:64, 2:3],
                        ident_sb[0:64, 0:64]).then_inc(sm.h, 1)
            sm.v += 1
        blk.tensor(_)
        AM9 = sm.v

        def _(v):
            v.wait_ge(sm.h, AM9)
            v.tensor_reduce(arow1[0:1, 2:3], pbt[0:1, 320:384], AX.X,
                            ALU.add).then_inc(sv.h, 1)
            sv.v += 1
            v.wait_ge(sv.h, sv.v)
            v.reciprocal(arow1[0:1, 3:4], arow1[0:1, 2:3]).then_inc(sv.h, 1)
            sv.v += 1
            v.wait_ge(sv.h, sv.v)
            v.tensor_copy(alpha2b[0:1, 3:4], arow1[0:1, 3:4]).then_inc(sv.h, 1)
            sv.v += 1
        blk.vector(_)
        AM10 = sv.v

        def _(t):
            t.wait_ge(sv.h, AM10)
            t.matmul(pb[6][0:64, 66:67], ones_sb[0:1, 0:64], alpha2b[0:1, 3:4],
                     start=True, stop=True).then_inc(sm.h, 1)
            sm.v += 1
        blk.tensor(_)
        AM11 = sm.v

        def _(v):
            v.wait_ge(sm.h, AM11)
            v.tensor_copy(acols[:, 3:4], pb[6][0:64, 66:67]).then_inc(sv.h, 1)
            sv.v += 1
            v.wait_ge(sv.h, sv.v)
            v.tensor_single_scalar(alpha2b[:, 4:68], alpha2[:, :],
                                   acols[:, 3:4], ALU.mult).then_inc(sv.h, 1)
            sv.v += 1
        blk.vector(_)
        AM12 = sv.v

        def _(t):
            t.wait_ge(sv.h, AM12)
            t.matmul(pb[6][0:32, 128:192], sela_sb[:, :], alpha2b[:, 4:68],
                     start=True, stop=True).then_inc(sm.h, 1)
            sm.v += 1
        blk.tensor(_)
        AM13 = sm.v

        def _(v):
            v.wait_ge(sm.h, AM13)
            v.tensor_copy(selout[:, :], pb[6][0:32, 128:192]).then_inc(sv.h, 1)
            sv.v += 1
        blk.vector(_)
        AM14 = sv.v

        def _(sync):
            sync.wait_ge(sv.h, AM14)
            sync.dma_start(out=d_alph, in_=selout[:, :]).then_inc(sst.h, 16)
            sst.v += 16
            sync.wait_ge(sst.h, sst.v)
            sync.dma_start(out=alpharow[0:1, :],
                           in_=d_alph.rearrange("a b -> (a b)").unsqueeze(0)).then_inc(sx2[1].h, 16)
            sx2[1].v += 16
        blk.sync(_)
        ALPHA_LD = sx2[1].v

    ln1T = at("ln1T", [128, NDC, TH], BF16, offset=32768)
    x0s5 = [at("x0s5", [128, NDC, 512], F32, offset=98304)] * 2
    xl5 = at("xl5", [128, NDC, 512], F32, offset=131072)
    sg5 = at("sg5", [128, NDC, 512], F32, offset=147456)
    sw5 = scr("sw5", [128, NDC, 512], F32, 0)
    x1b5 = at("x1b5", [128, NDC, 512], F32, offset=114688)
    x1bb = at("x1bb", [128, NDC, 512], BF16, offset=147456)
    sq15 = at("sq15", [128, NDC, 512], BF16, offset=155648)
    alpbr = scr("alpbr", [1, 2 * 512], BF16, 41216)

    if PHASES >= 5:
        sxw = S(nc, "sxw")
        sx5 = [S(nc, "sx50"), S(nc, "sx51")]
        GBASE_V, GBASE_A, GBASE_M = sv.v, sa.v, sm.v
        for jt in range(4):
            TV, TA, TM = sv.v, sa.v, sm.v

            def _(sync, jt=jt, TV=TV):
                if jt >= 1:
                    sync.wait_ge(sv.h, TV)
                else:
                    sync.wait_ge(sv.h, WPREP_QV)
                sync.dma_start(
                    out=x0s5[0][:, :, :],
                    in_=x0T.rearrange("(c p) t -> p c t", p=128)[:, :, 512 * jt:512 * (jt + 1)],
                ).then_inc(sx5[0].h, 16)
                sx5[0].v += 16
            blk.sync(_)

            def _(t, jt=jt, TV=TV, TA=TA):
                if jt == 0:
                    t.wait_ge(sx2[1].h, sx2[1].v)
                if jt >= 1:
                    t.wait_ge(sv.h, TV)
                t.matmul(pb[5][:, 0:512], ones_sb[0:1, :],
                         alpharow[0:1, 512 * jt:512 * (jt + 1)],
                         start=True, stop=True).then_inc(sm.h, 1)
                sm.v += 1
            blk.tensor(_)
            ALB = sm.v

            def _(v, jt=jt, ALB=ALB, TA=TA):
                v.wait_ge(sm.h, ALB)
                if jt == 0:
                    v.wait_ge(sci.h, WQ_CI)
                for ch in range(NDC):
                    ins = v.tensor_tensor(xl5[:, ch, :],
                                          valT[:, ch, 512 * jt:512 * (jt + 1)],
                                          pb[5][:, 0:512], ALU.mult)
                ins.then_inc(sv.h, 1)
                sv.v += 1
            blk.vector(_)
            G1 = sv.v

            def _(s, jt=jt, G1=G1):
                s.wait_ge(sv.h, G1)
                for ch in range(NDC):
                    ins = s.activation(sg5[:, ch, :], xl5[:, ch, :], AF.Sigmoid,
                                       scale=-1.0)
                ins.then_inc(sa.h, 1)
                sa.v += 1
                s.wait_ge(sa.h, sa.v)
                for ch in range(NDC):
                    ins = s.activation(sw5[:, ch, :], sg5[:, ch, :], AF.Square)
                ins.then_inc(sa.h, 1)
                sa.v += 1
            blk.scalar(_)
            G3 = sa.v

            def _(v, jt=jt, G3=G3):
                v.wait_ge(sa.h, G3)
                for ch in range(NDC):
                    ins = v.tensor_tensor(sg5[:, ch, :], xl5[:, ch, :],
                                          encT[:, ch, 512 * jt:512 * (jt + 1)],
                                          ALU.subtract)
                ins.then_inc(sv.h, 1)
                sv.v += 1
                v.wait_ge(sv.h, sv.v)
                for ch in range(NDC):
                    ins = v.tensor_tensor(xl5[:, ch, :], sw5[:, ch, :],
                                          sg5[:, ch, :], ALU.mult)
                ins.then_inc(sv.h, 1)
                sv.v += 1
                v.wait_ge(sv.h, sv.v)
                v.wait_ge(sx5[0].h, 16 * (jt + 1))
                for ch in range(NDC):
                    ins = v.scalar_tensor_tensor(sg5[:, ch, :], xl5[:, ch, :], 2.0,
                                                 x0s5[0][:, ch, :],
                                                 ALU.mult, ALU.add)
                ins.then_inc(sv.h, 1)
                sv.v += 1
                v.wait_ge(sv.h, sv.v)
                v.wait_ge(sxw.h, 16 * jt)
                for ch in range(NDC):
                    ins = v.scalar_tensor_tensor(x1b5[:, ch, :],
                                                 encT[:, ch, 512 * jt:512 * (jt + 1)],
                                                 2.0, sg5[:, ch, :],
                                                 ALU.mult, ALU.add)
                ins.then_inc(sv.h, 1)
                sv.v += 1
            blk.vector(_)
            G7 = sv.v

            def _(sync, jt=jt, G7=G7):
                sync.wait_ge(sv.h, G7)
                sync.dma_start(out=d_x1.rearrange("(c p) t -> p c t", p=128)
                               [:, :, 512 * jt:512 * (jt + 1)],
                               in_=x1b5[:, :, :]).then_inc(sxw.h, 16)
                sxw.v += 16
            blk.sync(_)

            def _(s, jt=jt, G7=G7):
                s.wait_ge(sv.h, G7)
                s.activation(x1bb[:, :, :], x1b5[:, :, :], AF.Copy)
                s.activation(sq15[:, :, :], x1b5[:, :, :], AF.Square).then_inc(sa.h, 1)
                sa.v += 1
            blk.scalar(_)
            L1A = sa.v

            def _(t, jt=jt, L1A=L1A):
                t.wait_ge(sa.h, L1A)
                onecol = ones_sb[:, 0:1]
                for c in range(NDC):
                    t.matmul(pb[0][0:1, 0:512], onecol, x1bb[:, c, :],
                             start=(c == 0), stop=(c == NDC - 1))
                for c in range(NDC):
                    mm = t.matmul(pb[1][0:1, 0:512], onecol, sq15[:, c, :],
                                  start=(c == 0), stop=(c == NDC - 1))
                mm.then_inc(sm.h, 1)
                sm.v += 1
            blk.tensor(_)
            L1M = sm.v

            def _(v, jt=jt, L1M=L1M):
                r = statrows
                inv = 1.0 / D
                v.wait_ge(sm.h, L1M)
                v.tensor_single_scalar(r[0:1, 0:512], pb[0][0:1, 0:512], -inv, ALU.mult)
                v.tensor_single_scalar(r[0:1, 512:1024], pb[1][0:1, 0:512], inv,
                                       ALU.mult).then_inc(sv.h, 1)
                sv.v += 1
                v.wait_ge(sv.h, sv.v)
                v.tensor_tensor(r[0:1, 1024:1536], r[0:1, 0:512], r[0:1, 0:512],
                                ALU.mult).then_inc(sv.h, 1)
                sv.v += 1
                v.wait_ge(sv.h, sv.v)
                v.tensor_tensor(r[0:1, 1024:1536], r[0:1, 512:1024], r[0:1, 1024:1536],
                                ALU.subtract).then_inc(sv.h, 1)
                sv.v += 1
                v.wait_ge(sv.h, sv.v)
                v.tensor_single_scalar(r[0:1, 1024:1536], r[0:1, 1024:1536], 1e-5,
                                       ALU.add).then_inc(sv.h, 1)
                sv.v += 1
            blk.vector(_)
            L1V = sv.v

            def _(s, jt=jt, L1V=L1V):
                s.wait_ge(sv.h, L1V)
                s.activation(statrows[0:1, 1536:2 * 1024], statrows[0:1, 1024:1536],
                             AF.Sqrt).then_inc(sa.h, 1)
                sa.v += 1
            blk.scalar(_)
            L1S = sa.v

            def _(v, jt=jt, L1S=L1S):
                r = statrows
                v.wait_ge(sa.h, L1S)
                v.reciprocal(r[0:1, 1024:1536], r[0:1, 1536:2048]).then_inc(sv.h, 1)
                sv.v += 1
                v.wait_ge(sv.h, sv.v)
                v.tensor_copy(alpbr[0:1, 0:512], r[0:1, 0:512])
                v.tensor_copy(alpbr[0:1, 512:1024], r[0:1, 1024:1536]).then_inc(sv.h, 1)
                sv.v += 1
            blk.vector(_)
            L1R = sv.v

            def _(t, jt=jt, L1R=L1R):
                t.wait_ge(sv.h, L1R)
                t.matmul(pb[2][:, 0:512], ones_sb[0:1, :], alpbr[0:1, 0:512],
                         start=True, stop=True)
                t.matmul(pb[3][:, 0:512], ones_sb[0:1, :], alpbr[0:1, 512:1024],
                         start=True, stop=True).then_inc(sm.h, 1)
                sm.v += 1
            blk.tensor(_)
            L1B = sm.v

            def _(v, jt=jt, L1B=L1B):
                v.wait_ge(sm.h, L1B)
                for c in range(NDC):
                    ins = v.tensor_tensor(ln1T[:, c, 512 * jt:512 * (jt + 1)],
                                          x1b5[:, c, :], pb[2][:, 0:512], ALU.add)
                ins.then_inc(sv.h, 1)
                sv.v += 1
                v.wait_ge(sv.h, sv.v)
                for c in range(NDC):
                    ins = v.tensor_tensor(ln1T[:, c, 512 * jt:512 * (jt + 1)],
                                          ln1T[:, c, 512 * jt:512 * (jt + 1)],
                                          pb[3][:, 0:512], ALU.mult)
                ins.then_inc(sv.h, 1)
                sv.v += 1
            blk.vector(_)

    if PHASES >= 6:
        P6V = sv.v

        wq1r = at("wq1r", [128, 32, NDC, 128], BF16, offset=65536)
        wq2rA = at("wq2rA", [128, 16, NDC, 128], BF16, offset=131072)
        wq2rB = scr("wq2rB", [128, 16, NDC, 128], BF16, 0)
        ccwq_flat = cc_wq_o.rearrange("a b -> (a b)")

        def _(sync):
            sync.wait_ge(scc.h, WQ_AG)
            sync.wait_ge(sv.h, P6V)
            sync.wait_ge(sxw.h, sxw.v)
            for r in range(NCORES):
                w1v = ccwq_flat[r * 2048 * 512: r * 2048 * 512 + 524288] \
                    .rearrange("(dch p fl col) -> p fl dch col",
                               p=128, fl=4, col=128)
                sync.dma_start(out=wq1r[:, 4 * r:4 * r + 4, :, :],
                               in_=w1v).then_inc(sg.h, 16)
                w2v = ccwq_flat[r * 2048 * 512 + 524288: (r + 1) * 2048 * 512] \
                    .rearrange("(fl p d2ch col) -> p fl d2ch col",
                               p=128, fl=4, col=128)
                w2dst = wq2rA if r < 4 else wq2rB
                sync.dma_start(out=w2dst[:, (4 * r) % 16:(4 * r) % 16 + 4, :, :],
                               in_=w2v).then_inc(sg.h, 16)
                sg.v += 32
        blk.sync(_)
        WQR_LD = sg.v

        hT6 = at("hT6", [128, 32, 512], BF16, offset=0)
        x1pp = at("x1pp", [128, 2, 512], F32, offset=199200)
        ybuf = at("ybuf", [128, 2, 512], F32, offset=203296)
        sx6 = S(nc, "sx6")
        d_x1v = d_x1.rearrange("(c p) t -> p c t", p=128)
        y_v = y.rearrange("(c p) t -> p c t", p=128)

        for jt in range(4):
            TBM, TBA, EVJ, STJ, LDJ = sm.v, sa.v, sv.v, sst.v, sx6.v

            def _(t, jt=jt, TBM=TBM, TBA=TBA, EVJ=EVJ):
                if jt == 0:
                    t.wait_ge(sg.h, WQR_LD)
                    t.wait_ge(sv.h, P6V)
                for fc in range(32):
                    if fc >= 2:
                        t.wait_ge(sa.h, TBA + fc - 1)
                    elif fc == 0:
                        t.wait_ge(sa.h, TBA)
                    for dch in range(NDC):
                        mm = t.matmul(pb[4 + fc % 2][:, 0:512],
                                      wq1r[:, fc, dch, :],
                                      ln1T[:, dch, 512 * jt:512 * (jt + 1)],
                                      start=(dch == 0), stop=(dch == NDC - 1))
                    mm.then_inc(sm.h, 1)
                    sm.v += 1
                t.wait_ge(sa.h, TBA + 32)
                for d2 in range(NDC):
                    if d2 >= 4:
                        t.wait_ge(sv.h, EVJ + d2 - 3)
                    elif d2 == 0:
                        t.wait_ge(sv.h, EVJ)
                    for fc in range(32):
                        w2t = (wq2rA[:, fc, d2, :] if fc < 16
                               else wq2rB[:, fc - 16, d2, :])
                        mm = t.matmul(pb[d2 % 4][:, 0:512], w2t, hT6[:, fc, :],
                                      start=(fc == 0), stop=(fc == 31))
                    mm.then_inc(sm.h, 1)
                    sm.v += 1
            blk.tensor(_)

            def _(s, jt=jt, TBM=TBM):
                for fc in range(32):
                    s.wait_ge(sm.h, TBM + fc + 1)
                    s.activation(hT6[:, fc, :], pb[4 + fc % 2][:, 0:512],
                                 AF.Relu, bias=b1_sb[:, fc:fc + 1]).then_inc(sa.h, 1)
                    sa.v += 1
            blk.scalar(_)

            def _(v, jt=jt, TBM=TBM, EVJ=EVJ, STJ=STJ, LDJ=LDJ):
                for d2 in range(NDC):
                    v.wait_ge(sm.h, TBM + 32 + d2 + 1)
                    v.wait_ge(sx6.h, LDJ + 16 * (d2 + 1))
                    if d2 >= 2:
                        v.wait_ge(sst.h, STJ + 16 * (d2 - 1))
                    v.scalar_tensor_tensor(
                        ybuf[:, d2 % 2, :], pb[d2 % 4][:, 0:512],
                        b2_sb[:, d2:d2 + 1], x1pp[:, d2 % 2, :],
                        ALU.add, ALU.add).then_inc(sv.h, 1)
                    sv.v += 1
            blk.vector(_)

            def _(sync, jt=jt, EVJ=EVJ):
                for d2 in range(NDC):
                    if d2 >= 2:
                        sync.wait_ge(sv.h, EVJ + d2 - 1)
                    sync.dma_start(
                        out=x1pp[:, d2 % 2:d2 % 2 + 1, :],
                        in_=d_x1v[:, d2:d2 + 1, 512 * jt:512 * (jt + 1)],
                    ).then_inc(sx6.h, 16)
                    sx6.v += 16
                    if d2 >= 2:
                        sync.wait_ge(sv.h, EVJ + d2 - 1)
                        sync.dma_start(
                            out=y_v[:, d2 - 2:d2 - 1, 512 * jt:512 * (jt + 1)],
                            in_=ybuf[:, d2 % 2:d2 % 2 + 1, :],
                        ).then_inc(sst.h, 16)
                        sst.v += 16
                for d2 in (NDC - 2, NDC - 1):
                    sync.wait_ge(sv.h, EVJ + d2 + 1)
                    sync.dma_start(
                        out=y_v[:, d2:d2 + 1, 512 * jt:512 * (jt + 1)],
                        in_=ybuf[:, d2 % 2:d2 % 2 + 1, :],
                    ).then_inc(sst.h, 16)
                    sst.v += 16
            blk.sync(_)

    def _(sync):
        for _sem in S.registry:
            if _sem.v > 0:
                sync.wait_ge(_sem.h, _sem.v)
        if DEBUG:
            if PHASES < 3:
                sync.dma_start(out=dbg["dbg_ln0"][:, 0:NDC, 0:TH],
                               in_=ln0h[:, :, :]).then_inc(sst.h, 16)
                sync.dma_start(out=dbg["dbg_q"], in_=qTs[:, :, :]).then_inc(sst.h, 16)
                sync.dma_start(out=dbg["dbg_k"], in_=kTs[:, :, :]).then_inc(sst.h, 16)
                sync.dma_start(out=dbg["dbg_v"],
                               in_=vtok.rearrange("p a b -> p (a b)")).then_inc(sst.h, 16)
                sst.v += 64
            if PHASES == 2:
                sync.dma_start(out=dbg["dbg_ql"],
                               in_=ql_bf.rearrange("p a b -> p (a b)")).then_inc(sst.h, 16)
                sync.dma_start(out=dbg["dbg_kl"],
                               in_=kl_bf.rearrange("p a b -> p (a b)")).then_inc(sst.h, 16)
                sync.dma_start(out=dbg["dbg_a2"],
                               in_=A2bf4.rearrange("p a b -> p (a b)")).then_inc(sst.h, 16)
                sync.dma_start(out=dbg["dbg_f"],
                               in_=F4bf.rearrange("p a b -> p (a b)")).then_inc(sst.h, 16)
                sync.dma_start(out=dbg["dbg_g"],
                               in_=G4bf.rearrange("p a b -> p (a b)")).then_inc(sst.h, 16)
                sst.v += 80
            if PHASES >= 2:
                sync.dma_start(out=dbg["dbg_scale"][0:1, 0:6], in_=scrow[0:1, 0:6]).then_inc(sst.h, 16)
                sst.v += 16
            if PHASES in (2, 3, 4, 5):
                sync.dma_start(out=dbg["dbg_out"], in_=outTo[:, :, :]).then_inc(sst.h, 16)
                sst.v += 16
            if PHASES in (4, 5):
                sync.dma_start(out=dbg["dbg_enc"], in_=encT[:, :, :]).then_inc(sst.h, 16)
                sync.dma_start(out=dbg["dbg_val"], in_=valT[:, :, :]).then_inc(sst.h, 16)
                sync.dma_start(out=dbg["dbg_a2d"], in_=a2dr[:, :]).then_inc(sst.h, 16)
                sst.v += 48
            if PHASES >= 4:
                sync.dma_start(out=dbg["dbg_alpha"], in_=alpharow[0:1, :]).then_inc(sst.h, 16)
                sst.v += 16
            if PHASES >= 6 and "dbg_ln1" in dbg:
                sync.dma_start(out=dbg["dbg_ln1"], in_=ln1T[:, :, :]).then_inc(sst.h, 16)
                sync.dma_start(out=dbg["dbg_wq1"],
                               in_=wq1r.rearrange("p a b c -> p (a b c)")).then_inc(sst.h, 16)
                sync.dma_start(out=dbg["dbg_wq2a"],
                               in_=wq2rA.rearrange("p a b c -> p (a b c)")).then_inc(sst.h, 16)
                sync.dma_start(out=dbg["dbg_wq2b"],
                               in_=wq2rB.rearrange("p a b c -> p (a b c)")).then_inc(sst.h, 16)
                sync.dma_start(out=dbg["dbg_h"],
                               in_=hT6.rearrange("p a b -> p (a b)")).then_inc(sst.h, 16)
                sync.dma_start(out=dbg["dbg_ccwqi"], in_=cc_wq_i).then_inc(sst.h, 16)
                sync.dma_start(out=dbg["dbg_ccwqo"], in_=cc_wq_o).then_inc(sst.h, 16)
                sst.v += 112
        sync.wait_ge(sst.h, sst.v)
    blk.sync(_)

    blk.__exit__(None, None, None)
    return nc



def _band_matrices(conv_w, parity):
    heads = range(4 * parity, 4 * parity + 4)
    out = np.zeros((128, 4 * 7 * 128), np.float32)
    s_idx = np.arange(128)[:, None]
    p_idx = np.arange(128)[None, :]
    for jloc, h in enumerate(heads):
        cw = np.zeros(300, np.float32)
        cw[:33] = conv_w[h, 0, :, 0]
        jA = s_idx - 128 - p_idx + 16
        jB = s_idx - p_idx + 16
        jC = s_idx + 128 - p_idx + 16
        A = np.where((jA >= 0) & (jA < 33), cw[np.clip(jA, 0, 299)], np.float32(0))
        B = np.where((jB >= 0) & (jB < 33), cw[np.clip(jB, 0, 299)], np.float32(0))
        C = np.where((jC >= 0) & (jC < 33), cw[np.clip(jC, 0, 299)], np.float32(0))
        base = jloc * 7 * 128
        out[:, base: base + 128] = A
        out[:, base + 128: base + 256] = B
        out[:, base + 256: base + 384] = C
        if parity == 1:
            out[:, base + 384: base + 512] = A
            out[:, base + 512: base + 640] = C
        else:
            out[:, base + 640: base + 768] = A
            out[:, base + 768: base + 896] = C
    return out


def _outw_other(out_w, par):
    w = np.array(out_w, copy=True)
    w[256 * par: 256 * par + 256, :] = 0
    return w


def make_in_maps(inputs):
    x0 = inputs["L0_attn_dense"]
    qkv_w = inputs["L0_attn_qkv_w"]
    conv_w = inputs["L0_attn_conv_w"]
    in_maps = []
    iota64 = np.broadcast_to(np.arange(64, dtype=np.float32), (128, 64)).copy()
    ones_bf = np.ones((128, 128), ml_dtypes.bfloat16)
    ident_bf = np.eye(128).astype(ml_dtypes.bfloat16)
    ikc = np.zeros((128, 2 * 768), ml_dtypes.bfloat16)
    for k, val in enumerate([7.0, 15.0, 13.0]):
        for ch in range(2):
            for p in range(128):
                ikc[p, ch * 768 + k * 256 + ch * 128 + p] = val

    for c in range(NCORES):
        b, par = c // 2, c % 2
        perm = (np.r_[2048:4096, 0:2048] if par else np.r_[0:4096])
        x0T = np.ascontiguousarray(x0[b].T[:, perm])
        hsl = slice(256 * par, 256 * par + 256)
        qkvw_c = np.ascontiguousarray(
            np.concatenate([qkv_w[:, 0:512][:, hsl],
                            qkv_w[:, 512:1024][:, hsl],
                            qkv_w[:, 1024:1536][:, hsl]], axis=1))
        sel = np.zeros((64, 32), ml_dtypes.bfloat16)
        for i in range(32):
            sel[32 * par + i, i] = 1
        m = {
            "x0T": x0T,
            "qkvw": qkvw_c,
            "wv": inputs["L0_attn_wv_w"],
            "wvb": np.ascontiguousarray(inputs["L0_attn_wv_b"].reshape(NDC, 128).T),
            "wqk": np.ascontiguousarray(
                np.concatenate([inputs["L0_attn_wq"], inputs["L0_attn_wk"]], axis=1)),
            "outw_o": _outw_other(inputs["L0_attn_out_w"], par),
            "outw_m": np.ascontiguousarray(
                inputs["L0_attn_out_w"][256 * par: 256 * par + 256]),
            "outb": np.ascontiguousarray(inputs["L0_attn_out_b"].reshape(NDC, 128).T),
            "w1s": np.ascontiguousarray(inputs["L0_ffn_W1"][512 * c: 512 * c + 512]),
            "w2s": np.ascontiguousarray(inputs["L0_ffn_W2"][:, 512 * c: 512 * c + 512]),
            "b1c": np.ascontiguousarray(inputs["L0_ffn_b1"].reshape(32, 128).T),
            "b2c": np.ascontiguousarray(inputs["L0_ffn_b2"].reshape(NDC, 128).T),
            "adjr": np.ascontiguousarray(
                inputs["L0_attn_adj_rows"][EPC * c: EPC * (c + 1)].reshape(NET, 128).T),
            "adjc": np.ascontiguousarray(
                inputs["L0_attn_adj_cols"][EPC * c: EPC * (c + 1)].reshape(NET, 128).T),
            "adjv": np.ascontiguousarray(
                inputs["L0_attn_adj_vals"][EPC * c: EPC * (c + 1)].reshape(NET, 128).T),
            "bands": _band_matrices(conv_w, par),
            "iota64": iota64,
            "ones_bf": ones_bf,
            "ident_bf": ident_bf,
            "ikc": ikc,
            "selalpha": sel,
        }
        in_maps.append(m)
    return in_maps


def kernel(**inputs):
    inputs = {k: np.asarray(v) for k, v in inputs.items()}
    in_maps = make_in_maps(inputs)
    nc = build_program()
    res = run_bass_kernel_spmd(nc, in_maps, core_ids=list(range(NCORES)))
    out = np.empty((4, T, D), np.float32)
    for c in range(NCORES):
        b, par = c // 2, c % 2
        out[b, 2048 * par: 2048 * par + 2048, :] = np.asarray(res.results[c]["y"]).T
    return out


if __name__ == "__main__":
    nc = build_program()
    print("build ok, instructions:", nc.n_instructions())
